# version 02
# baseline (speedup 1.0000x reference)
import numpy as np
import concourse.bass as bass
import concourse.mybir as mybir
from concourse.bass_utils import run_bass_kernel_spmd
from contextlib import ExitStack

F32 = mybir.dt.float32
F16 = mybir.dt.float16
U32 = mybir.dt.uint32
AF = mybir.ActivationFunctionType
OP = mybir.AluOpType

ENGINES = ("sync", "scalar", "vector", "tensor", "gpsimd")
C, CT, T, TC = 1024, 8, 4096, 512
NCH = T // TC
DI, DSL, NS, RK, E = 2048, 16, 16, 64, 8
PHASES = 7
DEBUG = ""
P5SUB = 4
SKIPS = ()
MOESUB = 3
MOEBAR = 0
MOESER = 0
P5SER = 0
SERALL = 1


def ts(i, n=128):
    return slice(i * n, (i + 1) * n)


class Prog:
    def __init__(self):
        self.q = {e: [] for e in ENGINES}
        self.led = {}
        self.serial = False
        self.sems = None

    def emit(self, eng, fn, inc=None, n=1):
        if self.serial and self.sems is not None:
            for oe, osem in self.sems.items():
                if oe != eng:
                    self.wait(eng, osem)
        if inc is not None:
            self.led[id(inc)] = self.led.get(id(inc), 0) + n
            def wrapped(e, fn=fn, inc=inc, n=n):
                fn(e).then_inc(inc, n)
        else:
            def wrapped(e, fn=fn):
                fn(e)
        self.q[eng].append(wrapped)
        return self.led.get(id(inc), 0) if inc is not None else None

    def count(self, sem):
        return self.led.get(id(sem), 0)

    def wait(self, eng, sem, val=None):
        v = self.count(sem) if val is None else val
        if v > 0:
            self.q[eng].append(lambda e, sem=sem, v=v: e.wait_ge(sem, v))

    def run(self, block):
        for e in ENGINES:
            def make(e):
                def _(eng):
                    for fn in self.q[e]:
                        fn(eng)
                return _
            getattr(block, e)(make(e))


def build_program():
    nc = bass.Bass("TRN2", target_bir_lowering=False)
    dr = {}

    def din(name, shape, dtype=F32):
        dr[name] = nc.dram_tensor(name, shape, dtype, kind="ExternalInput").ap()

    def dscr(name, shape, dtype=F16):
        dr[name] = nc.dram_tensor(name, shape, dtype, kind="Internal").ap()

    din("xT", [CT, 128, T])
    din("fr16", [8, 128, C], F16); din("fi16", [8, 128, C], F16)
    din("ident16", [128, 128], F16)
    din("wfr", [8, 128, C]); din("wfi", [8, 128, C])
    din("wkT_t", [8, 128, C]); din("wvT_t", [8, 128, C]); din("wrT_t", [8, 128, C])
    din("woT_t", [8, 128, C])
    din("winT_t", [32, 128, C])
    din("woutT_t", [8, 128, DI])
    din("wxT_t", [128, DSL * 96])
    din("wdtT_t", [RK, DI])
    din("wg_t", [8, 128, C])
    din("we_t", [E, 8, 128, C])
    din("smallv", [128, 48])
    din("smalld", [128, 368])
    din("be", [8, C])
    din("bgrow", [1, C])
    dscr("s_m", [8, 128, C])
    dscr("s_kv", [6, 8, 128, C])
    dscr("s_wo", [8, 128, C])
    dscr("s_win", [32, 128, C])
    dscr("s_wout", [8, 128, DI])
    dscr("s_wg", [8, 128, C])
    dscr("s_bg", [1, C])
    dscr("s_we", [E, 2, 128, 8 * 512])
    dscr("s_xcv", [NCH, 128, DSL * TC])
    dscr("s_sz", [NCH, 128, DSL * TC])
    dscr("s_xdbl", [NCH, 96, TC])
    dscr("s_xsp", [CT, 128, T], F32)
    out_ap = nc.dram_tensor("out", [CT, 128, T], F32, kind="ExternalOutput").ap()
    dbg_ap = nc.dram_tensor("dbg", [128, 8192], F32, kind="ExternalOutput").ap()

    ctx = ExitStack()
    with ctx:
        def sb(name, shape, dtype=F32, c=None):
            return (c or ctx).enter_context(nc.sbuf_tensor(name, shape, dtype))

        ident = sb("ident", [128, 128], F16)
        ones16 = sb("ones16", [128, 128], F16)
        sv = sb("sv", [128, 72])
        lam8 = sb("lam8", [128, 8]); etf8 = sb("etf8", [128, 8])
        bias05 = sb("bias05", [128, 8])
        sd_ = sb("sd_", [128, 368])
        aneg = sb("aneg", [128, DSL * NS])
        be16 = sb("be16", [33, C], F16)
        bg16 = be16[32:33, :]
        wx16 = sb("wx16", [128, DSL * 96], F16)
        wdt16 = sb("wdt16", [RK, DI], F16)
        cpool = sb("cpool", [128, 4])
        ident32 = sb("ident32", [128, 128])
        nc.const_aps.register(F32, 1e-5, cpool[:, 0:1])
        nc.const_aps.register(F32, 1.0, cpool[:, 1:2])
        stA = sb("stA", [128, 8]); stB = sb("stB", [128, 8])
        stS = sb("stS", [128, DSL * NS])
        ctail = sb("ctail", [128, DSL * 4])
        xsh0 = sb("xsh0", [128, 8], F16)

        pb = [ctx.enter_context(nc.psum_tensor(f"pb{i}", [128, 512], F32))
              for i in range(6)]
        pbig = ctx.enter_context(nc.psum_tensor("pbig", [128, 1024], F32))

        sems = {e: ctx.enter_context(nc.semaphore(f"bar_{e}")) for e in ENGINES}
        s_w = ctx.enter_context(nc.semaphore("s_w"))
        s_g = ctx.enter_context(nc.semaphore("s_g"))
        s_te = ctx.enter_context(nc.semaphore("s_te"))
        s_sc = ctx.enter_context(nc.semaphore("s_sc"))
        s_v = ctx.enter_context(nc.semaphore("s_v"))
        block = ctx.enter_context(nc.Block())

        p = Prog()
        SEM = {"tensor": s_te, "scalar": s_sc, "vector": s_v,
               "sync": s_w, "gpsimd": s_g}
        p.sems = SEM
        if SERALL:
            p.serial = True

        def dma(queue, out, in_):
            return p.emit(queue, lambda e, o=out, i=in_: e.dma_start(out=o, in_=i),
                          inc=SEM[queue], n=16)

        def te(fn, inc=True):
            return p.emit("tensor", fn, inc=s_te if inc else None)

        def sc(fn):
            return p.emit("scalar", fn, inc=s_sc)

        def ve(fn):
            return p.emit("vector", fn, inc=s_v)

        def vfence():
            p.wait("vector", s_v, p.count(s_v))

        def barrier():
            p.wait("sync", s_w)
            p.wait("gpsimd", s_g)
            for e in ENGINES:
                p.emit(e, lambda eng: eng.nop(), inc=sems[e], n=1)
            tgt = {e: p.count(sems[e]) for e in ENGINES}
            for e in ENGINES:
                for o in ENGINES:
                    if o != e:
                        p.wait(e, sems[o], tgt[o])

        dbg_done = []
        dbgst = sb("dbgst", [128, 96]) if DEBUG else None
        guards = {}

        def acquire(res, producer_eng):
            for sem, cnt in guards.pop(res, ()):
                p.wait(producer_eng, sem, cnt)

        def used(res, eng):
            guards.setdefault(res, []).append((SEM[eng], p.count(SEM[eng])))

        def dbg_f32(ap, col0, stage):
            n = ap.shape[1]
            sw = stage.shape[1]
            for o in range(0, n, sw):
                w = min(sw, n - o)
                np_ = ap.shape[0]
                sc(lambda e, ap=ap, o=o, w=w, np_=np_: e.activation(
                    out=stage[0:np_, 0:w], in_=ap[:, o:o + w], func=AF.Copy))
                dn = p.count(s_sc)
                p.wait("sync", s_sc, dn)
                p.emit("sync", lambda e, o=o, w=w, np_=np_: e.dma_start(
                    out=dbg_ap[0:np_, col0 + o:col0 + o + w],
                    in_=stage[0:np_, 0:w],
                    allow_slow_non_contiguous=True), inc=s_w, n=16)
                wd = p.count(s_w)
                p.wait("scalar", s_w, wd)

        def ln_rows(ch, xf16, _unused, mb, rb, t1, var, b0, b1):
            acquire(("lnxf", id(xf16)), "scalar")
            acquire(("lnrows", id(mb)), "scalar")
            for ct in range(CT):
                xs = x_res[:, ts(ct, T)][:, ts(ch, TC)]
                sc(lambda e, xs=xs, ct=ct: e.activation(
                    out=xf16[:, ts(ct, TC)], in_=xs, func=AF.Copy))
            cast_done = p.count(s_sc)
            if DEBUG == "ones" and not dbg_done:
                dbg_done.append(1)
                p.wait("scalar", s_sc, cast_done)
                dbg_f32(ones16[:], 0, mb)
                dbg_f32(xf16[:, 0:1024], 1024, mb)
                for ct in range(2):
                    dbg_f32(x_res[:, ts(ct, T)][:, 0:512], 2048 + ct * 512, mb)
            acquire(b0, "tensor"); acquire(b1, "tensor")
            p.wait("tensor", s_sc, cast_done)
            for ct in range(CT):
                te(lambda e, ct=ct: e.matmul(
                    out=pb[b0][:], lhsT=ones16[:], rhs=xf16[:, ts(ct, TC)],
                    start=(ct == 0), stop=(ct == CT - 1)), inc=(ct == CT - 1))
            s_done = p.count(s_te)
            if DEBUG == "psum" and not dbg_done:
                dbg_done.append(1)
                p.wait("scalar", s_te, s_done)
                dbg_f32(pb[b0][:], 0, mb)
            p.wait("scalar", s_te, s_done)
            for ct in range(CT):
                xs = x_res[:, ts(ct, T)][:, ts(ch, TC)]
                sc(lambda e, xs=xs, ct=ct: e.activation(
                    out=xf16[:, ts(ct, TC)], in_=xs, func=AF.Square))
            sqcast_done = p.count(s_sc)
            p.wait("tensor", s_sc, sqcast_done)
            for ct in range(CT):
                te(lambda e, ct=ct: e.matmul(
                    out=pb[b1][:], lhsT=ones16[:], rhs=xf16[:, ts(ct, TC)],
                    start=(ct == 0), stop=(ct == CT - 1)), inc=(ct == CT - 1))
            q_done = p.count(s_te)
            used(("lnxf", id(xf16)), "tensor")
            p.wait("scalar", s_te, s_done)
            sc(lambda e: e.activation(out=mb, in_=pb[b0][:], func=AF.Copy,
                                      scale=1.0 / 1024))
            used(b0, "scalar")
            if DEBUG == "mb" and not dbg_done:
                dbg_done.append(1)
                dbg_f32(mb, 0, var)

            sc(lambda e: e.activation(out=t1, in_=mb, func=AF.Square))
            sq_done = p.count(s_sc)
            p.wait("vector", s_te, q_done)
            p.wait("vector", s_sc, sq_done)
            ve(lambda e: e.scalar_tensor_tensor(
                out=var, in0=pb[b1][:], scalar=1.0 / 1024, in1=t1,
                op0=OP.mult, op1=OP.subtract))
            used(b1, "vector")
            var_done = p.count(s_v)
            p.wait("scalar", s_v, var_done)
            sc(lambda e: e.activation(out=t1, in_=var, func=AF.Sqrt, bias=1e-5))
            sqrt_done = p.count(s_sc)
            p.wait("vector", s_sc, sqrt_done)
            ve(lambda e: e.reciprocal(out=rb, in_=t1))
            ve(lambda e: e.tensor_tensor(out=var, in0=mb, in1=rb, op=OP.mult))
            used(("lnrows", id(mb)), "vector")
            if DEBUG == "ln" and ch == 0 and not dbg_done:
                dbg_done.append(1)
                dv = p.count(s_v)
                p.wait("scalar", s_v, dv)
                dbg_f32(mb, 0, t1)
                dbg_f32(rb, 512, t1)
                dbg_f32(var, 1024, t1)

        dma("sync", ident[:], dr["ident16"])
        dma("sync", sv[:, 0:48], dr["smallv"])
        dma("sync", sd_[:], dr["smalld"])
        sm_w = p.count(s_w)
        ve(lambda e: e.memset(ones16[:], 1.0))
        ve(lambda e: e.memset(cpool[:, 0:1], 1e-5))
        ve(lambda e: e.memset(cpool[:, 1:2], 1.0))
        ve(lambda e: e.memset(stA[:], 0.0))
        ve(lambda e: e.memset(stB[:], 0.0))
        ve(lambda e: e.memset(stS[:], 0.0))
        ve(lambda e: e.memset(ctail[:], 0.0))
        ve(lambda e: e.memset(xsh0[:], 0.0))
        p.wait("scalar", s_w, sm_w)
        sc(lambda e: e.activation(out=cpool[:, 2:3], in_=cpool[:, 0:1],
                                  func=AF.Copy, accum_out=cpool[:, 3:4]))
        sc(lambda e: e.activation(out=ident32[:], in_=ident[:], func=AF.Copy))
        sc(lambda e: e.activation(out=lam8[:], in_=sv[:, 0:8], func=AF.Exp))
        sc(lambda e: e.activation(out=lam8[:], in_=lam8[:], func=AF.Exp, scale=-1.0))
        sc(lambda e: e.activation(out=etf8[:], in_=sv[:, 8:16], func=AF.Exp))
        sc(lambda e: e.activation(out=bias05[:], in_=sv[:, 16:24], func=AF.Copy,
                                  scale=0.5))
        sc(lambda e: e.activation(out=aneg[:], in_=sd_[:, 112:112 + DSL * NS],
                                  func=AF.Exp))
        d0 = p.count(s_sc)
        p.wait("vector", s_sc, d0)
        ve(lambda e: e.tensor_scalar(out=aneg[:], in0=aneg[:], scalar1=-1.0,
                                     scalar2=None, op0=OP.mult))
        for j, src0 in ((0, 24), (1, 32), (2, 40)):
            ve(lambda e, j=j, s0=src0: e.tensor_scalar(
                out=sv[:, 48 + 8 * j:56 + 8 * j], in0=sv[:, s0:s0 + 8],
                scalar1=-1.0, scalar2=1.0, op0=OP.mult, op1=OP.add))
        smalls_v = p.count(s_v)

        with ExitStack() as prep:
            stg = sb("stg", [128, 4096], F32, c=prep)
            stg16 = sb("stg16", [128, 2048], F16, c=prep)
            fr = sb("fr", [128, 8 * C], F16, c=prep)
            fi = sb("fi", [128, 8 * C], F16, c=prep)
            w16a = sb("w16a", [128, 8 * C], F16, c=prep)
            w16b = sb("w16b", [128, 8 * C], F16, c=prep)
            w16c = sb("w16c", [128, 8 * C], F16, c=prep)
            ptre = sb("ptre", [128, 8 * C], F16, c=prep)
            ptim = sb("ptim", [128, 8 * C], F16, c=prep)

            dma("sync", stg[0:8, 0:1024], dr["be"])
            dma("sync", stg[0:1, 1024:2048], dr["bgrow"])
            dma("sync", stg[:, 2048:2048 + DSL * 96], dr["wxT_t"])
            w0 = p.count(s_w)
            p.wait("scalar", s_w, w0)
            p.wait("scalar", s_v, smalls_v)
            sc(lambda e: e.activation(out=be16[0:8, :], in_=stg[0:8, 0:1024],
                                      func=AF.Copy))
            sc(lambda e: e.activation(out=bg16, in_=stg[0:1, 1024:2048],
                                      func=AF.Copy))
            bgc = p.count(s_sc)
            p.wait("sync", s_sc, bgc)
            dma("sync", dr["s_bg"], bg16)
            sc(lambda e: e.activation(out=wx16[:], in_=stg[:, 2048:2048 + DSL * 96],
                                      func=AF.Copy))
            used(("stg",), "scalar")
            acquire(("stg",), "sync")
            w0b = dma("sync", stg[0:RK, 0:DI], dr["wdtT_t"])
            p.wait("scalar", s_w, w0b)
            sc(lambda e: e.activation(
                out=wdt16[:], in_=stg[0:RK, 0:DI], func=AF.Copy))
            used(("stg",), "scalar")

            for ct in range(CT):
                dma("sync", fr[:, ts(ct, C)], dr["fr16"][ct])
                dma("sync", fi[:, ts(ct, C)], dr["fi16"][ct])
            fw = p.count(s_w)

            def cast_sq(dst16, src_t, scale=1.0, extra=None, extrascale=None):
                for g in range(2):
                    wdone = None
                    acquire(("stg",), "sync")
                    for j in range(4):
                        wdone = dma("sync", stg[:, ts(j, C)], src_t[g * 4 + j])
                    p.wait("scalar", s_w, wdone)
                    for j in range(4):
                        ctt = g * 4 + j
                        sc(lambda e, j=j, ctt=ctt: e.activation(
                            out=dst16[:, ts(ctt, C)], in_=stg[:, ts(j, C)],
                            func=AF.Copy, scale=scale))
                        if extra is not None:
                            sc(lambda e, j=j, ctt=ctt: e.activation(
                                out=extra[:, ts(ctt, C)], in_=stg[:, ts(j, C)],
                                func=AF.Copy, scale=extrascale))
                    used(("stg",), "scalar")

            cast_sq(w16a, dr["wfr"])
            cast_sq(w16b, dr["wfi"], extra=w16c, extrascale=-1.0)
            casts_sc = p.count(s_sc)
            p.wait("tensor", s_w, fw)
            p.wait("tensor", s_sc, casts_sc)

            def big_mm(dst16, lhs1, rhs1, lhs2, rhs2, scale=1.0):
                for och in range(8):
                    for half in range(2):
                        b = half
                        acquire(b, "tensor")
                        for k in range(8):
                            te(lambda e, o=och, h=half, k=k, b=b, l=lhs1, r=rhs1:
                               e.matmul(out=pb[b][:],
                                        lhsT=l[:, ts(k, C)][:, ts(o)],
                                        rhs=r[:, ts(k, C)][:, ts(h, 512)],
                                        start=(k == 0), stop=False), inc=False)
                        for k in range(8):
                            te(lambda e, o=och, h=half, k=k, b=b, l=lhs2, r=rhs2:
                               e.matmul(out=pb[b][:],
                                        lhsT=l[:, ts(k, C)][:, ts(o)],
                                        rhs=r[:, ts(k, C)][:, ts(h, 512)],
                                        start=False, stop=(k == 7)),
                               inc=(k == 7))
                        mmd = p.count(s_te)
                        p.wait("scalar", s_te, mmd)
                        sc(lambda e, o=och, h=half, b=b, d=dst16, sca=scale:
                           e.activation(out=d[:, ts(o, C)][:, ts(h, 512)],
                                        in_=pb[b][:], func=AF.Copy, scale=sca))
                        used(b, "scalar")

            big_mm(ptre, w16a, fr, w16c, fi)
            big_mm(ptim, w16a, fi, w16b, fr)
            m16 = w16c
            big_mm(m16, ptre, fr, ptim, fi, scale=0.5 / 1024)
            mdone = p.count(s_sc)
            p.wait("sync", s_sc, mdone)
            for ct in range(CT):
                dma("sync", dr["s_m"][ct], m16[:, ts(ct, C)])

            fold_plan = [("wkT_t", 0, 24, 0), ("wkT_t", 1, 48, 0),
                         ("wvT_t", 2, 32, 0), ("wvT_t", 3, 56, 0),
                         ("wrT_t", 4, 40, 0), ("wrT_t", 5, 64, 0)]
            for src, vi, col0, _ in fold_plan:
                for sl in range(8):
                    acquire(("stg",), "sync")
                    wdone = dma("sync", stg[:, 0:1024], dr[src][sl])
                    p.wait("scalar", s_w, wdone)
                    for ct in range(CT):
                        sc(lambda e, ct=ct, c0=col0: e.activation(
                            out=stg16[:, ts(ct)], in_=stg[:, ts(ct)],
                            func=AF.Copy, scale=sv[:, c0 + ct:c0 + ct + 1]))
                    used(("stg",), "scalar")
                    dn = p.count(s_sc)
                    p.wait("sync", s_sc, dn)
                    dma("sync", dr["s_kv"][vi][sl], stg16[:, 0:1024])
                    used(("stg16",), "sync")
                    acquire(("stg16",), "scalar")

            def plain_cast(src_t, dst_t, n_t, rows=128, cols=1024, scale=1.0):
                for s in range(n_t):
                    acquire(("stg",), "sync")
                    wdone = dma("sync", stg[0:rows, 0:cols], src_t[s])
                    p.wait("scalar", s_w, wdone)
                    sc(lambda e, r=rows, cl=cols, sca=scale: e.activation(
                        out=stg16[0:r, 0:cl], in_=stg[0:r, 0:cl], func=AF.Copy,
                        scale=sca))
                    used(("stg",), "scalar")
                    dn = p.count(s_sc)
                    p.wait("sync", s_sc, dn)
                    dma("sync", dst_t[s], stg16[0:rows, 0:cols])
                    used(("stg16",), "sync")
                    acquire(("stg16",), "scalar")

            plain_cast(dr["woT_t"], dr["s_wo"], 8, scale=0.5)
            plain_cast(dr["winT_t"], dr["s_win"], 32)
            plain_cast(dr["wg_t"], dr["s_wg"], 8)
            for ee in range(E):
                for ct in range(CT):
                    acquire(("stg",), "sync")
                    wdone = dma("sync", stg[:, 0:1024], dr["we_t"][ee][ct])
                    p.wait("scalar", s_w, wdone)
                    sc(lambda e: e.activation(out=stg16[:, 0:1024],
                                              in_=stg[:, 0:1024], func=AF.Copy))
                    used(("stg",), "scalar")
                    dn = p.count(s_sc)
                    p.wait("sync", s_sc, dn)
                    for fh in range(2):
                        dma("sync",
                            dr["s_we"][ee][fh][:, ct * 512:(ct + 1) * 512],
                            stg16[:, fh * 512:fh * 512 + 512])
                    used(("stg16",), "sync")
                    acquire(("stg16",), "scalar")
            for co in range(8):
                acquire(("stg",), "sync")
                wdone = dma("sync", stg[:, 0:2048], dr["woutT_t"][co])
                p.wait("scalar", s_w, wdone)
                sc(lambda e: e.activation(out=stg16[:, 0:2048],
                                          in_=stg[:, 0:2048], func=AF.Copy))
                used(("stg",), "scalar")
                dn = p.count(s_sc)
                p.wait("sync", s_sc, dn)
                dma("sync", dr["s_wout"][co], stg16[:, 0:2048])
                used(("stg16",), "sync")
                acquire(("stg16",), "scalar")
            barrier()

        x_res = sb("x_res", [128, CT * T])
        for ct in range(CT):
            dma("sync", x_res[:, ts(ct, T)], dr["xT"][ct])
        barrier()

        def ffn0_phase(tag):
            with ExitStack() as sco:
                m16 = sb(f"{tag}m", [128, 8 * C], F16, c=sco)
                xn16 = sb(f"{tag}xn", [128, CT * TC], F16, c=sco)
                xf16 = sb(f"{tag}xf", [128, CT * TC], F16, c=sco)
                mb = sb(f"{tag}mb", [128, TC], c=sco)[:]
                rb = sb(f"{tag}rb", [128, TC], c=sco)[:]
                t1 = sb(f"{tag}t1", [128, TC], c=sco)[:]
                var = sb(f"{tag}v", [128, TC], c=sco)[:]
                for ct in range(CT):
                    dma("sync", m16[:, ts(ct, C)], dr["s_m"][ct])
                mw = p.count(s_w)
                p.wait("tensor", s_w, mw)
                if DEBUG == "m16" and tag == "p1":
                    p.wait("scalar", s_w, mw)
                    for ct in range(CT):
                        dbg_f32(m16[:, ts(ct, C)], ct * 1024, mb)
                for ch in range(NCH):
                    ln_rows(ch, xf16, None, mb, rb, t1, var, 0, 1)
                    acquire(("xn", id(xn16)), "vector")
                    for ct in range(CT):
                        xs = x_res[:, ts(ct, T)][:, ts(ch, TC)]
                        ve(lambda e, xs=xs: e.tensor_tensor(
                            out=t1, in0=xs, in1=rb, op=OP.mult))
                        ve(lambda e, ct=ct: e.tensor_tensor(
                            out=xn16[:, ts(ct, TC)], in0=t1, in1=var,
                            op=OP.subtract))
                    app = p.count(s_v)
                    for co in range(CT):
                        b = 2 + (co % 2)
                        acquire(b, "tensor")
                        p.wait("tensor", s_v, app)
                        for k in range(CT):
                            te(lambda e, co=co, k=k, b=b: e.matmul(
                                out=pb[b][:], lhsT=m16[:, ts(k, C)][:, ts(co)],
                                rhs=xn16[:, ts(k, TC)],
                                start=(k == 0), stop=(k == CT - 1)),
                               inc=(k == CT - 1))
                        used(("xn", id(xn16)), "tensor")
                        mmd = p.count(s_te)
                        p.wait("vector", s_te, mmd)
                        xs = x_res[:, ts(co, T)][:, ts(ch, TC)]
                        ve(lambda e, b=b, xs=xs, co=co: e.scalar_tensor_tensor(
                            out=t1, in0=pb[b][:], scalar=bias05[:, co:co + 1],
                            in1=xs, op0=OP.add, op1=OP.add))
                        used(b, "vector")
                        ve(lambda e, xs=xs: e.tensor_copy(out=xs, in_=t1))
                    used(("lnrows", id(mb)), "vector")
                barrier()

        if PHASES > 1 and 1 not in SKIPS:
            ffn0_phase("p1")

        def rwkv_phase():
            with ExitStack() as sco:
                xn16 = sb("p2xn", [128, CT * (TC + 1)], F16, c=sco)
                xf16 = sb("p2xf", [128, CT * TC], F16, c=sco)
                mb = sb("p2mb", [128, TC], c=sco)[:]; rb = sb("p2rb", [128, TC], c=sco)[:]
                t1 = sb("p2t1", [128, TC], c=sco)[:]; var = sb("p2v", [128, TC], c=sco)[:]
                kv16 = [sb("p2kv0", [128, 6 * C], F16, c=sco)] * 2
                wo16 = [sb(f"p2wo{i}", [128, C], F16, c=sco) for i in range(2)]
                rk16 = sb("p2rk", [128, 8 * TC], F16, c=sco)
                ek = sb("p2ek", [128, TC], c=sco)[:]; uu = sb("p2u", [128, TC], c=sco)[:]
                As = sb("p2A", [128, TC + 1], c=sco)
                Bs = sb("p2B", [128, TC + 1], c=sco)
                num = sb("p2n", [128, TC], c=sco)[:]; den = sb("p2d", [128, TC], c=sco)[:]
                rec = sb("p2r", [128, TC], c=sco)[:]; th = sb("p2th", [128, TC], c=sco)[:]

                def xnsl(ct):
                    return xn16[:, ct * (TC + 1):(ct + 1) * (TC + 1)]

                for ch in range(NCH):
                    ln_rows(ch, xf16, None, mb, rb, t1, var, 0, 1)
                    acquire("xn2", "vector")
                    for ct in range(CT):
                        ve(lambda e, ct=ct: e.tensor_copy(
                            out=xnsl(ct)[:, 0:1], in_=xsh0[:, ct:ct + 1]))
                    for ct in range(CT):
                        xs = x_res[:, ts(ct, T)][:, ts(ch, TC)]
                        ve(lambda e, xs=xs: e.tensor_tensor(
                            out=t1, in0=xs, in1=rb, op=OP.mult))
                        ve(lambda e, ct=ct: e.tensor_tensor(
                            out=xnsl(ct)[:, 1:TC + 1], in0=t1, in1=var,
                            op=OP.subtract))
                    for ct in range(CT):
                        ve(lambda e, ct=ct: e.tensor_copy(
                            out=xsh0[:, ct:ct + 1], in_=xnsl(ct)[:, TC:TC + 1]))
                    xn_done = p.count(s_v)
                    used(("lnrows", id(mb)), "vector")
                    acquire("rk2", "vector")

                    for sl in range(8):
                        kvb = kv16[sl % 2]
                        acquire(("kv", id(kvb)), "sync")
                        wvd = None
                        for v6 in range(6):
                            wvd = dma("sync", kvb[:, ts(v6, C)],
                                      dr["s_kv"][v6][sl])
                        p.wait("tensor", s_w, wvd)
                        p.wait("tensor", s_v, xn_done)
                        for wi in range(3):
                            b = wi
                            acquire(b, "tensor")
                            for ct in range(CT):
                                te(lambda e, wi=wi, ct=ct, b=b, kvb=kvb: e.matmul(
                                    out=pb[b][:],
                                    lhsT=kvb[:, ts(2 * wi, C)][:, ts(ct)],
                                    rhs=xnsl(ct)[:, 1:TC + 1],
                                    start=(ct == 0), stop=False), inc=False)
                            for ct in range(CT):
                                te(lambda e, wi=wi, ct=ct, b=b, kvb=kvb: e.matmul(
                                    out=pb[b][:],
                                    lhsT=kvb[:, ts(2 * wi + 1, C)][:, ts(ct)],
                                    rhs=xnsl(ct)[:, 0:TC],
                                    start=False, stop=(ct == CT - 1)),
                                   inc=(ct == CT - 1))
                        used("xn2", "tensor")
                        used(("kv", id(kvb)), "tensor")
                        mmd = p.count(s_te)
                        p.wait("scalar", s_te, mmd - 2)
                        sc(lambda e: e.activation(out=ek, in_=pb[0][:], func=AF.Exp))
                        used(0, "scalar")
                        ek_done = p.count(s_sc)
                        p.wait("scalar", s_te, mmd)
                        sc(lambda e: e.activation(out=th, in_=pb[2][:],
                                                  func=AF.Tanh, scale=0.5))
                        used(2, "scalar")
                        th_done = p.count(s_sc)
                        p.wait("vector", s_sc, ek_done)
                        p.wait("vector", s_te, mmd - 1)
                        ve(lambda e: e.tensor_tensor(out=uu, in0=ek, in1=pb[1][:],
                                                     op=OP.mult))
                        used(1, "vector")
                        ve(lambda e, sl=sl: e.tensor_copy(
                            out=As[:, 0:1], in_=stA[:, sl:sl + 1]))
                        ve(lambda e, sl=sl: e.tensor_copy(
                            out=Bs[:, 0:1], in_=stB[:, sl:sl + 1]))
                        ve(lambda e, sl=sl: e.tensor_tensor_scan(
                            out=As[:, 1:TC + 1],
                            data0=lam8[:, sl:sl + 1].broadcast_to((128, TC)),
                            data1=uu, initial=As[:, 0:1],
                            op0=OP.mult, op1=OP.add))
                        ve(lambda e, sl=sl: e.tensor_tensor_scan(
                            out=Bs[:, 1:TC + 1],
                            data0=lam8[:, sl:sl + 1].broadcast_to((128, TC)),
                            data1=ek, initial=Bs[:, 0:1],
                            op0=OP.mult, op1=OP.add))
                        ve(lambda e, sl=sl: e.tensor_copy(
                            out=stA[:, sl:sl + 1], in_=As[:, TC:TC + 1]))
                        ve(lambda e, sl=sl: e.tensor_copy(
                            out=stB[:, sl:sl + 1], in_=Bs[:, TC:TC + 1]))
                        ve(lambda e, sl=sl: e.scalar_tensor_tensor(
                            out=num, in0=uu, scalar=etf8[:, sl:sl + 1],
                            in1=As[:, 0:TC], op0=OP.mult, op1=OP.add))
                        ve(lambda e, sl=sl: e.scalar_tensor_tensor(
                            out=den, in0=ek, scalar=etf8[:, sl:sl + 1],
                            in1=Bs[:, 0:TC], op0=OP.mult, op1=OP.add))
                        ve(lambda e: e.reciprocal(out=rec, in_=den))
                        ve(lambda e: e.tensor_tensor(out=num, in0=num, in1=rec,
                                                     op=OP.mult))
                        p.wait("vector", s_sc, th_done)
                        ve(lambda e, sl=sl: e.scalar_tensor_tensor(
                            out=rk16[:, ts(sl, TC)], in0=th, scalar=1.0,
                            in1=num, op0=OP.add, op1=OP.mult))
                        if DEBUG == "rwkv" and ch == 0 and sl == 0 and not dbg_done:
                            dbg_done.append(1)
                            dv = p.count(s_v)
                            p.wait("scalar", s_v, dv)
                            dbg_f32(ek, 0, dbgst[:])
                            dbg_f32(uu, 512, dbgst[:])
                            dbg_f32(As[:, 1:TC + 1], 1024, dbgst[:])
                            dbg_f32(Bs[:, 1:TC + 1], 1536, dbgst[:])
                            dbg_f32(num, 2048, dbgst[:])
                            dbg_f32(th, 2560, dbgst[:])
                            dbg_f32(rk16[:, 0:TC], 3072, dbgst[:])
                            dbg_f32(xn16[:, 0:TC + 1], 3584, dbgst[:])
                    rk_done = p.count(s_v)
                    p.wait("tensor", s_v, rk_done)
                    for co in range(CT):
                        wob = wo16[co % 2]
                        acquire(("p2wo", id(wob)), "sync")
                        wod = dma("sync", wob[:], dr["s_wo"][co])
                        b = 3 + (co % 2)
                        acquire(b, "tensor")
                        p.wait("tensor", s_w, wod)
                        for ds in range(8):
                            te(lambda e, co=co, ds=ds, b=b, wob=wob: e.matmul(
                                out=pb[b][:], lhsT=wob[:, ts(ds)],
                                rhs=rk16[:, ts(ds, TC)],
                                start=(ds == 0), stop=(ds == 7)), inc=(ds == 7))
                        used(("p2wo", id(wob)), "tensor")
                        mmd = p.count(s_te)
                        p.wait("vector", s_te, mmd)
                        xs = x_res[:, ts(co, T)][:, ts(ch, TC)]
                        ve(lambda e, b=b, xs=xs: e.tensor_tensor(
                            out=xs, in0=pb[b][:], in1=xs, op=OP.add))
                        used(b, "vector")
                    used("rk2", "tensor")
                    acquire("xn2", "vector")
                barrier()

        if PHASES > 2 and 2 not in SKIPS:
            rwkv_phase()
        if PHASES > 3 and 3 not in SKIPS:
            ffn0_phase("p3")

        def moe_phase(tag):
            p.serial = bool(MOESER) or bool(SERALL)
            with ExitStack() as sco:
                xn16 = sb(f"{tag}xn", [128, CT * TC], F16, c=sco)
                xf16 = sb(f"{tag}xf", [128, CT * TC], F16, c=sco)
                mb = sb(f"{tag}mb", [128, TC], c=sco)[:]
                rb = sb(f"{tag}rb", [128, TC], c=sco)[:]
                t1 = sb(f"{tag}t1", [128, TC], c=sco)[:]
                var = sb(f"{tag}v", [128, TC], c=sco)[:]
                wu = sb(f"{tag}wu", [128, 8 * C], F16, c=sco)
                wg16 = wu
                we16 = [wu[:, 0:8 * 512], wu[:, 8 * 512:8 * C]]
                mx16 = [sb(f"{tag}mx0", [128, CT * TC], F16, c=sco)] * 2
                q32 = sb(f"{tag}q", [128, 1024], c=sco)
                mcol = sb(f"{tag}mc", [128, 4 * 8], c=sco)
                mrow = sb(f"{tag}mr", [8, 512], F16, c=sco)
                mb16 = sb(f"{tag}mb16", [128, 2 * TC], F16, c=sco)
                outT = sb(f"{tag}oT", [128, 4 * 512], c=sco)
                p05 = sb(f"{tag}p05", [128, 4], c=sco)
                mxc = sb(f"{tag}mxc", [128, 8], c=sco)
                eidxu = sb(f"{tag}eiu", [128, 1], U32, c=sco)
                eidxf = sb(f"{tag}eif", [128, 1], c=sco)
                sumc = sb(f"{tag}sum", [128, 2], c=sco)
                iota16 = sb(f"{tag}io", [128, 1024], F16, c=sco)
                p.emit("gpsimd", lambda e: e.iota(
                    out=iota16[:], pattern=[[1, 1024]], base=-2048,
                    channel_multiplier=0,
                    allow_small_or_imprecise_dtypes=True), inc=s_g, n=1)
                p.wait("vector", s_g, p.count(s_g))

                for ch in range(NCH):
                    acquire(("wu", id(we16[0])), "sync")
                    acquire(("wu", id(we16[1])), "sync")
                    for ct in range(CT):
                        dma("sync", wg16[:, ts(ct, C)], dr["s_wg"][ct])
                    wg_w = p.count(s_w)
                    p.wait("tensor", s_w, wg_w)
                    ln_rows(ch, xf16, None, mb, rb, t1, var, 0, 1)
                    acquire(("xn", id(xn16)), "vector")
                    for ct in range(CT):
                        xs = x_res[:, ts(ct, T)][:, ts(ch, TC)]
                        ve(lambda e, xs=xs: e.tensor_tensor(
                            out=t1, in0=xs, in1=rb, op=OP.mult))
                        ve(lambda e, ct=ct: e.tensor_tensor(
                            out=xn16[:, ts(ct, TC)], in0=t1, in1=var,
                            op=OP.subtract))
                    app = p.count(s_v)
                    used(("lnrows", id(mb)), "vector")
                    acquire(("lnxf", id(xf16)), "gpsimd")
                    bgw = dma("gpsimd", xf16[0:1, 0:1024], dr["s_bg"])
                    p.wait("tensor", s_g, bgw)
                    acquire(("mcol",), "vector")
                    acquire(("p05",), "vector")
                    for su in range(4):
                        acquire("pbig", "tensor")
                        p.wait("tensor", s_v, app)
                        for half in range(2):
                            for k in range(CT):
                                te(lambda e, su=su, h=half, k=k: e.matmul(
                                    out=pbig[:, ts(h, 512)],
                                    lhsT=xn16[:, ts(k, TC)][:, ts(su)],
                                    rhs=wg16[:, ts(k, C)][:, ts(h, 512)],
                                    start=(k == 0), stop=False), inc=False)
                            te(lambda e, h=half: e.matmul(
                                out=pbig[:, ts(h, 512)],
                                lhsT=ones16[0:1, :],
                                rhs=xf16[0:1, ts(h, 512)],
                                start=False, stop=True), inc=True)
                        used(("wu", id(we16[0])), "tensor")
                        used(("wu", id(we16[1])), "tensor")
                        used(("xn", id(xn16)), "tensor")
                        used(("lnxf", id(xf16)), "tensor")
                        mmd = p.count(s_te)
                        if DEBUG == "moe2" and su == 0 and not dbg_done:
                            dbg_done.append(1)
                            p.wait("scalar", s_te, mmd)
                            p.wait("scalar", s_v, app)
                            dbg_f32(pbig[:], 0, dbgst[:])
                            dbg_f32(xn16[:, 0:1024], 1024, dbgst[:])
                            dbg_f32(wg16[:, 0:1024], 2048, dbgst[:])
                            p.wait("vector", s_sc, p.count(s_sc))
                        p.wait("vector", s_te, mmd)
                        for h in range(2):
                            ve(lambda e, h=h: e.reduce_max(
                                out=mxc[:, 2 + h:3 + h], in_=pbig[:, ts(h, 512)],
                                axis=mybir.AxisListType.X))
                        vfence()
                        ve(lambda e: e.tensor_tensor(
                            out=mxc[:, 0:1], in0=mxc[:, 2:3], in1=mxc[:, 3:4],
                            op=OP.max))
                        vfence()
                        ve(lambda e: e.tensor_scalar(
                            out=mxc[:, 1:2], in0=mxc[:, 0:1], scalar1=-1.0,
                            scalar2=None, op0=OP.mult))
                        neg_done = p.count(s_v)
                        acquire(("q32",), "scalar")
                        p.wait("scalar", s_v, neg_done)
                        for h in range(2):
                            sc(lambda e, h=h: e.activation(
                                out=q32[:, ts(h, 512)], in_=pbig[:, ts(h, 512)],
                                func=AF.Exp, bias=mxc[:, 1:2], scale=1.0))
                        q_done = p.count(s_sc)
                        p.wait("vector", s_sc, q_done)
                        for h in range(2):
                            ve(lambda e, h=h: e.reduce_sum(
                                out=sumc[:, h:h + 1], in_=q32[:, ts(h, 512)],
                                axis=mybir.AxisListType.X))
                        vfence()
                        ve(lambda e: e.tensor_tensor(
                            out=sumc[:, 0:1], in0=sumc[:, 0:1],
                            in1=sumc[:, 1:2], op=OP.add))
                        vfence()
                        ve(lambda e: e.tensor_scalar(
                            out=sumc[:, 0:1], in0=sumc[:, 0:1], scalar1=2.0,
                            scalar2=None, op0=OP.mult))
                        vfence()
                        ve(lambda e, su=su: e.reciprocal(out=p05[:, su:su + 1],
                                                         in_=sumc[:, 0:1]))
                        for h in range(2):
                            ve(lambda e, h=h: e.tensor_scalar(
                                out=q32[:, ts(h, 512)], in0=pbig[:, ts(h, 512)],
                                scalar1=mxc[:, 0:1], scalar2=None,
                                op0=OP.is_equal))
                        used("pbig", "vector")
                        ve(lambda e: e.tensor_tensor(
                            out=q32[:], in0=q32[:], in1=iota16[:], op=OP.mult))
                        ve(lambda e: e.tensor_reduce(
                            out=eidxf[:], in_=q32[:],
                            axis=mybir.AxisListType.X, op=OP.min))
                        used(("q32",), "vector")
                        vfence()
                        ve(lambda e: e.tensor_scalar(
                            out=eidxf[:], in0=eidxf[:], scalar1=2048.0,
                            scalar2=None, op0=OP.add))
                        vfence()
                        ve(lambda e: e.tensor_copy(out=eidxu[:], in_=eidxf[:]))
                        vfence()
                        ve(lambda e: e.tensor_scalar(
                            out=eidxu[:], in0=eidxu[:], scalar1=7,
                            scalar2=None, op0=OP.bitwise_and))
                        vfence()
                        ve(lambda e: e.tensor_copy(out=eidxf[:], in_=eidxu[:]))
                        vfence()
                        for ee in range(E):
                            ve(lambda e, su=su, ee=ee: e.tensor_scalar(
                                out=mcol[:, su * 8 + ee:su * 8 + ee + 1],
                                in0=eidxf[:], scalar1=float(ee), scalar2=None,
                                op0=OP.is_equal))
                        if DEBUG == "moe" and su == 0 and not dbg_done:
                            dbg_done.append(1)
                            dv = p.count(s_v)
                            p.wait("scalar", s_v, dv)
                            dbg_f32(q32[:], 0, dbgst[:])
                            dbg_f32(mxc[:, 0:8], 1024, dbgst[:])
                            dbg_f32(sumc[:, 0:1], 1032, dbgst[:])
                            dbg_f32(p05[:, 0:1], 1033, dbgst[:])
                            dbg_f32(eidxf[:], 1034, dbgst[:])
                            dbg_f32(xn16[:, 0:512], 2048, dbgst[:])
                    mask_done = p.count(s_v)
                    p.wait("tensor", s_v, mask_done)
                    acquire(("mrow",), "scalar")
                    for su in range(4):
                        b = su % 2
                        acquire(b, "tensor")
                        te(lambda e, su=su, b=b: e.transpose(
                            out=pb[b][0:8, 0:128], in_=mcol[:, su * 8:su * 8 + 8],
                            identity=ident32[:]))
                        trd = p.count(s_te)
                        p.wait("scalar", s_te, trd)
                        sc(lambda e, su=su, b=b: e.activation(
                            out=mrow[:, ts(su)], in_=pb[b][0:8, 0:128],
                            func=AF.Copy))
                        used(b, "scalar")
                    used(("mcol",), "tensor")
                    mrow_done = p.count(s_sc)
                    p.wait("tensor", s_sc, mrow_done)
                    p.wait("gpsimd", s_sc, mrow_done)
                    fm_done = p.count(s_sc)
                    p.wait("tensor", s_sc, mrow_done)
                    if MOEBAR & 1:
                        barrier()
                    for fh in range(2 if MOESUB >= 2 else 0):
                        for ee in range(E):
                            web = we16[ee % 2]
                            acquire(("wu", id(we16[ee % 2])), "sync")
                            wed = dma("sync", web, dr["s_we"][ee][fh])
                            half = ee % 2
                            acquire(("mslot",), "gpsimd")
                            msw = dma("gpsimd", mb16[0:1, 512:1024],
                                      mrow[ee:ee + 1, :])
                            acquire("pbig", "tensor")
                            p.wait("tensor", s_g, msw)
                            te(lambda e, ee=ee, h=half: e.matmul(
                                out=pbig[:, ts(h, 512)],
                                lhsT=ones16[0:1, :],
                                rhs=mb16[0:1, 512:1024],
                                start=True, stop=True))
                            used(("mslot",), "tensor")
                            bd = p.count(s_te)
                            acquire(("mb16", 0), "scalar")
                            p.wait("scalar", s_te, bd)
                            sc(lambda e, ee=ee, h=half: e.activation(
                                out=mb16[:, 0:TC],
                                in_=pbig[:, ts(h, 512)], func=AF.Copy))
                            used("pbig", "scalar")
                            mbd = p.count(s_sc)
                            mx = mx16[ee % 2]
                            acquire(("mx", id(mx)), "vector")
                            p.wait("vector", s_sc, mbd)
                            for ct in range(CT):
                                ve(lambda e, ct=ct, ee=ee, mx=mx: e.tensor_tensor(
                                    out=mx[:, ts(ct, TC)],
                                    in0=xn16[:, ts(ct, TC)],
                                    in1=mb16[:, 0:TC], op=OP.mult))
                            used(("mb16", 0), "vector")
                            mxd = p.count(s_v)
                            if ee == 0:
                                for su in range(4):
                                    acquire(su, "tensor")
                            p.wait("tensor", s_v, mxd)
                            p.wait("tensor", s_w, wed)
                            for su in range(4):
                                for k in range(CT):
                                    te(lambda e, fh=fh, su=su, k=k, ee=ee,
                                       mx=mx, web=web: e.matmul(
                                        out=pb[su][:],
                                        lhsT=mx[:, ts(k, TC)][:, ts(su)],
                                        rhs=web[:, ts(k, 512)],
                                        start=(ee == 0 and k == 0), stop=False),
                                       inc=False)
                            p.emit("tensor", lambda e: e.nop(), inc=s_te)
                            used(("wu", id(we16[ee % 2])), "tensor")
                            used(("mx", id(mx)), "tensor")
                        for su in range(4):
                            te(lambda e, fh=fh, su=su: e.matmul(
                                out=pb[su][:], lhsT=mrow[:, ts(su)],
                                rhs=be16[0:8, ts(fh, 512)],
                                start=False, stop=True), inc=True)
                        used(("mrow",), "tensor")
                        emm = p.count(s_te)
                        acquire(("outT",), "scalar")
                        p.wait("scalar", s_te, emm)
                        for su in range(4):
                            sc(lambda e, su=su: e.activation(
                                out=outT[:, ts(su, 512)],
                                in_=pb[su][:], func=AF.Relu,
                                scale=p05[:, su:su + 1]))
                            used(su, "scalar")
                        used(("p05",), "scalar")
                        ev_done = p.count(s_sc)
                        if (DEBUG == "p4x" and ch == 0 and fh == 0
                                and not dbg_done):
                            dbg_done.append(1)
                            dbg_f32(outT[:, 0:2048], 0, dbgst[:])
                            dbg_f32(mrow[0:8, 0:512], 2048, dbgst[:])
                            dbg_f32(mb16[:, 0:1024], 2560, dbgst[:])
                            dbg_f32(p05[:, 0:4], 3584, dbgst[:])
                            dbg_f32(mcol[:, 0:32], 3588, dbgst[:])
                        p.wait("tensor", s_sc, ev_done)
                        for ft in range(4):
                            b = 4 + (ft % 2)
                            acquire(b, "tensor")
                            for su in range(4):
                                te(lambda e, ft=ft, su=su, b=b: e.transpose(
                                    out=pb[b][:, ts(su)],
                                    in_=outT[:, su * 512 + ft * 128:
                                             su * 512 + (ft + 1) * 128],
                                    identity=ident32[:]), inc=(su == 3))
                            trd = p.count(s_te)
                            if (DEBUG == "p4add" and ch == 0 and fh == 0
                                    and ft == 0 and not dbg_done):
                                dbg_done.append(1)
                                p.wait("scalar", s_te, trd)
                                dbg_f32(pb[b][:], 0, dbgst[:])
                                dbg_f32(outT[:, 0:512], 512, dbgst[:])
                                dbg_f32(x_res[:, 0:512], 1024, dbgst[:])
                                p.wait("vector", s_sc, p.count(s_sc))
                            p.wait("vector", s_te, trd)
                            co = fh * 4 + ft
                            xs = x_res[:, ts(co, T)][:, ts(ch, TC)]
                            if MOESUB >= 3:
                                ve(lambda e, b=b, xs=xs: e.tensor_tensor(
                                    out=xs, in0=pb[b][:], in1=xs, op=OP.add))
                            else:
                                ve(lambda e, b=b: e.tensor_copy(
                                    out=var, in_=pb[b][:]))
                            used(b, "vector")
                        used(("outT",), "tensor")
                        if MOEBAR & 2:
                            barrier()
                    if MOEBAR & 4:
                        barrier()
                    if DEBUG == "p4end" and ch == NCH - 1:
                        dv = p.count(s_v)
                        p.wait("scalar", s_v, dv)
                        dbg_f32(x_res[:, 0:512], 0, dbgst[:])
                        dbg_f32(x_res[:, 7 * T + 3584:7 * T + 4096], 512, dbgst[:])
                barrier()

        if PHASES > 4 and 4 not in SKIPS:
            moe_phase("p4")
            p.serial = bool(SERALL)

        def mamba_phase(sub=99):
            if sub < 1:
                return
            p.serial = bool(P5SER & 1) or bool(SERALL)
            with ExitStack() as sco:
                xn16 = sb("p5xn", [128, CT * TC], F16, c=sco)
                xf16 = sb("p5xf", [128, CT * TC], F16, c=sco)
                mb = sb("p5mb", [128, TC], c=sco)[:]; rb = sb("p5rb", [128, TC], c=sco)[:]
                t1 = sb("p5t1", [128, TC], c=sco)[:]; var = sb("p5v", [128, TC], c=sco)[:]
                wi16 = [sb(f"p5wi{i}", [128, 8 * 128], F16, c=sco) for i in range(2)]
                xin = sb("p5xi", [128, TC + 4], c=sco)
                cv0 = rb
                cv1 = t1
                xcv = sb("p5xcv", [128, DSL * TC], F16, c=sco)
                szb = sb("p5sz", [128, DSL * TC], F16, c=sco)
                xdst = sb("p5xd", [96, TC], F16, c=sco)
                if DEBUG == "p5a":
                    dbg_f32(x_res[:, 0:512], 0, dbgst[:])
                for ch in range(NCH):
                    if DEBUG == "p5a" and ch == 1:
                        dv = p.count(s_v)
                        p.wait("scalar", s_v, dv)
                        dbg_f32(x_res[:, 0:512], 512, dbgst[:])
                        dbg_f32(xn16[:, 0:512], 1024, dbgst[:])
                        dbg_f32(xcv[:, 0:512], 1536, dbgst[:])
                    ln_rows(ch, xf16, None, mb, rb, t1, var, 3, 4)
                    acquire("xn5", "vector")
                    for ct in range(CT):
                        xs = x_res[:, ts(ct, T)][:, ts(ch, TC)]
                        ve(lambda e, xs=xs: e.tensor_tensor(
                            out=t1, in0=xs, in1=rb, op=OP.mult))
                        ve(lambda e, ct=ct: e.tensor_tensor(
                            out=xn16[:, ts(ct, TC)], in0=t1, in1=var,
                            op=OP.subtract))
                    app = p.count(s_v)
                    p.wait("tensor", s_v, app)
                    acquire(2, "tensor")
                    for half, nsl in ((0, DSL), (1, DSL)):
                        for sl in range(nsl):
                            wib = wi16[sl % 2]
                            acquire(("wi", id(wib)), "sync")
                            wid = dma("sync", wib[:], dr["s_win"][half * DSL + sl])
                            b = sl % 2
                            acquire(b, "tensor")
                            p.wait("tensor", s_w, wid)
                            for k in range(CT):
                                te(lambda e, k=k, b=b, wib=wib: e.matmul(
                                    out=pb[b][:], lhsT=wib[:, ts(k)],
                                    rhs=xn16[:, ts(k, TC)],
                                    start=(k == 0), stop=(k == CT - 1)),
                                   inc=(k == CT - 1))
                            used(("wi", id(wib)), "tensor")
                            used("xn5", "tensor")
                            mmd = p.count(s_te)
                            if half == 0:
                                acquire(("xin",), "scalar")
                                sc(lambda e, sl=sl: e.activation(
                                    out=xin[:, 0:3],
                                    in_=ctail[:, sl * 4:sl * 4 + 3],
                                    func=AF.Copy))
                                p.wait("scalar", s_te, mmd)
                                sc(lambda e, b=b: e.activation(
                                    out=xin[:, 3:TC + 3], in_=pb[b][:],
                                    func=AF.Copy))
                                used(b, "scalar")
                                sc(lambda e, sl=sl: e.activation(
                                    out=ctail[:, sl * 4:sl * 4 + 3],
                                    in_=xin[:, TC:TC + 3], func=AF.Copy))
                                xin_done = p.count(s_sc)
                                p.wait("vector", s_sc, xin_done)
                                ve(lambda e, sl=sl: e.tensor_scalar(
                                    out=cv0[:], in0=xin[:, 0:TC],
                                    scalar1=sd_[:, sl * 4:sl * 4 + 1],
                                    scalar2=None, op0=OP.mult))
                                ve(lambda e, sl=sl: e.scalar_tensor_tensor(
                                    out=cv1[:], in0=xin[:, 1:TC + 1],
                                    scalar=sd_[:, sl * 4 + 1:sl * 4 + 2],
                                    in1=cv0[:], op0=OP.mult, op1=OP.add))
                                ve(lambda e, sl=sl: e.scalar_tensor_tensor(
                                    out=cv0[:], in0=xin[:, 2:TC + 2],
                                    scalar=sd_[:, sl * 4 + 2:sl * 4 + 3],
                                    in1=cv1[:], op0=OP.mult, op1=OP.add))
                                ve(lambda e, sl=sl: e.scalar_tensor_tensor(
                                    out=cv1[:], in0=xin[:, 3:TC + 3],
                                    scalar=sd_[:, sl * 4 + 3:sl * 4 + 4],
                                    in1=cv0[:], op0=OP.mult, op1=OP.add))
                                used(("xin",), "vector")
                                cvd = p.count(s_v)
                                p.wait("scalar", s_v, cvd)
                                sc(lambda e, sl=sl: e.activation(
                                    out=xcv[:, ts(sl, TC)], in_=cv1[:],
                                    func=AF.Silu,
                                    bias=sd_[:, 64 + sl:64 + sl + 1]))
                                xcvd = p.count(s_sc)
                                p.wait("tensor", s_sc, xcvd)
                                te(lambda e, sl=sl: e.matmul(
                                    out=pb[2][0:96, :],
                                    lhsT=wx16[:, sl * 96:(sl + 1) * 96],
                                    rhs=xcv[:, ts(sl, TC)],
                                    start=(sl == 0), stop=(sl == DSL - 1)),
                                   inc=(sl == DSL - 1))
                            else:
                                p.wait("scalar", s_te, mmd)
                                sc(lambda e, sl=sl, b=b: e.activation(
                                    out=szb[:, ts(sl, TC)], in_=pb[b][:],
                                    func=AF.Silu))
                                used(b, "scalar")
                        if half == 0:
                            xd_mm = p.count(s_te)
                            p.wait("scalar", s_te, xd_mm)
                            sc(lambda e: e.activation(out=xdst[:], in_=pb[2][0:96, :],
                                                      func=AF.Copy))
                            used(2, "scalar")
                            xd_done = p.count(s_sc)
                            p.wait("gpsimd", s_sc, xd_done)
                            dma("gpsimd", dr["s_xdbl"][ch], xdst[:])
                            dma("gpsimd", dr["s_xcv"][ch], xcv[:])
                            used(("xcvbuf",), "gpsimd")
                    used(("lnrows", id(mb)), "vector")
                    szdone = p.count(s_sc)
                    p.wait("gpsimd", s_sc, szdone)
                    dma("gpsimd", dr["s_sz"][ch], szb[:])
                    gcount = p.count(s_g)
                    p.wait("scalar", s_g, gcount)
                barrier()
            if sub < 2:
                return
            for ct in range(CT):
                dma("sync", dr["s_xsp"][ct], x_res[:, ts(ct, T)])
            barrier()
            if sub < 3:
                return
            p.serial = bool(P5SER & 2) or bool(SERALL)
            with ExitStack() as sco:
                yf16 = sb("p5yf", [128, DSL * TC], F16, c=sco)
                bbcb = sb("p5bb", [128, 2 * NS * TC], F16, c=sco)
                xdbl = [sb("p5cx0", [96, TC], F16, c=sco)] * 2
                dt32 = [sb("p5dt0", [128, TC], c=sco)] * 2
                dtx = sb("p5dtx", [128, TC], F16, c=sco)
                da32 = [sb(f"p5da{i}", [128, TC], c=sco) for i in range(2)]
                u16 = sb("p5u", [128, TC], F16, c=sco)
                s16 = [sb(f"p5s{i}", [128, TC], F16, c=sco) for i in range(2)]
                tmp16 = u16
                yy = sb("p5y0", [128, TC], c=sco)[:]
                xcvs = [sb("p5xv0", [128, TC], F16, c=sco)] * 2
                szs = [sb("p5zs0", [128, TC], F16, c=sco)] * 2
                woc = [sb(f"p5wo{i}", [128, DI // 4], F16, c=sco) for i in range(2)]
                for ch in range(NCH):
                    xdb = xdbl[ch % 2]
                    acquire(("xdbl", id(xdb)), "gpsimd")
                    xdd = dma("gpsimd", xdb[:], dr["s_xdbl"][ch])
                    p.wait("gpsimd", s_g, xdd)
                    acquire(("bbcb",), "scalar")
                    for kind in range(2):
                        acquire(("yf",), "gpsimd")
                        rw = p.emit("gpsimd", lambda e, k=kind, xdb=xdb:
                                    e.dma_start(
                                        out=yf16[0:1, 0:NS * TC],
                                        in_=xdb[64 + k * NS:64 + (k + 1) * NS, :],
                                        allow_slow_non_contiguous=True),
                                    inc=s_g, n=16)
                        p.wait("tensor", s_g, rw)
                        for n in range(NS):
                            b = 4 + (n % 2)
                            slot = n * 2 + kind
                            acquire(b, "tensor")
                            te(lambda e, n=n, b=b: e.matmul(
                                out=pb[b][:], lhsT=ones16[0:1, :],
                                rhs=yf16[0:1, n * TC:(n + 1) * TC],
                                start=True, stop=True))
                            bd = p.count(s_te)
                            p.wait("scalar", s_te, bd)
                            sc(lambda e, slot=slot, b=b: e.activation(
                                out=bbcb[:, slot * TC:(slot + 1) * TC],
                                in_=pb[b][:], func=AF.Copy))
                            used(b, "scalar")
                        used(("yf",), "tensor")
                    used(("xdbl", id(xdb)), "gpsimd")
                    bb_done = p.count(s_sc)
                    p.wait("vector", s_sc, bb_done)
                    acquire(("yf",), "vector")
                    for sl in range(DSL):
                        dtb = dt32[sl % 2]
                        acquire(0, "tensor")
                        te(lambda e, sl=sl, xdb=xdb: e.matmul(
                            out=pb[0][:], lhsT=wdt16[:, ts(sl)],
                            rhs=xdb[0:RK, :], start=True, stop=True))
                        dtd = p.count(s_te)
                        acquire(("dt", id(dtb)), "scalar")
                        p.wait("scalar", s_te, dtd)
                        sc(lambda e, sl=sl, dtb=dtb: e.activation(
                            out=dtb[:], in_=pb[0][:], func=AF.Exp,
                            bias=sd_[:, 96 + sl:96 + sl + 1]))
                        used(0, "scalar")
                        sc(lambda e, dtb=dtb: e.activation(
                            out=dtb[:], in_=dtb[:], func=AF.Ln, bias=1.0))
                        dt_done = p.count(s_sc)
                        xcb = xcvs[sl % 2]; szc = szs[sl % 2]
                        acquire(("xcv", id(xcb)), "gpsimd")
                        xcd = dma("gpsimd", xcb[:],
                                  dr["s_xcv"][ch][:, ts(sl, TC)])
                        acquire(("szs", id(szc)), "gpsimd")
                        szd = dma("gpsimd", szc[:],
                                  dr["s_sz"][ch][:, ts(sl, TC)])
                        p.wait("vector", s_sc, dt_done)
                        p.wait("vector", s_g, xcd)
                        acquire(("dtx",), "vector")
                        ve(lambda e, dtb=dtb, xcb=xcb: e.tensor_tensor(
                            out=dtx[:], in0=dtb[:], in1=xcb[:], op=OP.mult))
                        for n in range(NS):
                            dab = da32[n % 2]
                            acquire(("da", id(dab)), "scalar")
                            sc(lambda e, sl=sl, n=n, dab=dab, dtb=dtb:
                               e.activation(out=dab[:], in_=dtb[:], func=AF.Exp,
                                            scale=aneg[:, sl * NS + n:
                                                       sl * NS + n + 1]))
                            dad = p.count(s_sc)
                            ve(lambda e, n=n: e.tensor_tensor(
                                out=u16[:], in0=dtx[:],
                                in1=bbcb[:, (n * 2) * TC:(n * 2 + 1) * TC],
                                op=OP.mult))
                            p.wait("vector", s_sc, dad)
                            sb_ = s16[n % 2]
                            acquire(("s16", id(sb_)), "vector")
                            ve(lambda e, sl=sl, n=n, sb_=sb_, dab=dab:
                               e.tensor_tensor_scan(
                                   out=sb_[:], data0=dab[:], data1=u16[:],
                                   initial=stS[:, sl * NS + n:sl * NS + n + 1],
                                   op0=OP.mult, op1=OP.add))
                            used(("da", id(dab)), "vector")
                            scn = p.count(s_v)
                            p.wait("scalar", s_v, scn)
                            sc(lambda e, sl=sl, n=n, sb_=sb_: e.activation(
                                out=stS[:, sl * NS + n:sl * NS + n + 1],
                                in_=sb_[:, TC - 1:TC], func=AF.Copy))
                            used(("s16", id(sb_)), "scalar")
                            ve(lambda e, n=n, sb_=sb_: e.tensor_tensor(
                                out=tmp16[:], in0=sb_[:],
                                in1=bbcb[:, (n * 2 + 1) * TC:(n * 2 + 2) * TC],
                                op=OP.mult))
                            used(("s16", id(sb_)), "vector")
                            if n == 0:
                                ve(lambda e: e.tensor_copy(out=yy, in_=tmp16[:]))
                            else:
                                ve(lambda e: e.tensor_tensor(
                                    out=yy, in0=tmp16[:], in1=yy, op=OP.add))
                        used(("dt", id(dtb)), "vector")
                        used(("dtx",), "vector")
                        if (DEBUG == "p5c" and ch == 0 and sl == 0
                                and not dbg_done):
                            dbg_done.append(1)
                            dv = p.count(s_v)
                            p.wait("scalar", s_v, dv)
                            dbg_f32(dtb[:], 0, dbgst[:])
                            dbg_f32(dtx[:], 512, dbgst[:])
                            dbg_f32(s16[(NS - 1) % 2][:], 1024, dbgst[:])
                            dbg_f32(yy, 1536, dbgst[:])
                            dbg_f32(bbcb[:, 0:512], 2048, dbgst[:])
                            dbg_f32(bbcb[:, 512:1024], 2560, dbgst[:])
                            dbg_f32(xcvs[0][:], 3072, dbgst[:])
                            dbg_f32(xdbl[0][0:64, 0:512], 3584, dbgst[:])
                        p.wait("vector", s_g, szd)
                        ve(lambda e, szc=szc: e.tensor_tensor(
                            out=yy, in0=yy, in1=szc[:], op=OP.mult))
                        used(("szs", id(szc)), "vector")
                        ve(lambda e, sl=sl, xcb=xcb: e.scalar_tensor_tensor(
                            out=yf16[:, ts(sl, TC)], in0=xcb[:],
                            scalar=sd_[:, 80 + sl:80 + sl + 1], in1=yy,
                            op0=OP.mult, op1=OP.add))
                        used(("xcv", id(xcb)), "vector")
                    used(("bbcb",), "vector")
                    yf_done = p.count(s_v)
                    p.wait("tensor", s_v, yf_done)
                    for co in range(CT):
                        b = 1 + (co % 2)
                        acquire(b, "tensor")
                        for dh in range(4):
                            wob = woc[dh % 2]
                            acquire(("woc", id(wob)), "sync")
                            wod = dma("sync", wob[:],
                                      dr["s_wout"][co][:, dh * 512:(dh + 1) * 512])
                            p.wait("tensor", s_w, wod)
                            for j in range(4):
                                sl = dh * 4 + j
                                te(lambda e, sl=sl, j=j, b=b, wob=wob, dh=dh:
                                   e.matmul(
                                    out=pb[b][:], lhsT=wob[:, ts(j)],
                                    rhs=yf16[:, ts(sl, TC)],
                                    start=(sl == 0), stop=(sl == DSL - 1)),
                                   inc=(j == 3))
                            used(("woc", id(wob)), "tensor")
                        mmd = p.count(s_te)
                        p.wait("scalar", s_te, mmd)
                        xs = x_res[:, ts(co, T)][:, ts(ch, TC)]
                        sc(lambda e, b=b, xs=xs: e.activation(
                            out=xs, in_=pb[b][:], func=AF.Copy))
                        used(b, "scalar")
                    used(("yf",), "tensor")
                    used(("xdbl", id(xdb)), "tensor")
                barrier()
            if sub < 4:
                return
            p.serial = bool(SERALL)
            with ExitStack() as sco:
                bnc = [sb(f"p5bn{i}", [128, T], c=sco) for i in range(2)]
                for ct in range(CT):
                    bb = bnc[ct % 2]
                    acquire(("bnc", id(bb)), "gpsimd")
                    bd = dma("gpsimd", bb[:], dr["s_xsp"][ct])
                    p.wait("vector", s_g, bd)
                    ve(lambda e, ct=ct, bb=bb: e.tensor_tensor(
                        out=x_res[:, ts(ct, T)], in0=bb[:],
                        in1=x_res[:, ts(ct, T)], op=OP.add))
                    used(("bnc", id(bb)), "vector")
                barrier()

        if PHASES > 5:
            mamba_phase(P5SUB)
            p.serial = bool(SERALL)
        if PHASES > 6:
            moe_phase("p6")

        for ct in range(CT):
            dma("sync", out_ap[ct], x_res[:, ts(ct, T)])
        p.wait("sync", s_w)
        barrier()
        p.run(block)
    return nc


_NC_CACHE = {}


def _tile8(w):
    a = w.shape[0] // 128
    b = w.shape[1] // 128
    return np.ascontiguousarray(
        w.reshape(a, 128, b, 128).transpose(2, 1, 0, 3).reshape(b, 128, a * 128))


def kernel(L0_attn_x, L0_attn_w_k, L0_attn_w_v, L0_attn_w_r, L0_attn_w_o,
           L0_attn_time_decay, L0_attn_time_first, L0_attn_time_mix_key,
           L0_attn_time_mix_value, L0_attn_time_mix_receptance,
           L0_ffn_w_freq_real, L0_ffn_w_freq_imag, L0_ffn_bias,
           L1_attn_W_in, L1_attn_conv_w, L1_attn_conv_b, L1_attn_W_x,
           L1_attn_W_dt, L1_attn_b_dt, L1_attn_A_log, L1_attn_D,
           L1_attn_W_out, L1_ffn_Wg, L1_ffn_bg, L1_ffn_We, L1_ffn_be):
    f32 = lambda v: np.ascontiguousarray(np.asarray(v, dtype=np.float32))
    x = f32(L0_attn_x)
    B = x.shape[0]

    n_, d_ = np.meshgrid(np.arange(C), np.arange(C), indexing="ij")
    ang = (-2.0 * np.pi / C) * (n_ * d_ % C)
    fr16 = np.cos(ang).astype(np.float16).reshape(8, 128, C)
    fi16 = np.sin(ang).astype(np.float16).reshape(8, 128, C)
    ident16 = np.eye(128, dtype=np.float16)

    def colpack(v, k):
        return np.ascontiguousarray(f32(v).reshape(k, 128).T)

    smallv = np.zeros((128, 48), np.float32)
    smallv[:, 0:8] = colpack(L0_attn_time_decay, 8)
    smallv[:, 8:16] = colpack(L0_attn_time_first, 8)
    smallv[:, 16:24] = colpack(L0_ffn_bias, 8)
    smallv[:, 24:32] = colpack(np.asarray(L0_attn_time_mix_key).reshape(-1), 8)
    smallv[:, 32:40] = colpack(np.asarray(L0_attn_time_mix_value).reshape(-1), 8)
    smallv[:, 40:48] = colpack(np.asarray(L0_attn_time_mix_receptance).reshape(-1), 8)

    smalld = np.zeros((128, 368), np.float32)
    cw = f32(L1_attn_conv_w).reshape(DI, 4)
    smalld[:, 0:64] = cw.reshape(DSL, 128, 4).transpose(1, 0, 2).reshape(128, 64)
    smalld[:, 64:80] = colpack(L1_attn_conv_b, DSL)
    smalld[:, 80:96] = colpack(L1_attn_D, DSL)
    smalld[:, 96:112] = colpack(L1_attn_b_dt, DSL)
    al = f32(L1_attn_A_log)
    smalld[:, 112:112 + DSL * NS] = (
        al.reshape(DSL, 128, NS).transpose(1, 0, 2).reshape(128, DSL * NS))

    common = {
        "fr16": fr16, "fi16": fi16, "ident16": ident16,
        "wfr": f32(L0_ffn_w_freq_real).reshape(8, 128, C),
        "wfi": f32(L0_ffn_w_freq_imag).reshape(8, 128, C),
        "wkT_t": _tile8(f32(L0_attn_w_k).T),
        "wvT_t": _tile8(f32(L0_attn_w_v).T),
        "wrT_t": _tile8(f32(L0_attn_w_r).T),
        "woT_t": _tile8(f32(L0_attn_w_o).T),
        "winT_t": _tile8(f32(L1_attn_W_in).T),
        "woutT_t": _tile8(f32(L1_attn_W_out).T),
        "wxT_t": np.ascontiguousarray(
            f32(L1_attn_W_x).T.reshape(DSL, 128, 96).transpose(1, 0, 2)
            .reshape(128, DSL * 96)),
        "wdtT_t": f32(L1_attn_W_dt).T,
        "wg_t": f32(L1_ffn_Wg).reshape(8, 128, C),
        "we_t": f32(L1_ffn_We).reshape(E, 8, 128, C),
        "smallv": smallv, "smalld": smalld,
        "be": f32(L1_ffn_be), "bgrow": f32(L1_ffn_bg).reshape(1, C),
    }

    key = "prog"
    if key not in _NC_CACHE:
        _NC_CACHE[key] = build_program()
    nc = _NC_CACHE[key]

    in_maps = []
    for b in range(B):
        m = dict(common)
        m["xT"] = np.ascontiguousarray(x[b].T).reshape(CT, 128, T)
        in_maps.append(m)
    res = run_bass_kernel_spmd(nc, in_maps, core_ids=list(range(B)))
    out = np.empty((B, T, C), np.float32)
    for b in range(B):
        o = res.results[b]["out"].reshape(C, T)
        out[b] = o.T
    return out




# version 04
# speedup vs baseline: 1.3885x; 1.3885x over previous
import numpy as np
import concourse.bass as bass
import concourse.mybir as mybir
from concourse.bass_utils import run_bass_kernel_spmd
from contextlib import ExitStack

F32 = mybir.dt.float32
F16 = mybir.dt.float16
U32 = mybir.dt.uint32
AF = mybir.ActivationFunctionType
OP = mybir.AluOpType

ENGINES = ("sync", "scalar", "vector", "tensor", "gpsimd")
C, CT, T, TC = 1024, 8, 4096, 512
NCH = T // TC
DI, DSL, NS, RK, E = 2048, 16, 16, 64, 8
PHASES = 7
DEBUG = ""
P5SUB = 4
SKIPS = ()
MOESUB = 3
MOEBAR = 0
MOESER = 1
P5SER = 1
SERALL = 0


def ts(i, n=128):
    return slice(i * n, (i + 1) * n)


class Prog:
    def __init__(self):
        self.q = {e: [] for e in ENGINES}
        self.led = {}
        self.serial = False
        self.sems = None

    def emit(self, eng, fn, inc=None, n=1):
        if self.serial and self.sems is not None:
            for oe, osem in self.sems.items():
                if oe != eng:
                    self.wait(eng, osem)
        if inc is not None:
            self.led[id(inc)] = self.led.get(id(inc), 0) + n
            def wrapped(e, fn=fn, inc=inc, n=n):
                fn(e).then_inc(inc, n)
        else:
            def wrapped(e, fn=fn):
                fn(e)
        self.q[eng].append(wrapped)
        return self.led.get(id(inc), 0) if inc is not None else None

    def count(self, sem):
        return self.led.get(id(sem), 0)

    def wait(self, eng, sem, val=None):
        v = self.count(sem) if val is None else val
        if v > 0:
            self.q[eng].append(lambda e, sem=sem, v=v: e.wait_ge(sem, v))

    def run(self, block):
        for e in ENGINES:
            def make(e):
                def _(eng):
                    for fn in self.q[e]:
                        fn(eng)
                return _
            getattr(block, e)(make(e))


def build_program():
    nc = bass.Bass("TRN2", target_bir_lowering=False)
    dr = {}

    def din(name, shape, dtype=F32):
        dr[name] = nc.dram_tensor(name, shape, dtype, kind="ExternalInput").ap()

    def dscr(name, shape, dtype=F16):
        dr[name] = nc.dram_tensor(name, shape, dtype, kind="Internal").ap()

    din("xT", [CT, 128, T])
    din("fr16", [8, 128, C], F16); din("fi16", [8, 128, C], F16)
    din("ident16", [128, 128], F16)
    din("wfr", [8, 128, C]); din("wfi", [8, 128, C])
    din("wkT_t", [8, 128, C]); din("wvT_t", [8, 128, C]); din("wrT_t", [8, 128, C])
    din("woT_t", [8, 128, C])
    din("winT_t", [32, 128, C])
    din("woutT_t", [8, 128, DI])
    din("wxT_t", [128, DSL * 96])
    din("wdtT_t", [RK, DI])
    din("wg_t", [8, 128, C])
    din("we_t", [E, 8, 128, C])
    din("smallv", [128, 48])
    din("smalld", [128, 368])
    din("be", [8, C])
    din("bgrow", [1, C])
    dscr("s_m", [8, 128, C])
    dscr("s_kv", [6, 8, 128, C])
    dscr("s_wo", [8, 128, C])
    dscr("s_win", [32, 128, C])
    dscr("s_wout", [8, 128, DI])
    dscr("s_wg", [8, 128, C])
    dscr("s_bg", [1, C])
    dscr("s_we", [E, 2, 128, 8 * 512])
    dscr("s_xcv", [NCH, 128, DSL * TC])
    dscr("s_sz", [NCH, 128, DSL * TC])
    dscr("s_xdbl", [NCH, 96, TC])
    dscr("s_xsp", [CT, 128, T], F32)
    out_ap = nc.dram_tensor("out", [CT, 128, T], F32, kind="ExternalOutput").ap()
    dbg_ap = nc.dram_tensor("dbg", [128, 8192], F32, kind="ExternalOutput").ap()

    ctx = ExitStack()
    with ctx:
        def sb(name, shape, dtype=F32, c=None):
            return (c or ctx).enter_context(nc.sbuf_tensor(name, shape, dtype))

        ident = sb("ident", [128, 128], F16)
        ones16 = sb("ones16", [128, 128], F16)
        sv = sb("sv", [128, 72])
        lam8 = sb("lam8", [128, 8]); etf8 = sb("etf8", [128, 8])
        bias05 = sb("bias05", [128, 8])
        sd_ = sb("sd_", [128, 368])
        aneg = sb("aneg", [128, DSL * NS])
        be16 = sb("be16", [33, C], F16)
        bg16 = be16[32:33, :]
        wx16 = sb("wx16", [128, DSL * 96], F16)
        wdt16 = sb("wdt16", [RK, DI], F16)
        cpool = sb("cpool", [128, 4])
        ident32 = sb("ident32", [128, 128])
        nc.const_aps.register(F32, 1e-5, cpool[:, 0:1])
        nc.const_aps.register(F32, 1.0, cpool[:, 1:2])
        stA = sb("stA", [128, 8]); stB = sb("stB", [128, 8])
        stS = sb("stS", [128, DSL * NS])
        ctail = sb("ctail", [128, DSL * 4])
        xsh0 = sb("xsh0", [128, 8], F16)

        pb = [ctx.enter_context(nc.psum_tensor(f"pb{i}", [128, 512], F32))
              for i in range(6)]
        pbig = ctx.enter_context(nc.psum_tensor("pbig", [128, 1024], F32))

        sems = {e: ctx.enter_context(nc.semaphore(f"bar_{e}")) for e in ENGINES}
        s_w = ctx.enter_context(nc.semaphore("s_w"))
        s_g = ctx.enter_context(nc.semaphore("s_g"))
        s_te = ctx.enter_context(nc.semaphore("s_te"))
        s_sc = ctx.enter_context(nc.semaphore("s_sc"))
        s_v = ctx.enter_context(nc.semaphore("s_v"))
        block = ctx.enter_context(nc.Block())

        p = Prog()
        SEM = {"tensor": s_te, "scalar": s_sc, "vector": s_v,
               "sync": s_w, "gpsimd": s_g}
        p.sems = SEM
        if SERALL:
            p.serial = True

        def dma(queue, out, in_):
            return p.emit(queue, lambda e, o=out, i=in_: e.dma_start(out=o, in_=i),
                          inc=SEM[queue], n=16)

        def te(fn, inc=True):
            return p.emit("tensor", fn, inc=s_te if inc else None)

        def sc(fn):
            return p.emit("scalar", fn, inc=s_sc)

        def ve(fn):
            return p.emit("vector", fn, inc=s_v)

        def vfence():
            p.wait("vector", s_v, p.count(s_v))

        def barrier():
            p.wait("sync", s_w)
            p.wait("gpsimd", s_g)
            for e in ENGINES:
                p.emit(e, lambda eng: eng.nop(), inc=sems[e], n=1)
            tgt = {e: p.count(sems[e]) for e in ENGINES}
            for e in ENGINES:
                for o in ENGINES:
                    if o != e:
                        p.wait(e, sems[o], tgt[o])

        dbg_done = []
        dbgst = sb("dbgst", [128, 96]) if DEBUG else None
        guards = {}

        def acquire(res, producer_eng):
            for sem, cnt in guards.pop(res, ()):
                p.wait(producer_eng, sem, cnt)

        def used(res, eng):
            guards.setdefault(res, []).append((SEM[eng], p.count(SEM[eng])))

        def dbg_f32(ap, col0, stage):
            n = ap.shape[1]
            sw = stage.shape[1]
            for o in range(0, n, sw):
                w = min(sw, n - o)
                np_ = ap.shape[0]
                sc(lambda e, ap=ap, o=o, w=w, np_=np_: e.activation(
                    out=stage[0:np_, 0:w], in_=ap[:, o:o + w], func=AF.Copy))
                dn = p.count(s_sc)
                p.wait("sync", s_sc, dn)
                p.emit("sync", lambda e, o=o, w=w, np_=np_: e.dma_start(
                    out=dbg_ap[0:np_, col0 + o:col0 + o + w],
                    in_=stage[0:np_, 0:w],
                    allow_slow_non_contiguous=True), inc=s_w, n=16)
                wd = p.count(s_w)
                p.wait("scalar", s_w, wd)

        def ln_rows(ch, xf16, _unused, mb, rb, t1, var, b0, b1):
            acquire(("lnxf", id(xf16)), "scalar")
            acquire(("lnrows", id(mb)), "scalar")
            for ct in range(CT):
                xs = x_res[:, ts(ct, T)][:, ts(ch, TC)]
                sc(lambda e, xs=xs, ct=ct: e.activation(
                    out=xf16[:, ts(ct, TC)], in_=xs, func=AF.Copy))
            cast_done = p.count(s_sc)
            if DEBUG == "ones" and not dbg_done:
                dbg_done.append(1)
                p.wait("scalar", s_sc, cast_done)
                dbg_f32(ones16[:], 0, mb)
                dbg_f32(xf16[:, 0:1024], 1024, mb)
                for ct in range(2):
                    dbg_f32(x_res[:, ts(ct, T)][:, 0:512], 2048 + ct * 512, mb)
            acquire(b0, "tensor"); acquire(b1, "tensor")
            p.wait("tensor", s_sc, cast_done)
            for ct in range(CT):
                te(lambda e, ct=ct: e.matmul(
                    out=pb[b0][:], lhsT=ones16[:], rhs=xf16[:, ts(ct, TC)],
                    start=(ct == 0), stop=(ct == CT - 1)), inc=(ct == CT - 1))
            s_done = p.count(s_te)
            if DEBUG == "psum" and not dbg_done:
                dbg_done.append(1)
                p.wait("scalar", s_te, s_done)
                dbg_f32(pb[b0][:], 0, mb)
            p.wait("scalar", s_te, s_done)
            for ct in range(CT):
                xs = x_res[:, ts(ct, T)][:, ts(ch, TC)]
                sc(lambda e, xs=xs, ct=ct: e.activation(
                    out=xf16[:, ts(ct, TC)], in_=xs, func=AF.Square))
            sqcast_done = p.count(s_sc)
            p.wait("tensor", s_sc, sqcast_done)
            for ct in range(CT):
                te(lambda e, ct=ct: e.matmul(
                    out=pb[b1][:], lhsT=ones16[:], rhs=xf16[:, ts(ct, TC)],
                    start=(ct == 0), stop=(ct == CT - 1)), inc=(ct == CT - 1))
            q_done = p.count(s_te)
            used(("lnxf", id(xf16)), "tensor")
            p.wait("scalar", s_te, s_done)
            sc(lambda e: e.activation(out=mb, in_=pb[b0][:], func=AF.Copy,
                                      scale=1.0 / 1024))
            used(b0, "scalar")
            if DEBUG == "mb" and not dbg_done:
                dbg_done.append(1)
                dbg_f32(mb, 0, var)

            sc(lambda e: e.activation(out=t1, in_=mb, func=AF.Square))
            sq_done = p.count(s_sc)
            p.wait("vector", s_te, q_done)
            p.wait("vector", s_sc, sq_done)
            ve(lambda e: e.scalar_tensor_tensor(
                out=var, in0=pb[b1][:], scalar=1.0 / 1024, in1=t1,
                op0=OP.mult, op1=OP.subtract))
            used(b1, "vector")
            var_done = p.count(s_v)
            p.wait("scalar", s_v, var_done)
            sc(lambda e: e.activation(out=t1, in_=var, func=AF.Sqrt, bias=1e-5))
            sqrt_done = p.count(s_sc)
            p.wait("vector", s_sc, sqrt_done)
            ve(lambda e: e.reciprocal(out=rb, in_=t1))
            ve(lambda e: e.tensor_tensor(out=var, in0=mb, in1=rb, op=OP.mult))
            used(("lnrows", id(mb)), "vector")
            if DEBUG == "ln" and ch == 0 and not dbg_done:
                dbg_done.append(1)
                dv = p.count(s_v)
                p.wait("scalar", s_v, dv)
                dbg_f32(mb, 0, t1)
                dbg_f32(rb, 512, t1)
                dbg_f32(var, 1024, t1)

        dma("sync", ident[:], dr["ident16"])
        dma("sync", sv[:, 0:48], dr["smallv"])
        dma("sync", sd_[:], dr["smalld"])
        sm_w = p.count(s_w)
        ve(lambda e: e.memset(ones16[:], 1.0))
        ve(lambda e: e.memset(cpool[:, 0:1], 1e-5))
        ve(lambda e: e.memset(cpool[:, 1:2], 1.0))
        ve(lambda e: e.memset(stA[:], 0.0))
        ve(lambda e: e.memset(stB[:], 0.0))
        ve(lambda e: e.memset(stS[:], 0.0))
        ve(lambda e: e.memset(ctail[:], 0.0))
        ve(lambda e: e.memset(xsh0[:], 0.0))
        p.wait("scalar", s_w, sm_w)
        sc(lambda e: e.activation(out=cpool[:, 2:3], in_=cpool[:, 0:1],
                                  func=AF.Copy, accum_out=cpool[:, 3:4]))
        sc(lambda e: e.activation(out=ident32[:], in_=ident[:], func=AF.Copy))
        sc(lambda e: e.activation(out=lam8[:], in_=sv[:, 0:8], func=AF.Exp))
        sc(lambda e: e.activation(out=lam8[:], in_=lam8[:], func=AF.Exp, scale=-1.0))
        sc(lambda e: e.activation(out=etf8[:], in_=sv[:, 8:16], func=AF.Exp))
        sc(lambda e: e.activation(out=bias05[:], in_=sv[:, 16:24], func=AF.Copy,
                                  scale=0.5))
        sc(lambda e: e.activation(out=aneg[:], in_=sd_[:, 112:112 + DSL * NS],
                                  func=AF.Exp))
        d0 = p.count(s_sc)
        p.wait("vector", s_sc, d0)
        ve(lambda e: e.tensor_scalar(out=aneg[:], in0=aneg[:], scalar1=-1.0,
                                     scalar2=None, op0=OP.mult))
        for j, src0 in ((0, 24), (1, 32), (2, 40)):
            ve(lambda e, j=j, s0=src0: e.tensor_scalar(
                out=sv[:, 48 + 8 * j:56 + 8 * j], in0=sv[:, s0:s0 + 8],
                scalar1=-1.0, scalar2=1.0, op0=OP.mult, op1=OP.add))
        smalls_v = p.count(s_v)

        with ExitStack() as prep:
            stg = sb("stg", [128, 4096], F32, c=prep)
            stg16 = sb("stg16", [128, 2048], F16, c=prep)
            fr = sb("fr", [128, 8 * C], F16, c=prep)
            fi = sb("fi", [128, 8 * C], F16, c=prep)
            w16a = sb("w16a", [128, 8 * C], F16, c=prep)
            w16b = sb("w16b", [128, 8 * C], F16, c=prep)
            w16c = sb("w16c", [128, 8 * C], F16, c=prep)
            ptre = sb("ptre", [128, 8 * C], F16, c=prep)
            ptim = sb("ptim", [128, 8 * C], F16, c=prep)

            dma("sync", stg[0:8, 0:1024], dr["be"])
            dma("sync", stg[0:1, 1024:2048], dr["bgrow"])
            dma("sync", stg[:, 2048:2048 + DSL * 96], dr["wxT_t"])
            w0 = p.count(s_w)
            p.wait("scalar", s_w, w0)
            p.wait("scalar", s_v, smalls_v)
            sc(lambda e: e.activation(out=be16[0:8, :], in_=stg[0:8, 0:1024],
                                      func=AF.Copy))
            sc(lambda e: e.activation(out=bg16, in_=stg[0:1, 1024:2048],
                                      func=AF.Copy))
            bgc = p.count(s_sc)
            p.wait("sync", s_sc, bgc)
            dma("sync", dr["s_bg"], bg16)
            sc(lambda e: e.activation(out=wx16[:], in_=stg[:, 2048:2048 + DSL * 96],
                                      func=AF.Copy))
            used(("stg",), "scalar")
            acquire(("stg",), "sync")
            w0b = dma("sync", stg[0:RK, 0:DI], dr["wdtT_t"])
            p.wait("scalar", s_w, w0b)
            sc(lambda e: e.activation(
                out=wdt16[:], in_=stg[0:RK, 0:DI], func=AF.Copy))
            used(("stg",), "scalar")

            for ct in range(CT):
                dma("sync", fr[:, ts(ct, C)], dr["fr16"][ct])
                dma("sync", fi[:, ts(ct, C)], dr["fi16"][ct])
            fw = p.count(s_w)

            def cast_sq(dst16, src_t, scale=1.0, extra=None, extrascale=None):
                for g in range(2):
                    wdone = None
                    acquire(("stg",), "sync")
                    for j in range(4):
                        wdone = dma("sync", stg[:, ts(j, C)], src_t[g * 4 + j])
                    p.wait("scalar", s_w, wdone)
                    for j in range(4):
                        ctt = g * 4 + j
                        sc(lambda e, j=j, ctt=ctt: e.activation(
                            out=dst16[:, ts(ctt, C)], in_=stg[:, ts(j, C)],
                            func=AF.Copy, scale=scale))
                        if extra is not None:
                            sc(lambda e, j=j, ctt=ctt: e.activation(
                                out=extra[:, ts(ctt, C)], in_=stg[:, ts(j, C)],
                                func=AF.Copy, scale=extrascale))
                    used(("stg",), "scalar")

            cast_sq(w16a, dr["wfr"])
            cast_sq(w16b, dr["wfi"], extra=w16c, extrascale=-1.0)
            casts_sc = p.count(s_sc)
            p.wait("tensor", s_w, fw)
            p.wait("tensor", s_sc, casts_sc)

            def big_mm(dst16, lhs1, rhs1, lhs2, rhs2, scale=1.0):
                for och in range(8):
                    for half in range(2):
                        b = half
                        acquire(b, "tensor")
                        for k in range(8):
                            te(lambda e, o=och, h=half, k=k, b=b, l=lhs1, r=rhs1:
                               e.matmul(out=pb[b][:],
                                        lhsT=l[:, ts(k, C)][:, ts(o)],
                                        rhs=r[:, ts(k, C)][:, ts(h, 512)],
                                        start=(k == 0), stop=False), inc=False)
                        for k in range(8):
                            te(lambda e, o=och, h=half, k=k, b=b, l=lhs2, r=rhs2:
                               e.matmul(out=pb[b][:],
                                        lhsT=l[:, ts(k, C)][:, ts(o)],
                                        rhs=r[:, ts(k, C)][:, ts(h, 512)],
                                        start=False, stop=(k == 7)),
                               inc=(k == 7))
                        mmd = p.count(s_te)
                        p.wait("scalar", s_te, mmd)
                        sc(lambda e, o=och, h=half, b=b, d=dst16, sca=scale:
                           e.activation(out=d[:, ts(o, C)][:, ts(h, 512)],
                                        in_=pb[b][:], func=AF.Copy, scale=sca))
                        used(b, "scalar")

            big_mm(ptre, w16a, fr, w16c, fi)
            big_mm(ptim, w16a, fi, w16b, fr)
            m16 = w16c
            big_mm(m16, ptre, fr, ptim, fi, scale=0.5 / 1024)
            mdone = p.count(s_sc)
            p.wait("sync", s_sc, mdone)
            for ct in range(CT):
                dma("sync", dr["s_m"][ct], m16[:, ts(ct, C)])

            fold_plan = [("wkT_t", 0, 24, 0), ("wkT_t", 1, 48, 0),
                         ("wvT_t", 2, 32, 0), ("wvT_t", 3, 56, 0),
                         ("wrT_t", 4, 40, 0), ("wrT_t", 5, 64, 0)]
            for src, vi, col0, _ in fold_plan:
                for sl in range(8):
                    acquire(("stg",), "sync")
                    wdone = dma("sync", stg[:, 0:1024], dr[src][sl])
                    p.wait("scalar", s_w, wdone)
                    for ct in range(CT):
                        sc(lambda e, ct=ct, c0=col0: e.activation(
                            out=stg16[:, ts(ct)], in_=stg[:, ts(ct)],
                            func=AF.Copy, scale=sv[:, c0 + ct:c0 + ct + 1]))
                    used(("stg",), "scalar")
                    dn = p.count(s_sc)
                    p.wait("sync", s_sc, dn)
                    dma("sync", dr["s_kv"][vi][sl], stg16[:, 0:1024])
                    used(("stg16",), "sync")
                    acquire(("stg16",), "scalar")

            def plain_cast(src_t, dst_t, n_t, rows=128, cols=1024, scale=1.0):
                for s in range(n_t):
                    acquire(("stg",), "sync")
                    wdone = dma("sync", stg[0:rows, 0:cols], src_t[s])
                    p.wait("scalar", s_w, wdone)
                    sc(lambda e, r=rows, cl=cols, sca=scale: e.activation(
                        out=stg16[0:r, 0:cl], in_=stg[0:r, 0:cl], func=AF.Copy,
                        scale=sca))
                    used(("stg",), "scalar")
                    dn = p.count(s_sc)
                    p.wait("sync", s_sc, dn)
                    dma("sync", dst_t[s], stg16[0:rows, 0:cols])
                    used(("stg16",), "sync")
                    acquire(("stg16",), "scalar")

            plain_cast(dr["woT_t"], dr["s_wo"], 8, scale=0.5)
            plain_cast(dr["winT_t"], dr["s_win"], 32)
            plain_cast(dr["wg_t"], dr["s_wg"], 8)
            for ee in range(E):
                for ct in range(CT):
                    acquire(("stg",), "sync")
                    wdone = dma("sync", stg[:, 0:1024], dr["we_t"][ee][ct])
                    p.wait("scalar", s_w, wdone)
                    sc(lambda e: e.activation(out=stg16[:, 0:1024],
                                              in_=stg[:, 0:1024], func=AF.Copy))
                    used(("stg",), "scalar")
                    dn = p.count(s_sc)
                    p.wait("sync", s_sc, dn)
                    for fh in range(2):
                        dma("sync",
                            dr["s_we"][ee][fh][:, ct * 512:(ct + 1) * 512],
                            stg16[:, fh * 512:fh * 512 + 512])
                    used(("stg16",), "sync")
                    acquire(("stg16",), "scalar")
            for co in range(8):
                acquire(("stg",), "sync")
                wdone = dma("sync", stg[:, 0:2048], dr["woutT_t"][co])
                p.wait("scalar", s_w, wdone)
                sc(lambda e: e.activation(out=stg16[:, 0:2048],
                                          in_=stg[:, 0:2048], func=AF.Copy))
                used(("stg",), "scalar")
                dn = p.count(s_sc)
                p.wait("sync", s_sc, dn)
                dma("sync", dr["s_wout"][co], stg16[:, 0:2048])
                used(("stg16",), "sync")
                acquire(("stg16",), "scalar")
            barrier()

        x_res = sb("x_res", [128, CT * T])
        for ct in range(CT):
            dma("sync", x_res[:, ts(ct, T)], dr["xT"][ct])
        barrier()

        def ffn0_phase(tag):
            with ExitStack() as sco:
                m16 = sb(f"{tag}m", [128, 8 * C], F16, c=sco)
                xn16 = sb(f"{tag}xn", [128, CT * TC], F16, c=sco)
                xf16 = sb(f"{tag}xf", [128, CT * TC], F16, c=sco)
                mb = sb(f"{tag}mb", [128, TC], c=sco)[:]
                rb = sb(f"{tag}rb", [128, TC], c=sco)[:]
                t1 = sb(f"{tag}t1", [128, TC], c=sco)[:]
                var = sb(f"{tag}v", [128, TC], c=sco)[:]
                for ct in range(CT):
                    dma("sync", m16[:, ts(ct, C)], dr["s_m"][ct])
                mw = p.count(s_w)
                p.wait("tensor", s_w, mw)
                if DEBUG == "m16" and tag == "p1":
                    p.wait("scalar", s_w, mw)
                    for ct in range(CT):
                        dbg_f32(m16[:, ts(ct, C)], ct * 1024, mb)
                for ch in range(NCH):
                    ln_rows(ch, xf16, None, mb, rb, t1, var, 0, 1)
                    acquire(("xn", id(xn16)), "vector")
                    for ct in range(CT):
                        xs = x_res[:, ts(ct, T)][:, ts(ch, TC)]
                        ve(lambda e, xs=xs: e.tensor_tensor(
                            out=t1, in0=xs, in1=rb, op=OP.mult))
                        ve(lambda e, ct=ct: e.tensor_tensor(
                            out=xn16[:, ts(ct, TC)], in0=t1, in1=var,
                            op=OP.subtract))
                    app = p.count(s_v)
                    for co in range(CT):
                        b = 2 + (co % 2)
                        acquire(b, "tensor")
                        p.wait("tensor", s_v, app)
                        for k in range(CT):
                            te(lambda e, co=co, k=k, b=b: e.matmul(
                                out=pb[b][:], lhsT=m16[:, ts(k, C)][:, ts(co)],
                                rhs=xn16[:, ts(k, TC)],
                                start=(k == 0), stop=(k == CT - 1)),
                               inc=(k == CT - 1))
                        used(("xn", id(xn16)), "tensor")
                        mmd = p.count(s_te)
                        p.wait("vector", s_te, mmd)
                        xs = x_res[:, ts(co, T)][:, ts(ch, TC)]
                        ve(lambda e, b=b, xs=xs, co=co: e.scalar_tensor_tensor(
                            out=t1, in0=pb[b][:], scalar=bias05[:, co:co + 1],
                            in1=xs, op0=OP.add, op1=OP.add))
                        used(b, "vector")
                        ve(lambda e, xs=xs: e.tensor_copy(out=xs, in_=t1))
                    used(("lnrows", id(mb)), "vector")
                barrier()

        if PHASES > 1 and 1 not in SKIPS:
            ffn0_phase("p1")

        def rwkv_phase():
            with ExitStack() as sco:
                xn16 = sb("p2xn", [128, CT * (TC + 1)], F16, c=sco)
                xf16 = sb("p2xf", [128, CT * TC], F16, c=sco)
                mb = sb("p2mb", [128, TC], c=sco)[:]; rb = sb("p2rb", [128, TC], c=sco)[:]
                t1 = sb("p2t1", [128, TC], c=sco)[:]; var = sb("p2v", [128, TC], c=sco)[:]
                kv16 = [sb("p2kv0", [128, 6 * C], F16, c=sco)] * 2
                wo16 = [sb(f"p2wo{i}", [128, C], F16, c=sco) for i in range(2)]
                rk16 = sb("p2rk", [128, 8 * TC], F16, c=sco)
                ek = sb("p2ek", [128, TC], c=sco)[:]; uu = sb("p2u", [128, TC], c=sco)[:]
                As = sb("p2A", [128, TC + 1], c=sco)
                Bs = sb("p2B", [128, TC + 1], c=sco)
                num = sb("p2n", [128, TC], c=sco)[:]; den = sb("p2d", [128, TC], c=sco)[:]
                rec = sb("p2r", [128, TC], c=sco)[:]; th = sb("p2th", [128, TC], c=sco)[:]

                def xnsl(ct):
                    return xn16[:, ct * (TC + 1):(ct + 1) * (TC + 1)]

                for ch in range(NCH):
                    ln_rows(ch, xf16, None, mb, rb, t1, var, 0, 1)
                    acquire("xn2", "vector")
                    for ct in range(CT):
                        ve(lambda e, ct=ct: e.tensor_copy(
                            out=xnsl(ct)[:, 0:1], in_=xsh0[:, ct:ct + 1]))
                    for ct in range(CT):
                        xs = x_res[:, ts(ct, T)][:, ts(ch, TC)]
                        ve(lambda e, xs=xs: e.tensor_tensor(
                            out=t1, in0=xs, in1=rb, op=OP.mult))
                        ve(lambda e, ct=ct: e.tensor_tensor(
                            out=xnsl(ct)[:, 1:TC + 1], in0=t1, in1=var,
                            op=OP.subtract))
                    for ct in range(CT):
                        ve(lambda e, ct=ct: e.tensor_copy(
                            out=xsh0[:, ct:ct + 1], in_=xnsl(ct)[:, TC:TC + 1]))
                    xn_done = p.count(s_v)
                    used(("lnrows", id(mb)), "vector")
                    acquire("rk2", "vector")

                    for sl in range(8):
                        kvb = kv16[sl % 2]
                        acquire(("kv", id(kvb)), "sync")
                        wvd = None
                        for v6 in range(6):
                            wvd = dma("sync", kvb[:, ts(v6, C)],
                                      dr["s_kv"][v6][sl])
                        p.wait("tensor", s_w, wvd)
                        p.wait("tensor", s_v, xn_done)
                        for wi in range(3):
                            b = wi
                            acquire(b, "tensor")
                            for ct in range(CT):
                                te(lambda e, wi=wi, ct=ct, b=b, kvb=kvb: e.matmul(
                                    out=pb[b][:],
                                    lhsT=kvb[:, ts(2 * wi, C)][:, ts(ct)],
                                    rhs=xnsl(ct)[:, 1:TC + 1],
                                    start=(ct == 0), stop=False), inc=False)
                            for ct in range(CT):
                                te(lambda e, wi=wi, ct=ct, b=b, kvb=kvb: e.matmul(
                                    out=pb[b][:],
                                    lhsT=kvb[:, ts(2 * wi + 1, C)][:, ts(ct)],
                                    rhs=xnsl(ct)[:, 0:TC],
                                    start=False, stop=(ct == CT - 1)),
                                   inc=(ct == CT - 1))
                        used("xn2", "tensor")
                        used(("kv", id(kvb)), "tensor")
                        mmd = p.count(s_te)
                        p.wait("scalar", s_te, mmd - 2)
                        sc(lambda e: e.activation(out=ek, in_=pb[0][:], func=AF.Exp))
                        used(0, "scalar")
                        ek_done = p.count(s_sc)
                        p.wait("scalar", s_te, mmd)
                        sc(lambda e: e.activation(out=th, in_=pb[2][:],
                                                  func=AF.Tanh, scale=0.5))
                        used(2, "scalar")
                        th_done = p.count(s_sc)
                        p.wait("vector", s_sc, ek_done)
                        p.wait("vector", s_te, mmd - 1)
                        ve(lambda e: e.tensor_tensor(out=uu, in0=ek, in1=pb[1][:],
                                                     op=OP.mult))
                        used(1, "vector")
                        ve(lambda e, sl=sl: e.tensor_copy(
                            out=As[:, 0:1], in_=stA[:, sl:sl + 1]))
                        ve(lambda e, sl=sl: e.tensor_copy(
                            out=Bs[:, 0:1], in_=stB[:, sl:sl + 1]))
                        ve(lambda e, sl=sl: e.tensor_tensor_scan(
                            out=As[:, 1:TC + 1],
                            data0=lam8[:, sl:sl + 1].broadcast_to((128, TC)),
                            data1=uu, initial=As[:, 0:1],
                            op0=OP.mult, op1=OP.add))
                        ve(lambda e, sl=sl: e.tensor_tensor_scan(
                            out=Bs[:, 1:TC + 1],
                            data0=lam8[:, sl:sl + 1].broadcast_to((128, TC)),
                            data1=ek, initial=Bs[:, 0:1],
                            op0=OP.mult, op1=OP.add))
                        ve(lambda e, sl=sl: e.tensor_copy(
                            out=stA[:, sl:sl + 1], in_=As[:, TC:TC + 1]))
                        ve(lambda e, sl=sl: e.tensor_copy(
                            out=stB[:, sl:sl + 1], in_=Bs[:, TC:TC + 1]))
                        ve(lambda e, sl=sl: e.scalar_tensor_tensor(
                            out=num, in0=uu, scalar=etf8[:, sl:sl + 1],
                            in1=As[:, 0:TC], op0=OP.mult, op1=OP.add))
                        ve(lambda e, sl=sl: e.scalar_tensor_tensor(
                            out=den, in0=ek, scalar=etf8[:, sl:sl + 1],
                            in1=Bs[:, 0:TC], op0=OP.mult, op1=OP.add))
                        ve(lambda e: e.reciprocal(out=rec, in_=den))
                        ve(lambda e: e.tensor_tensor(out=num, in0=num, in1=rec,
                                                     op=OP.mult))
                        p.wait("vector", s_sc, th_done)
                        ve(lambda e, sl=sl: e.scalar_tensor_tensor(
                            out=rk16[:, ts(sl, TC)], in0=th, scalar=1.0,
                            in1=num, op0=OP.add, op1=OP.mult))
                        if DEBUG == "rwkv" and ch == 0 and sl == 0 and not dbg_done:
                            dbg_done.append(1)
                            dv = p.count(s_v)
                            p.wait("scalar", s_v, dv)
                            dbg_f32(ek, 0, dbgst[:])
                            dbg_f32(uu, 512, dbgst[:])
                            dbg_f32(As[:, 1:TC + 1], 1024, dbgst[:])
                            dbg_f32(Bs[:, 1:TC + 1], 1536, dbgst[:])
                            dbg_f32(num, 2048, dbgst[:])
                            dbg_f32(th, 2560, dbgst[:])
                            dbg_f32(rk16[:, 0:TC], 3072, dbgst[:])
                            dbg_f32(xn16[:, 0:TC + 1], 3584, dbgst[:])
                    rk_done = p.count(s_v)
                    p.wait("tensor", s_v, rk_done)
                    for co in range(CT):
                        wob = wo16[co % 2]
                        acquire(("p2wo", id(wob)), "sync")
                        wod = dma("sync", wob[:], dr["s_wo"][co])
                        b = 3 + (co % 2)
                        acquire(b, "tensor")
                        p.wait("tensor", s_w, wod)
                        for ds in range(8):
                            te(lambda e, co=co, ds=ds, b=b, wob=wob: e.matmul(
                                out=pb[b][:], lhsT=wob[:, ts(ds)],
                                rhs=rk16[:, ts(ds, TC)],
                                start=(ds == 0), stop=(ds == 7)), inc=(ds == 7))
                        used(("p2wo", id(wob)), "tensor")
                        mmd = p.count(s_te)
                        p.wait("vector", s_te, mmd)
                        xs = x_res[:, ts(co, T)][:, ts(ch, TC)]
                        ve(lambda e, b=b, xs=xs: e.tensor_tensor(
                            out=xs, in0=pb[b][:], in1=xs, op=OP.add))
                        used(b, "vector")
                    used("rk2", "tensor")
                    acquire("xn2", "vector")
                barrier()

        if PHASES > 2 and 2 not in SKIPS:
            rwkv_phase()
        if PHASES > 3 and 3 not in SKIPS:
            ffn0_phase("p3")

        def moe_phase(tag):
            p.serial = bool(MOESER) or bool(SERALL)
            with ExitStack() as sco:
                xn16 = sb(f"{tag}xn", [128, CT * TC], F16, c=sco)
                xf16 = sb(f"{tag}xf", [128, CT * TC], F16, c=sco)
                mb = sb(f"{tag}mb", [128, TC], c=sco)[:]
                rb = sb(f"{tag}rb", [128, TC], c=sco)[:]
                t1 = sb(f"{tag}t1", [128, TC], c=sco)[:]
                var = sb(f"{tag}v", [128, TC], c=sco)[:]
                wu = sb(f"{tag}wu", [128, 8 * C], F16, c=sco)
                wg16 = wu
                we16 = [wu[:, 0:8 * 512], wu[:, 8 * 512:8 * C]]
                mx16 = [sb(f"{tag}mx0", [128, CT * TC], F16, c=sco)] * 2
                q32 = sb(f"{tag}q", [128, 1024], c=sco)
                mcol = sb(f"{tag}mc", [128, 4 * 8], c=sco)
                mrow = sb(f"{tag}mr", [8, 512], F16, c=sco)
                mb16 = sb(f"{tag}mb16", [128, 2 * TC], F16, c=sco)
                outT = sb(f"{tag}oT", [128, 4 * 512], c=sco)
                p05 = sb(f"{tag}p05", [128, 4], c=sco)
                mxc = sb(f"{tag}mxc", [128, 8], c=sco)
                eidxu = sb(f"{tag}eiu", [128, 1], U32, c=sco)
                eidxf = sb(f"{tag}eif", [128, 1], c=sco)
                sumc = sb(f"{tag}sum", [128, 2], c=sco)
                iota16 = sb(f"{tag}io", [128, 1024], F16, c=sco)
                p.emit("gpsimd", lambda e: e.iota(
                    out=iota16[:], pattern=[[1, 1024]], base=-2048,
                    channel_multiplier=0,
                    allow_small_or_imprecise_dtypes=True), inc=s_g, n=1)
                p.wait("vector", s_g, p.count(s_g))

                for ch in range(NCH):
                    acquire(("wu", id(we16[0])), "sync")
                    acquire(("wu", id(we16[1])), "sync")
                    for ct in range(CT):
                        dma("sync", wg16[:, ts(ct, C)], dr["s_wg"][ct])
                    wg_w = p.count(s_w)
                    p.wait("tensor", s_w, wg_w)
                    ln_rows(ch, xf16, None, mb, rb, t1, var, 0, 1)
                    acquire(("xn", id(xn16)), "vector")
                    for ct in range(CT):
                        xs = x_res[:, ts(ct, T)][:, ts(ch, TC)]
                        ve(lambda e, xs=xs: e.tensor_tensor(
                            out=t1, in0=xs, in1=rb, op=OP.mult))
                        ve(lambda e, ct=ct: e.tensor_tensor(
                            out=xn16[:, ts(ct, TC)], in0=t1, in1=var,
                            op=OP.subtract))
                    app = p.count(s_v)
                    used(("lnrows", id(mb)), "vector")
                    acquire(("lnxf", id(xf16)), "gpsimd")
                    bgw = dma("gpsimd", xf16[0:1, 0:1024], dr["s_bg"])
                    p.wait("tensor", s_g, bgw)
                    acquire(("mcol",), "vector")
                    acquire(("p05",), "vector")
                    for su in range(4):
                        acquire("pbig", "tensor")
                        p.wait("tensor", s_v, app)
                        for half in range(2):
                            for k in range(CT):
                                te(lambda e, su=su, h=half, k=k: e.matmul(
                                    out=pbig[:, ts(h, 512)],
                                    lhsT=xn16[:, ts(k, TC)][:, ts(su)],
                                    rhs=wg16[:, ts(k, C)][:, ts(h, 512)],
                                    start=(k == 0), stop=False), inc=False)
                            te(lambda e, h=half: e.matmul(
                                out=pbig[:, ts(h, 512)],
                                lhsT=ones16[0:1, :],
                                rhs=xf16[0:1, ts(h, 512)],
                                start=False, stop=True), inc=True)
                        used(("wu", id(we16[0])), "tensor")
                        used(("wu", id(we16[1])), "tensor")
                        used(("xn", id(xn16)), "tensor")
                        used(("lnxf", id(xf16)), "tensor")
                        mmd = p.count(s_te)
                        if DEBUG == "moe2" and su == 0 and not dbg_done:
                            dbg_done.append(1)
                            p.wait("scalar", s_te, mmd)
                            p.wait("scalar", s_v, app)
                            dbg_f32(pbig[:], 0, dbgst[:])
                            dbg_f32(xn16[:, 0:1024], 1024, dbgst[:])
                            dbg_f32(wg16[:, 0:1024], 2048, dbgst[:])
                            p.wait("vector", s_sc, p.count(s_sc))
                        p.wait("vector", s_te, mmd)
                        for h in range(2):
                            ve(lambda e, h=h: e.reduce_max(
                                out=mxc[:, 2 + h:3 + h], in_=pbig[:, ts(h, 512)],
                                axis=mybir.AxisListType.X))
                        vfence()
                        ve(lambda e: e.tensor_tensor(
                            out=mxc[:, 0:1], in0=mxc[:, 2:3], in1=mxc[:, 3:4],
                            op=OP.max))
                        vfence()
                        ve(lambda e: e.tensor_scalar(
                            out=mxc[:, 1:2], in0=mxc[:, 0:1], scalar1=-1.0,
                            scalar2=None, op0=OP.mult))
                        neg_done = p.count(s_v)
                        acquire(("q32",), "scalar")
                        p.wait("scalar", s_v, neg_done)
                        for h in range(2):
                            sc(lambda e, h=h: e.activation(
                                out=q32[:, ts(h, 512)], in_=pbig[:, ts(h, 512)],
                                func=AF.Exp, bias=mxc[:, 1:2], scale=1.0))
                        q_done = p.count(s_sc)
                        p.wait("vector", s_sc, q_done)
                        for h in range(2):
                            ve(lambda e, h=h: e.reduce_sum(
                                out=sumc[:, h:h + 1], in_=q32[:, ts(h, 512)],
                                axis=mybir.AxisListType.X))
                        vfence()
                        ve(lambda e: e.tensor_tensor(
                            out=sumc[:, 0:1], in0=sumc[:, 0:1],
                            in1=sumc[:, 1:2], op=OP.add))
                        vfence()
                        ve(lambda e: e.tensor_scalar(
                            out=sumc[:, 0:1], in0=sumc[:, 0:1], scalar1=2.0,
                            scalar2=None, op0=OP.mult))
                        vfence()
                        ve(lambda e, su=su: e.reciprocal(out=p05[:, su:su + 1],
                                                         in_=sumc[:, 0:1]))
                        for h in range(2):
                            ve(lambda e, h=h: e.tensor_scalar(
                                out=q32[:, ts(h, 512)], in0=pbig[:, ts(h, 512)],
                                scalar1=mxc[:, 0:1], scalar2=None,
                                op0=OP.is_equal))
                        used("pbig", "vector")
                        ve(lambda e: e.tensor_tensor(
                            out=q32[:], in0=q32[:], in1=iota16[:], op=OP.mult))
                        ve(lambda e: e.tensor_reduce(
                            out=eidxf[:], in_=q32[:],
                            axis=mybir.AxisListType.X, op=OP.min))
                        used(("q32",), "vector")
                        vfence()
                        ve(lambda e: e.tensor_scalar(
                            out=eidxf[:], in0=eidxf[:], scalar1=2048.0,
                            scalar2=None, op0=OP.add))
                        vfence()
                        ve(lambda e: e.tensor_copy(out=eidxu[:], in_=eidxf[:]))
                        vfence()
                        ve(lambda e: e.tensor_scalar(
                            out=eidxu[:], in0=eidxu[:], scalar1=7,
                            scalar2=None, op0=OP.bitwise_and))
                        vfence()
                        ve(lambda e: e.tensor_copy(out=eidxf[:], in_=eidxu[:]))
                        vfence()
                        for ee in range(E):
                            ve(lambda e, su=su, ee=ee: e.tensor_scalar(
                                out=mcol[:, su * 8 + ee:su * 8 + ee + 1],
                                in0=eidxf[:], scalar1=float(ee), scalar2=None,
                                op0=OP.is_equal))
                        if DEBUG == "moe" and su == 0 and not dbg_done:
                            dbg_done.append(1)
                            dv = p.count(s_v)
                            p.wait("scalar", s_v, dv)
                            dbg_f32(q32[:], 0, dbgst[:])
                            dbg_f32(mxc[:, 0:8], 1024, dbgst[:])
                            dbg_f32(sumc[:, 0:1], 1032, dbgst[:])
                            dbg_f32(p05[:, 0:1], 1033, dbgst[:])
                            dbg_f32(eidxf[:], 1034, dbgst[:])
                            dbg_f32(xn16[:, 0:512], 2048, dbgst[:])
                    mask_done = p.count(s_v)
                    p.wait("tensor", s_v, mask_done)
                    acquire(("mrow",), "scalar")
                    for su in range(4):
                        b = su % 2
                        acquire(b, "tensor")
                        te(lambda e, su=su, b=b: e.transpose(
                            out=pb[b][0:8, 0:128], in_=mcol[:, su * 8:su * 8 + 8],
                            identity=ident32[:]))
                        trd = p.count(s_te)
                        p.wait("scalar", s_te, trd)
                        sc(lambda e, su=su, b=b: e.activation(
                            out=mrow[:, ts(su)], in_=pb[b][0:8, 0:128],
                            func=AF.Copy))
                        used(b, "scalar")
                    used(("mcol",), "tensor")
                    mrow_done = p.count(s_sc)
                    p.wait("tensor", s_sc, mrow_done)
                    p.wait("gpsimd", s_sc, mrow_done)
                    fm_done = p.count(s_sc)
                    p.wait("tensor", s_sc, mrow_done)
                    if MOEBAR & 1:
                        barrier()
                    for fh in range(2 if MOESUB >= 2 else 0):
                        for ee in range(E):
                            web = we16[ee % 2]
                            acquire(("wu", id(we16[ee % 2])), "sync")
                            wed = dma("sync", web, dr["s_we"][ee][fh])
                            half = ee % 2
                            acquire(("mslot",), "gpsimd")
                            msw = dma("gpsimd", mb16[0:1, 512:1024],
                                      mrow[ee:ee + 1, :])
                            acquire("pbig", "tensor")
                            p.wait("tensor", s_g, msw)
                            te(lambda e, ee=ee, h=half: e.matmul(
                                out=pbig[:, ts(h, 512)],
                                lhsT=ones16[0:1, :],
                                rhs=mb16[0:1, 512:1024],
                                start=True, stop=True))
                            used(("mslot",), "tensor")
                            bd = p.count(s_te)
                            acquire(("mb16", 0), "scalar")
                            p.wait("scalar", s_te, bd)
                            sc(lambda e, ee=ee, h=half: e.activation(
                                out=mb16[:, 0:TC],
                                in_=pbig[:, ts(h, 512)], func=AF.Copy))
                            used("pbig", "scalar")
                            mbd = p.count(s_sc)
                            mx = mx16[ee % 2]
                            acquire(("mx", id(mx)), "vector")
                            p.wait("vector", s_sc, mbd)
                            for ct in range(CT):
                                ve(lambda e, ct=ct, ee=ee, mx=mx: e.tensor_tensor(
                                    out=mx[:, ts(ct, TC)],
                                    in0=xn16[:, ts(ct, TC)],
                                    in1=mb16[:, 0:TC], op=OP.mult))
                            used(("mb16", 0), "vector")
                            mxd = p.count(s_v)
                            if ee == 0:
                                for su in range(4):
                                    acquire(su, "tensor")
                            p.wait("tensor", s_v, mxd)
                            p.wait("tensor", s_w, wed)
                            for su in range(4):
                                for k in range(CT):
                                    te(lambda e, fh=fh, su=su, k=k, ee=ee,
                                       mx=mx, web=web: e.matmul(
                                        out=pb[su][:],
                                        lhsT=mx[:, ts(k, TC)][:, ts(su)],
                                        rhs=web[:, ts(k, 512)],
                                        start=(ee == 0 and k == 0), stop=False),
                                       inc=False)
                            p.emit("tensor", lambda e: e.nop(), inc=s_te)
                            used(("wu", id(we16[ee % 2])), "tensor")
                            used(("mx", id(mx)), "tensor")
                        for su in range(4):
                            te(lambda e, fh=fh, su=su: e.matmul(
                                out=pb[su][:], lhsT=mrow[:, ts(su)],
                                rhs=be16[0:8, ts(fh, 512)],
                                start=False, stop=True), inc=True)
                        used(("mrow",), "tensor")
                        emm = p.count(s_te)
                        acquire(("outT",), "scalar")
                        p.wait("scalar", s_te, emm)
                        for su in range(4):
                            sc(lambda e, su=su: e.activation(
                                out=outT[:, ts(su, 512)],
                                in_=pb[su][:], func=AF.Relu,
                                scale=p05[:, su:su + 1]))
                            used(su, "scalar")
                        used(("p05",), "scalar")
                        ev_done = p.count(s_sc)
                        if (DEBUG == "p4x" and ch == 0 and fh == 0
                                and not dbg_done):
                            dbg_done.append(1)
                            dbg_f32(outT[:, 0:2048], 0, dbgst[:])
                            dbg_f32(mrow[0:8, 0:512], 2048, dbgst[:])
                            dbg_f32(mb16[:, 0:1024], 2560, dbgst[:])
                            dbg_f32(p05[:, 0:4], 3584, dbgst[:])
                            dbg_f32(mcol[:, 0:32], 3588, dbgst[:])
                        p.wait("tensor", s_sc, ev_done)
                        for ft in range(4):
                            b = 4 + (ft % 2)
                            acquire(b, "tensor")
                            for su in range(4):
                                te(lambda e, ft=ft, su=su, b=b: e.transpose(
                                    out=pb[b][:, ts(su)],
                                    in_=outT[:, su * 512 + ft * 128:
                                             su * 512 + (ft + 1) * 128],
                                    identity=ident32[:]), inc=(su == 3))
                            trd = p.count(s_te)
                            if (DEBUG == "p4add" and ch == 0 and fh == 0
                                    and ft == 0 and not dbg_done):
                                dbg_done.append(1)
                                p.wait("scalar", s_te, trd)
                                dbg_f32(pb[b][:], 0, dbgst[:])
                                dbg_f32(outT[:, 0:512], 512, dbgst[:])
                                dbg_f32(x_res[:, 0:512], 1024, dbgst[:])
                                p.wait("vector", s_sc, p.count(s_sc))
                            p.wait("vector", s_te, trd)
                            co = fh * 4 + ft
                            xs = x_res[:, ts(co, T)][:, ts(ch, TC)]
                            if MOESUB >= 3:
                                ve(lambda e, b=b, xs=xs: e.tensor_tensor(
                                    out=xs, in0=pb[b][:], in1=xs, op=OP.add))
                            else:
                                ve(lambda e, b=b: e.tensor_copy(
                                    out=var, in_=pb[b][:]))
                            used(b, "vector")
                        used(("outT",), "tensor")
                        if MOEBAR & 2:
                            barrier()
                    if MOEBAR & 4:
                        barrier()
                    if DEBUG == "p4end" and ch == NCH - 1:
                        dv = p.count(s_v)
                        p.wait("scalar", s_v, dv)
                        dbg_f32(x_res[:, 0:512], 0, dbgst[:])
                        dbg_f32(x_res[:, 7 * T + 3584:7 * T + 4096], 512, dbgst[:])
                barrier()

        if PHASES > 4 and 4 not in SKIPS:
            moe_phase("p4")
            p.serial = bool(SERALL)

        def mamba_phase(sub=99):
            if sub < 1:
                return
            p.serial = bool(P5SER & 1) or bool(SERALL)
            with ExitStack() as sco:
                xn16 = sb("p5xn", [128, CT * TC], F16, c=sco)
                xf16 = sb("p5xf", [128, CT * TC], F16, c=sco)
                mb = sb("p5mb", [128, TC], c=sco)[:]; rb = sb("p5rb", [128, TC], c=sco)[:]
                t1 = sb("p5t1", [128, TC], c=sco)[:]; var = sb("p5v", [128, TC], c=sco)[:]
                wi16 = [sb(f"p5wi{i}", [128, 8 * 128], F16, c=sco) for i in range(2)]
                xin = sb("p5xi", [128, TC + 4], c=sco)
                cv0 = rb
                cv1 = t1
                xcv = sb("p5xcv", [128, DSL * TC], F16, c=sco)
                szb = sb("p5sz", [128, DSL * TC], F16, c=sco)
                xdst = sb("p5xd", [96, TC], F16, c=sco)
                if DEBUG == "p5a":
                    dbg_f32(x_res[:, 0:512], 0, dbgst[:])
                for ch in range(NCH):
                    if DEBUG == "p5a" and ch == 1:
                        dv = p.count(s_v)
                        p.wait("scalar", s_v, dv)
                        dbg_f32(x_res[:, 0:512], 512, dbgst[:])
                        dbg_f32(xn16[:, 0:512], 1024, dbgst[:])
                        dbg_f32(xcv[:, 0:512], 1536, dbgst[:])
                    ln_rows(ch, xf16, None, mb, rb, t1, var, 3, 4)
                    acquire("xn5", "vector")
                    for ct in range(CT):
                        xs = x_res[:, ts(ct, T)][:, ts(ch, TC)]
                        ve(lambda e, xs=xs: e.tensor_tensor(
                            out=t1, in0=xs, in1=rb, op=OP.mult))
                        ve(lambda e, ct=ct: e.tensor_tensor(
                            out=xn16[:, ts(ct, TC)], in0=t1, in1=var,
                            op=OP.subtract))
                    app = p.count(s_v)
                    p.wait("tensor", s_v, app)
                    acquire(2, "tensor")
                    for half, nsl in ((0, DSL), (1, DSL)):
                        for sl in range(nsl):
                            wib = wi16[sl % 2]
                            acquire(("wi", id(wib)), "sync")
                            wid = dma("sync", wib[:], dr["s_win"][half * DSL + sl])
                            b = sl % 2
                            acquire(b, "tensor")
                            p.wait("tensor", s_w, wid)
                            for k in range(CT):
                                te(lambda e, k=k, b=b, wib=wib: e.matmul(
                                    out=pb[b][:], lhsT=wib[:, ts(k)],
                                    rhs=xn16[:, ts(k, TC)],
                                    start=(k == 0), stop=(k == CT - 1)),
                                   inc=(k == CT - 1))
                            used(("wi", id(wib)), "tensor")
                            used("xn5", "tensor")
                            mmd = p.count(s_te)
                            if half == 0:
                                acquire(("xin",), "scalar")
                                sc(lambda e, sl=sl: e.activation(
                                    out=xin[:, 0:3],
                                    in_=ctail[:, sl * 4:sl * 4 + 3],
                                    func=AF.Copy))
                                p.wait("scalar", s_te, mmd)
                                sc(lambda e, b=b: e.activation(
                                    out=xin[:, 3:TC + 3], in_=pb[b][:],
                                    func=AF.Copy))
                                used(b, "scalar")
                                sc(lambda e, sl=sl: e.activation(
                                    out=ctail[:, sl * 4:sl * 4 + 3],
                                    in_=xin[:, TC:TC + 3], func=AF.Copy))
                                xin_done = p.count(s_sc)
                                p.wait("vector", s_sc, xin_done)
                                ve(lambda e, sl=sl: e.tensor_scalar(
                                    out=cv0[:], in0=xin[:, 0:TC],
                                    scalar1=sd_[:, sl * 4:sl * 4 + 1],
                                    scalar2=None, op0=OP.mult))
                                ve(lambda e, sl=sl: e.scalar_tensor_tensor(
                                    out=cv1[:], in0=xin[:, 1:TC + 1],
                                    scalar=sd_[:, sl * 4 + 1:sl * 4 + 2],
                                    in1=cv0[:], op0=OP.mult, op1=OP.add))
                                ve(lambda e, sl=sl: e.scalar_tensor_tensor(
                                    out=cv0[:], in0=xin[:, 2:TC + 2],
                                    scalar=sd_[:, sl * 4 + 2:sl * 4 + 3],
                                    in1=cv1[:], op0=OP.mult, op1=OP.add))
                                ve(lambda e, sl=sl: e.scalar_tensor_tensor(
                                    out=cv1[:], in0=xin[:, 3:TC + 3],
                                    scalar=sd_[:, sl * 4 + 3:sl * 4 + 4],
                                    in1=cv0[:], op0=OP.mult, op1=OP.add))
                                used(("xin",), "vector")
                                cvd = p.count(s_v)
                                p.wait("scalar", s_v, cvd)
                                sc(lambda e, sl=sl: e.activation(
                                    out=xcv[:, ts(sl, TC)], in_=cv1[:],
                                    func=AF.Silu,
                                    bias=sd_[:, 64 + sl:64 + sl + 1]))
                                xcvd = p.count(s_sc)
                                p.wait("tensor", s_sc, xcvd)
                                te(lambda e, sl=sl: e.matmul(
                                    out=pb[2][0:96, :],
                                    lhsT=wx16[:, sl * 96:(sl + 1) * 96],
                                    rhs=xcv[:, ts(sl, TC)],
                                    start=(sl == 0), stop=(sl == DSL - 1)),
                                   inc=(sl == DSL - 1))
                            else:
                                p.wait("scalar", s_te, mmd)
                                sc(lambda e, sl=sl, b=b: e.activation(
                                    out=szb[:, ts(sl, TC)], in_=pb[b][:],
                                    func=AF.Silu))
                                used(b, "scalar")
                        if half == 0:
                            xd_mm = p.count(s_te)
                            p.wait("scalar", s_te, xd_mm)
                            sc(lambda e: e.activation(out=xdst[:], in_=pb[2][0:96, :],
                                                      func=AF.Copy))
                            used(2, "scalar")
                            xd_done = p.count(s_sc)
                            p.wait("gpsimd", s_sc, xd_done)
                            dma("gpsimd", dr["s_xdbl"][ch], xdst[:])
                            dma("gpsimd", dr["s_xcv"][ch], xcv[:])
                            used(("xcvbuf",), "gpsimd")
                    used(("lnrows", id(mb)), "vector")
                    szdone = p.count(s_sc)
                    p.wait("gpsimd", s_sc, szdone)
                    dma("gpsimd", dr["s_sz"][ch], szb[:])
                    gcount = p.count(s_g)
                    p.wait("scalar", s_g, gcount)
                barrier()
            if sub < 2:
                return
            for ct in range(CT):
                dma("sync", dr["s_xsp"][ct], x_res[:, ts(ct, T)])
            barrier()
            if sub < 3:
                return
            p.serial = bool(P5SER & 2) or bool(SERALL)
            with ExitStack() as sco:
                yf16 = sb("p5yf", [128, DSL * TC], F16, c=sco)
                bbcb = sb("p5bb", [128, 2 * NS * TC], F16, c=sco)
                xdbl = [sb("p5cx0", [96, TC], F16, c=sco)] * 2
                dt32 = [sb("p5dt0", [128, TC], c=sco)] * 2
                dtx = sb("p5dtx", [128, TC], F16, c=sco)
                da32 = [sb(f"p5da{i}", [128, TC], c=sco) for i in range(2)]
                u16 = sb("p5u", [128, TC], F16, c=sco)
                s16 = [sb(f"p5s{i}", [128, TC], F16, c=sco) for i in range(2)]
                tmp16 = u16
                yy = sb("p5y0", [128, TC], c=sco)[:]
                xcvs = [sb("p5xv0", [128, TC], F16, c=sco)] * 2
                szs = [sb("p5zs0", [128, TC], F16, c=sco)] * 2
                woc = [sb(f"p5wo{i}", [128, DI // 4], F16, c=sco) for i in range(2)]
                for ch in range(NCH):
                    xdb = xdbl[ch % 2]
                    acquire(("xdbl", id(xdb)), "gpsimd")
                    xdd = dma("gpsimd", xdb[:], dr["s_xdbl"][ch])
                    p.wait("gpsimd", s_g, xdd)
                    acquire(("bbcb",), "scalar")
                    for kind in range(2):
                        acquire(("yf",), "gpsimd")
                        rw = p.emit("gpsimd", lambda e, k=kind, xdb=xdb:
                                    e.dma_start(
                                        out=yf16[0:1, 0:NS * TC],
                                        in_=xdb[64 + k * NS:64 + (k + 1) * NS, :],
                                        allow_slow_non_contiguous=True),
                                    inc=s_g, n=16)
                        p.wait("tensor", s_g, rw)
                        for n in range(NS):
                            b = 4 + (n % 2)
                            slot = n * 2 + kind
                            acquire(b, "tensor")
                            te(lambda e, n=n, b=b: e.matmul(
                                out=pb[b][:], lhsT=ones16[0:1, :],
                                rhs=yf16[0:1, n * TC:(n + 1) * TC],
                                start=True, stop=True))
                            bd = p.count(s_te)
                            p.wait("scalar", s_te, bd)
                            sc(lambda e, slot=slot, b=b: e.activation(
                                out=bbcb[:, slot * TC:(slot + 1) * TC],
                                in_=pb[b][:], func=AF.Copy))
                            used(b, "scalar")
                        used(("yf",), "tensor")
                    used(("xdbl", id(xdb)), "gpsimd")
                    bb_done = p.count(s_sc)
                    p.wait("vector", s_sc, bb_done)
                    acquire(("yf",), "vector")
                    for sl in range(DSL):
                        dtb = dt32[sl % 2]
                        acquire(0, "tensor")
                        te(lambda e, sl=sl, xdb=xdb: e.matmul(
                            out=pb[0][:], lhsT=wdt16[:, ts(sl)],
                            rhs=xdb[0:RK, :], start=True, stop=True))
                        dtd = p.count(s_te)
                        acquire(("dt", id(dtb)), "scalar")
                        p.wait("scalar", s_te, dtd)
                        sc(lambda e, sl=sl, dtb=dtb: e.activation(
                            out=dtb[:], in_=pb[0][:], func=AF.Exp,
                            bias=sd_[:, 96 + sl:96 + sl + 1]))
                        used(0, "scalar")
                        sc(lambda e, dtb=dtb: e.activation(
                            out=dtb[:], in_=dtb[:], func=AF.Ln, bias=1.0))
                        dt_done = p.count(s_sc)
                        xcb = xcvs[sl % 2]; szc = szs[sl % 2]
                        acquire(("xcv", id(xcb)), "gpsimd")
                        xcd = dma("gpsimd", xcb[:],
                                  dr["s_xcv"][ch][:, ts(sl, TC)])
                        acquire(("szs", id(szc)), "gpsimd")
                        szd = dma("gpsimd", szc[:],
                                  dr["s_sz"][ch][:, ts(sl, TC)])
                        p.wait("vector", s_sc, dt_done)
                        p.wait("vector", s_g, xcd)
                        acquire(("dtx",), "vector")
                        ve(lambda e, dtb=dtb, xcb=xcb: e.tensor_tensor(
                            out=dtx[:], in0=dtb[:], in1=xcb[:], op=OP.mult))
                        for n in range(NS):
                            dab = da32[n % 2]
                            acquire(("da", id(dab)), "scalar")
                            sc(lambda e, sl=sl, n=n, dab=dab, dtb=dtb:
                               e.activation(out=dab[:], in_=dtb[:], func=AF.Exp,
                                            scale=aneg[:, sl * NS + n:
                                                       sl * NS + n + 1]))
                            dad = p.count(s_sc)
                            ve(lambda e, n=n: e.tensor_tensor(
                                out=u16[:], in0=dtx[:],
                                in1=bbcb[:, (n * 2) * TC:(n * 2 + 1) * TC],
                                op=OP.mult))
                            p.wait("vector", s_sc, dad)
                            sb_ = s16[n % 2]
                            acquire(("s16", id(sb_)), "vector")
                            ve(lambda e, sl=sl, n=n, sb_=sb_, dab=dab:
                               e.tensor_tensor_scan(
                                   out=sb_[:], data0=dab[:], data1=u16[:],
                                   initial=stS[:, sl * NS + n:sl * NS + n + 1],
                                   op0=OP.mult, op1=OP.add))
                            used(("da", id(dab)), "vector")
                            scn = p.count(s_v)
                            p.wait("scalar", s_v, scn)
                            sc(lambda e, sl=sl, n=n, sb_=sb_: e.activation(
                                out=stS[:, sl * NS + n:sl * NS + n + 1],
                                in_=sb_[:, TC - 1:TC], func=AF.Copy))
                            used(("s16", id(sb_)), "scalar")
                            ve(lambda e, n=n, sb_=sb_: e.tensor_tensor(
                                out=tmp16[:], in0=sb_[:],
                                in1=bbcb[:, (n * 2 + 1) * TC:(n * 2 + 2) * TC],
                                op=OP.mult))
                            used(("s16", id(sb_)), "vector")
                            if n == 0:
                                ve(lambda e: e.tensor_copy(out=yy, in_=tmp16[:]))
                            else:
                                ve(lambda e: e.tensor_tensor(
                                    out=yy, in0=tmp16[:], in1=yy, op=OP.add))
                        used(("dt", id(dtb)), "vector")
                        used(("dtx",), "vector")
                        if (DEBUG == "p5c" and ch == 0 and sl == 0
                                and not dbg_done):
                            dbg_done.append(1)
                            dv = p.count(s_v)
                            p.wait("scalar", s_v, dv)
                            dbg_f32(dtb[:], 0, dbgst[:])
                            dbg_f32(dtx[:], 512, dbgst[:])
                            dbg_f32(s16[(NS - 1) % 2][:], 1024, dbgst[:])
                            dbg_f32(yy, 1536, dbgst[:])
                            dbg_f32(bbcb[:, 0:512], 2048, dbgst[:])
                            dbg_f32(bbcb[:, 512:1024], 2560, dbgst[:])
                            dbg_f32(xcvs[0][:], 3072, dbgst[:])
                            dbg_f32(xdbl[0][0:64, 0:512], 3584, dbgst[:])
                        p.wait("vector", s_g, szd)
                        ve(lambda e, szc=szc: e.tensor_tensor(
                            out=yy, in0=yy, in1=szc[:], op=OP.mult))
                        used(("szs", id(szc)), "vector")
                        ve(lambda e, sl=sl, xcb=xcb: e.scalar_tensor_tensor(
                            out=yf16[:, ts(sl, TC)], in0=xcb[:],
                            scalar=sd_[:, 80 + sl:80 + sl + 1], in1=yy,
                            op0=OP.mult, op1=OP.add))
                        used(("xcv", id(xcb)), "vector")
                    used(("bbcb",), "vector")
                    yf_done = p.count(s_v)
                    p.wait("tensor", s_v, yf_done)
                    for co in range(CT):
                        b = 1 + (co % 2)
                        acquire(b, "tensor")
                        for dh in range(4):
                            wob = woc[dh % 2]
                            acquire(("woc", id(wob)), "sync")
                            wod = dma("sync", wob[:],
                                      dr["s_wout"][co][:, dh * 512:(dh + 1) * 512])
                            p.wait("tensor", s_w, wod)
                            for j in range(4):
                                sl = dh * 4 + j
                                te(lambda e, sl=sl, j=j, b=b, wob=wob, dh=dh:
                                   e.matmul(
                                    out=pb[b][:], lhsT=wob[:, ts(j)],
                                    rhs=yf16[:, ts(sl, TC)],
                                    start=(sl == 0), stop=(sl == DSL - 1)),
                                   inc=(j == 3))
                            used(("woc", id(wob)), "tensor")
                        mmd = p.count(s_te)
                        p.wait("scalar", s_te, mmd)
                        xs = x_res[:, ts(co, T)][:, ts(ch, TC)]
                        sc(lambda e, b=b, xs=xs: e.activation(
                            out=xs, in_=pb[b][:], func=AF.Copy))
                        used(b, "scalar")
                    used(("yf",), "tensor")
                    used(("xdbl", id(xdb)), "tensor")
                barrier()
            if sub < 4:
                return
            p.serial = bool(SERALL)
            with ExitStack() as sco:
                bnc = [sb(f"p5bn{i}", [128, T], c=sco) for i in range(2)]
                for ct in range(CT):
                    bb = bnc[ct % 2]
                    acquire(("bnc", id(bb)), "gpsimd")
                    bd = dma("gpsimd", bb[:], dr["s_xsp"][ct])
                    p.wait("vector", s_g, bd)
                    ve(lambda e, ct=ct, bb=bb: e.tensor_tensor(
                        out=x_res[:, ts(ct, T)], in0=bb[:],
                        in1=x_res[:, ts(ct, T)], op=OP.add))
                    used(("bnc", id(bb)), "vector")
                barrier()

        if PHASES > 5:
            mamba_phase(P5SUB)
            p.serial = bool(SERALL)
        if PHASES > 6:
            moe_phase("p6")

        for ct in range(CT):
            dma("sync", out_ap[ct], x_res[:, ts(ct, T)])
        p.wait("sync", s_w)
        barrier()
        p.run(block)
    return nc


_NC_CACHE = {}


def _tile8(w):
    a = w.shape[0] // 128
    b = w.shape[1] // 128
    return np.ascontiguousarray(
        w.reshape(a, 128, b, 128).transpose(2, 1, 0, 3).reshape(b, 128, a * 128))


def kernel(L0_attn_x, L0_attn_w_k, L0_attn_w_v, L0_attn_w_r, L0_attn_w_o,
           L0_attn_time_decay, L0_attn_time_first, L0_attn_time_mix_key,
           L0_attn_time_mix_value, L0_attn_time_mix_receptance,
           L0_ffn_w_freq_real, L0_ffn_w_freq_imag, L0_ffn_bias,
           L1_attn_W_in, L1_attn_conv_w, L1_attn_conv_b, L1_attn_W_x,
           L1_attn_W_dt, L1_attn_b_dt, L1_attn_A_log, L1_attn_D,
           L1_attn_W_out, L1_ffn_Wg, L1_ffn_bg, L1_ffn_We, L1_ffn_be):
    f32 = lambda v: np.ascontiguousarray(np.asarray(v, dtype=np.float32))
    x = f32(L0_attn_x)
    B = x.shape[0]

    n_, d_ = np.meshgrid(np.arange(C), np.arange(C), indexing="ij")
    ang = (-2.0 * np.pi / C) * (n_ * d_ % C)
    fr16 = np.cos(ang).astype(np.float16).reshape(8, 128, C)
    fi16 = np.sin(ang).astype(np.float16).reshape(8, 128, C)
    ident16 = np.eye(128, dtype=np.float16)

    def colpack(v, k):
        return np.ascontiguousarray(f32(v).reshape(k, 128).T)

    smallv = np.zeros((128, 48), np.float32)
    smallv[:, 0:8] = colpack(L0_attn_time_decay, 8)
    smallv[:, 8:16] = colpack(L0_attn_time_first, 8)
    smallv[:, 16:24] = colpack(L0_ffn_bias, 8)
    smallv[:, 24:32] = colpack(np.asarray(L0_attn_time_mix_key).reshape(-1), 8)
    smallv[:, 32:40] = colpack(np.asarray(L0_attn_time_mix_value).reshape(-1), 8)
    smallv[:, 40:48] = colpack(np.asarray(L0_attn_time_mix_receptance).reshape(-1), 8)

    smalld = np.zeros((128, 368), np.float32)
    cw = f32(L1_attn_conv_w).reshape(DI, 4)
    smalld[:, 0:64] = cw.reshape(DSL, 128, 4).transpose(1, 0, 2).reshape(128, 64)
    smalld[:, 64:80] = colpack(L1_attn_conv_b, DSL)
    smalld[:, 80:96] = colpack(L1_attn_D, DSL)
    smalld[:, 96:112] = colpack(L1_attn_b_dt, DSL)
    al = f32(L1_attn_A_log)
    smalld[:, 112:112 + DSL * NS] = (
        al.reshape(DSL, 128, NS).transpose(1, 0, 2).reshape(128, DSL * NS))

    common = {
        "fr16": fr16, "fi16": fi16, "ident16": ident16,
        "wfr": f32(L0_ffn_w_freq_real).reshape(8, 128, C),
        "wfi": f32(L0_ffn_w_freq_imag).reshape(8, 128, C),
        "wkT_t": _tile8(f32(L0_attn_w_k).T),
        "wvT_t": _tile8(f32(L0_attn_w_v).T),
        "wrT_t": _tile8(f32(L0_attn_w_r).T),
        "woT_t": _tile8(f32(L0_attn_w_o).T),
        "winT_t": _tile8(f32(L1_attn_W_in).T),
        "woutT_t": _tile8(f32(L1_attn_W_out).T),
        "wxT_t": np.ascontiguousarray(
            f32(L1_attn_W_x).T.reshape(DSL, 128, 96).transpose(1, 0, 2)
            .reshape(128, DSL * 96)),
        "wdtT_t": f32(L1_attn_W_dt).T,
        "wg_t": f32(L1_ffn_Wg).reshape(8, 128, C),
        "we_t": f32(L1_ffn_We).reshape(E, 8, 128, C),
        "smallv": smallv, "smalld": smalld,
        "be": f32(L1_ffn_be), "bgrow": f32(L1_ffn_bg).reshape(1, C),
    }

    key = "prog"
    if key not in _NC_CACHE:
        _NC_CACHE[key] = build_program()
    nc = _NC_CACHE[key]

    in_maps = []
    for b in range(B):
        m = dict(common)
        m["xT"] = np.ascontiguousarray(x[b].T).reshape(CT, 128, T)
        in_maps.append(m)
    res = run_bass_kernel_spmd(nc, in_maps, core_ids=list(range(B)))
    out = np.empty((B, T, C), np.float32)
    for b in range(B):
        o = res.results[b]["out"].reshape(C, T)
        out[b] = o.T
    return out




# version 07
# speedup vs baseline: 1.6862x; 1.2143x over previous
import numpy as np
import concourse.bass as bass
import concourse.mybir as mybir
from concourse.bass_utils import run_bass_kernel_spmd
from contextlib import ExitStack

F32 = mybir.dt.float32
F16 = mybir.dt.float16
U32 = mybir.dt.uint32
AF = mybir.ActivationFunctionType
OP = mybir.AluOpType

ENGINES = ("sync", "scalar", "vector", "tensor", "gpsimd")
C, CT, T, TC = 1024, 8, 4096, 512
NCH = T // TC
DI, DSL, NS, RK, E = 2048, 16, 16, 64, 8
PHASES = 7
DEBUG = ""
P5SUB = 4
SKIPS = ()
MOESUB = 3
MOEBAR = 0
MOESER = 2
P5SER = 0
SERALL = 0


def ts(i, n=128):
    return slice(i * n, (i + 1) * n)


class Prog:
    def __init__(self):
        self.q = {e: [] for e in ENGINES}
        self.led = {}
        self.serial = False
        self.sems = None

    def emit(self, eng, fn, inc=None, n=1):
        if self.serial and self.sems is not None:
            for oe, osem in self.sems.items():
                if oe != eng:
                    self.wait(eng, osem)
        if inc is not None:
            self.led[id(inc)] = self.led.get(id(inc), 0) + n
            def wrapped(e, fn=fn, inc=inc, n=n):
                fn(e).then_inc(inc, n)
        else:
            def wrapped(e, fn=fn):
                fn(e)
        self.q[eng].append(wrapped)
        return self.led.get(id(inc), 0) if inc is not None else None

    def count(self, sem):
        return self.led.get(id(sem), 0)

    def wait(self, eng, sem, val=None):
        v = self.count(sem) if val is None else val
        if v > 0:
            self.q[eng].append(lambda e, sem=sem, v=v: e.wait_ge(sem, v))

    def run(self, block):
        for e in ENGINES:
            def make(e):
                def _(eng):
                    for fn in self.q[e]:
                        fn(eng)
                return _
            getattr(block, e)(make(e))


def build_program():
    nc = bass.Bass("TRN2", target_bir_lowering=False)
    dr = {}

    def din(name, shape, dtype=F32):
        dr[name] = nc.dram_tensor(name, shape, dtype, kind="ExternalInput").ap()

    def dscr(name, shape, dtype=F16):
        dr[name] = nc.dram_tensor(name, shape, dtype, kind="Internal").ap()

    din("xT", [CT, 128, T])
    din("fr16", [8, 128, C], F16); din("fi16", [8, 128, C], F16)
    din("ident16", [128, 128], F16)
    din("wfr", [8, 128, C]); din("wfi", [8, 128, C])
    din("wkT_t", [8, 128, C]); din("wvT_t", [8, 128, C]); din("wrT_t", [8, 128, C])
    din("woT_t", [8, 128, C])
    din("winT_t", [32, 128, C])
    din("woutT_t", [8, 128, DI])
    din("wxT_t", [128, DSL * 96])
    din("wdtT_t", [RK, DI])
    din("wg_t", [8, 128, C])
    din("we_t", [E, 8, 128, C])
    din("smallv", [128, 48])
    din("smalld", [128, 368])
    din("be", [8, C])
    din("bgrow", [1, C])
    dscr("s_m", [8, 128, C])
    dscr("s_kv", [6, 8, 128, C])
    dscr("s_wo", [8, 128, C])
    dscr("s_win", [32, 128, C])
    dscr("s_wout", [8, 128, DI])
    dscr("s_wg", [8, 128, C])
    dscr("s_bg", [1, C])
    dscr("s_we", [E, 2, 128, 8 * 512])
    dscr("s_xcv", [NCH, 128, DSL * TC])
    dscr("s_sz", [NCH, 128, DSL * TC])
    dscr("s_xdbl", [NCH, 96, TC])
    dscr("s_xsp", [CT, 128, T], F32)
    out_ap = nc.dram_tensor("out", [CT, 128, T], F32, kind="ExternalOutput").ap()
    dbg_ap = nc.dram_tensor("dbg", [128, 8192], F32, kind="ExternalOutput").ap()

    ctx = ExitStack()
    with ctx:
        def sb(name, shape, dtype=F32, c=None):
            return (c or ctx).enter_context(nc.sbuf_tensor(name, shape, dtype))

        ident = sb("ident", [128, 128], F16)
        ones16 = sb("ones16", [128, 128], F16)
        sv = sb("sv", [128, 72])
        lam8 = sb("lam8", [128, 8]); etf8 = sb("etf8", [128, 8])
        bias05 = sb("bias05", [128, 8])
        sd_ = sb("sd_", [128, 368])
        aneg = sb("aneg", [128, DSL * NS])
        be16 = sb("be16", [33, C], F16)
        bg16 = be16[32:33, :]
        wx16 = sb("wx16", [128, DSL * 96], F16)
        wdt16 = sb("wdt16", [RK, DI], F16)
        cpool = sb("cpool", [128, 4])
        ident32 = sb("ident32", [128, 128])
        nc.const_aps.register(F32, 1e-5, cpool[:, 0:1])
        nc.const_aps.register(F32, 1.0, cpool[:, 1:2])
        stA = sb("stA", [128, 8]); stB = sb("stB", [128, 8])
        stS = sb("stS", [128, DSL * NS])
        ctail = sb("ctail", [128, DSL * 4])
        xsh0 = sb("xsh0", [128, 8], F16)

        pb = [ctx.enter_context(nc.psum_tensor(f"pb{i}", [128, 512], F32))
              for i in range(6)]
        pbig = ctx.enter_context(nc.psum_tensor("pbig", [128, 1024], F32))

        sems = {e: ctx.enter_context(nc.semaphore(f"bar_{e}")) for e in ENGINES}
        s_w = ctx.enter_context(nc.semaphore("s_w"))
        s_g = ctx.enter_context(nc.semaphore("s_g"))
        s_te = ctx.enter_context(nc.semaphore("s_te"))
        s_sc = ctx.enter_context(nc.semaphore("s_sc"))
        s_v = ctx.enter_context(nc.semaphore("s_v"))
        block = ctx.enter_context(nc.Block())

        p = Prog()
        SEM = {"tensor": s_te, "scalar": s_sc, "vector": s_v,
               "sync": s_w, "gpsimd": s_g}
        p.sems = SEM
        if SERALL:
            p.serial = True

        def dma(queue, out, in_):
            return p.emit(queue, lambda e, o=out, i=in_: e.dma_start(out=o, in_=i),
                          inc=SEM[queue], n=16)

        def te(fn, inc=True):
            return p.emit("tensor", fn, inc=s_te if inc else None)

        def sc(fn):
            return p.emit("scalar", fn, inc=s_sc)

        def ve(fn):
            return p.emit("vector", fn, inc=s_v)

        def vfence():
            p.wait("vector", s_v, p.count(s_v))

        def barrier():
            p.wait("sync", s_w)
            p.wait("gpsimd", s_g)
            for e in ENGINES:
                p.emit(e, lambda eng: eng.nop(), inc=sems[e], n=1)
            tgt = {e: p.count(sems[e]) for e in ENGINES}
            for e in ENGINES:
                for o in ENGINES:
                    if o != e:
                        p.wait(e, sems[o], tgt[o])

        dbg_done = []
        dbgst = sb("dbgst", [128, 96]) if DEBUG else None
        guards = {}

        def acquire(res, producer_eng):
            for sem, cnt in guards.pop(res, ()):
                p.wait(producer_eng, sem, cnt)

        def used(res, eng):
            guards.setdefault(res, []).append((SEM[eng], p.count(SEM[eng])))

        def dbg_f32(ap, col0, stage):
            n = ap.shape[1]
            sw = stage.shape[1]
            for o in range(0, n, sw):
                w = min(sw, n - o)
                np_ = ap.shape[0]
                sc(lambda e, ap=ap, o=o, w=w, np_=np_: e.activation(
                    out=stage[0:np_, 0:w], in_=ap[:, o:o + w], func=AF.Copy))
                dn = p.count(s_sc)
                p.wait("sync", s_sc, dn)
                p.emit("sync", lambda e, o=o, w=w, np_=np_: e.dma_start(
                    out=dbg_ap[0:np_, col0 + o:col0 + o + w],
                    in_=stage[0:np_, 0:w],
                    allow_slow_non_contiguous=True), inc=s_w, n=16)
                wd = p.count(s_w)
                p.wait("scalar", s_w, wd)

        def ln_rows(ch, xf16, _unused, mb, rb, t1, var, b0, b1):
            acquire(("lnxf", id(xf16)), "scalar")
            acquire(("lnrows", id(mb)), "scalar")
            for ct in range(CT):
                xs = x_res[:, ts(ct, T)][:, ts(ch, TC)]
                sc(lambda e, xs=xs, ct=ct: e.activation(
                    out=xf16[:, ts(ct, TC)], in_=xs, func=AF.Copy))
            cast_done = p.count(s_sc)
            if DEBUG == "ones" and not dbg_done:
                dbg_done.append(1)
                p.wait("scalar", s_sc, cast_done)
                dbg_f32(ones16[:], 0, mb)
                dbg_f32(xf16[:, 0:1024], 1024, mb)
                for ct in range(2):
                    dbg_f32(x_res[:, ts(ct, T)][:, 0:512], 2048 + ct * 512, mb)
            acquire(b0, "tensor"); acquire(b1, "tensor")
            p.wait("tensor", s_sc, cast_done)
            for ct in range(CT):
                te(lambda e, ct=ct: e.matmul(
                    out=pb[b0][:], lhsT=ones16[:], rhs=xf16[:, ts(ct, TC)],
                    start=(ct == 0), stop=(ct == CT - 1)), inc=(ct == CT - 1))
            s_done = p.count(s_te)
            if DEBUG == "psum" and not dbg_done:
                dbg_done.append(1)
                p.wait("scalar", s_te, s_done)
                dbg_f32(pb[b0][:], 0, mb)
            p.wait("scalar", s_te, s_done)
            for ct in range(CT):
                xs = x_res[:, ts(ct, T)][:, ts(ch, TC)]
                sc(lambda e, xs=xs, ct=ct: e.activation(
                    out=xf16[:, ts(ct, TC)], in_=xs, func=AF.Square))
            sqcast_done = p.count(s_sc)
            p.wait("tensor", s_sc, sqcast_done)
            for ct in range(CT):
                te(lambda e, ct=ct: e.matmul(
                    out=pb[b1][:], lhsT=ones16[:], rhs=xf16[:, ts(ct, TC)],
                    start=(ct == 0), stop=(ct == CT - 1)), inc=(ct == CT - 1))
            q_done = p.count(s_te)
            used(("lnxf", id(xf16)), "tensor")
            p.wait("scalar", s_te, s_done)
            sc(lambda e: e.activation(out=mb, in_=pb[b0][:], func=AF.Copy,
                                      scale=1.0 / 1024))
            used(b0, "scalar")
            if DEBUG == "mb" and not dbg_done:
                dbg_done.append(1)
                dbg_f32(mb, 0, var)

            sc(lambda e: e.activation(out=t1, in_=mb, func=AF.Square))
            sq_done = p.count(s_sc)
            p.wait("vector", s_te, q_done)
            p.wait("vector", s_sc, sq_done)
            ve(lambda e: e.scalar_tensor_tensor(
                out=var, in0=pb[b1][:], scalar=1.0 / 1024, in1=t1,
                op0=OP.mult, op1=OP.subtract))
            used(b1, "vector")
            var_done = p.count(s_v)
            p.wait("scalar", s_v, var_done)
            sc(lambda e: e.activation(out=t1, in_=var, func=AF.Sqrt, bias=1e-5))
            sqrt_done = p.count(s_sc)
            p.wait("vector", s_sc, sqrt_done)
            ve(lambda e: e.reciprocal(out=rb, in_=t1))
            ve(lambda e: e.tensor_tensor(out=var, in0=mb, in1=rb, op=OP.mult))
            used(("lnrows", id(mb)), "vector")
            if DEBUG == "ln" and ch == 0 and not dbg_done:
                dbg_done.append(1)
                dv = p.count(s_v)
                p.wait("scalar", s_v, dv)
                dbg_f32(mb, 0, t1)
                dbg_f32(rb, 512, t1)
                dbg_f32(var, 1024, t1)

        dma("sync", ident[:], dr["ident16"])
        dma("sync", sv[:, 0:48], dr["smallv"])
        dma("sync", sd_[:], dr["smalld"])
        sm_w = p.count(s_w)
        ve(lambda e: e.memset(ones16[:], 1.0))
        ve(lambda e: e.memset(cpool[:, 0:1], 1e-5))
        ve(lambda e: e.memset(cpool[:, 1:2], 1.0))
        ve(lambda e: e.memset(stA[:], 0.0))
        ve(lambda e: e.memset(stB[:], 0.0))
        ve(lambda e: e.memset(stS[:], 0.0))
        ve(lambda e: e.memset(ctail[:], 0.0))
        ve(lambda e: e.memset(xsh0[:], 0.0))
        p.wait("scalar", s_w, sm_w)
        sc(lambda e: e.activation(out=cpool[:, 2:3], in_=cpool[:, 0:1],
                                  func=AF.Copy, accum_out=cpool[:, 3:4]))
        sc(lambda e: e.activation(out=ident32[:], in_=ident[:], func=AF.Copy))
        sc(lambda e: e.activation(out=lam8[:], in_=sv[:, 0:8], func=AF.Exp))
        sc(lambda e: e.activation(out=lam8[:], in_=lam8[:], func=AF.Exp, scale=-1.0))
        sc(lambda e: e.activation(out=etf8[:], in_=sv[:, 8:16], func=AF.Exp))
        sc(lambda e: e.activation(out=bias05[:], in_=sv[:, 16:24], func=AF.Copy,
                                  scale=0.5))
        sc(lambda e: e.activation(out=aneg[:], in_=sd_[:, 112:112 + DSL * NS],
                                  func=AF.Exp))
        d0 = p.count(s_sc)
        p.wait("vector", s_sc, d0)
        ve(lambda e: e.tensor_scalar(out=aneg[:], in0=aneg[:], scalar1=-1.0,
                                     scalar2=None, op0=OP.mult))
        for j, src0 in ((0, 24), (1, 32), (2, 40)):
            ve(lambda e, j=j, s0=src0: e.tensor_scalar(
                out=sv[:, 48 + 8 * j:56 + 8 * j], in0=sv[:, s0:s0 + 8],
                scalar1=-1.0, scalar2=1.0, op0=OP.mult, op1=OP.add))
        smalls_v = p.count(s_v)

        with ExitStack() as prep:
            stg = sb("stg", [128, 4096], F32, c=prep)
            stg16 = sb("stg16", [128, 2048], F16, c=prep)
            fr = sb("fr", [128, 8 * C], F16, c=prep)
            fi = sb("fi", [128, 8 * C], F16, c=prep)
            w16a = sb("w16a", [128, 8 * C], F16, c=prep)
            w16b = sb("w16b", [128, 8 * C], F16, c=prep)
            w16c = sb("w16c", [128, 8 * C], F16, c=prep)
            ptre = sb("ptre", [128, 8 * C], F16, c=prep)
            ptim = sb("ptim", [128, 8 * C], F16, c=prep)

            dma("sync", stg[0:8, 0:1024], dr["be"])
            dma("sync", stg[0:1, 1024:2048], dr["bgrow"])
            dma("sync", stg[:, 2048:2048 + DSL * 96], dr["wxT_t"])
            w0 = p.count(s_w)
            p.wait("scalar", s_w, w0)
            p.wait("scalar", s_v, smalls_v)
            sc(lambda e: e.activation(out=be16[0:8, :], in_=stg[0:8, 0:1024],
                                      func=AF.Copy))
            sc(lambda e: e.activation(out=bg16, in_=stg[0:1, 1024:2048],
                                      func=AF.Copy))
            bgc = p.count(s_sc)
            p.wait("sync", s_sc, bgc)
            dma("sync", dr["s_bg"], bg16)
            sc(lambda e: e.activation(out=wx16[:], in_=stg[:, 2048:2048 + DSL * 96],
                                      func=AF.Copy))
            used(("stg",), "scalar")
            acquire(("stg",), "sync")
            w0b = dma("sync", stg[0:RK, 0:DI], dr["wdtT_t"])
            p.wait("scalar", s_w, w0b)
            sc(lambda e: e.activation(
                out=wdt16[:], in_=stg[0:RK, 0:DI], func=AF.Copy))
            used(("stg",), "scalar")

            for ct in range(CT):
                dma("sync", fr[:, ts(ct, C)], dr["fr16"][ct])
                dma("sync", fi[:, ts(ct, C)], dr["fi16"][ct])
            fw = p.count(s_w)

            def cast_sq(dst16, src_t, scale=1.0, extra=None, extrascale=None):
                for g in range(2):
                    wdone = None
                    acquire(("stg",), "sync")
                    for j in range(4):
                        wdone = dma("sync", stg[:, ts(j, C)], src_t[g * 4 + j])
                    p.wait("scalar", s_w, wdone)
                    for j in range(4):
                        ctt = g * 4 + j
                        sc(lambda e, j=j, ctt=ctt: e.activation(
                            out=dst16[:, ts(ctt, C)], in_=stg[:, ts(j, C)],
                            func=AF.Copy, scale=scale))
                        if extra is not None:
                            sc(lambda e, j=j, ctt=ctt: e.activation(
                                out=extra[:, ts(ctt, C)], in_=stg[:, ts(j, C)],
                                func=AF.Copy, scale=extrascale))
                    used(("stg",), "scalar")

            cast_sq(w16a, dr["wfr"])
            cast_sq(w16b, dr["wfi"], extra=w16c, extrascale=-1.0)
            casts_sc = p.count(s_sc)
            p.wait("tensor", s_w, fw)
            p.wait("tensor", s_sc, casts_sc)

            def big_mm(dst16, lhs1, rhs1, lhs2, rhs2, scale=1.0):
                for och in range(8):
                    for half in range(2):
                        b = half
                        acquire(b, "tensor")
                        for k in range(8):
                            te(lambda e, o=och, h=half, k=k, b=b, l=lhs1, r=rhs1:
                               e.matmul(out=pb[b][:],
                                        lhsT=l[:, ts(k, C)][:, ts(o)],
                                        rhs=r[:, ts(k, C)][:, ts(h, 512)],
                                        start=(k == 0), stop=False), inc=False)
                        for k in range(8):
                            te(lambda e, o=och, h=half, k=k, b=b, l=lhs2, r=rhs2:
                               e.matmul(out=pb[b][:],
                                        lhsT=l[:, ts(k, C)][:, ts(o)],
                                        rhs=r[:, ts(k, C)][:, ts(h, 512)],
                                        start=False, stop=(k == 7)),
                               inc=(k == 7))
                        mmd = p.count(s_te)
                        p.wait("scalar", s_te, mmd)
                        sc(lambda e, o=och, h=half, b=b, d=dst16, sca=scale:
                           e.activation(out=d[:, ts(o, C)][:, ts(h, 512)],
                                        in_=pb[b][:], func=AF.Copy, scale=sca))
                        used(b, "scalar")

            big_mm(ptre, w16a, fr, w16c, fi)
            big_mm(ptim, w16a, fi, w16b, fr)
            m16 = w16c
            big_mm(m16, ptre, fr, ptim, fi, scale=0.5 / 1024)
            mdone = p.count(s_sc)
            p.wait("sync", s_sc, mdone)
            for ct in range(CT):
                dma("sync", dr["s_m"][ct], m16[:, ts(ct, C)])

            fold_plan = [("wkT_t", 0, 24, 0), ("wkT_t", 1, 48, 0),
                         ("wvT_t", 2, 32, 0), ("wvT_t", 3, 56, 0),
                         ("wrT_t", 4, 40, 0), ("wrT_t", 5, 64, 0)]
            for src, vi, col0, _ in fold_plan:
                for sl in range(8):
                    acquire(("stg",), "sync")
                    wdone = dma("sync", stg[:, 0:1024], dr[src][sl])
                    p.wait("scalar", s_w, wdone)
                    for ct in range(CT):
                        sc(lambda e, ct=ct, c0=col0: e.activation(
                            out=stg16[:, ts(ct)], in_=stg[:, ts(ct)],
                            func=AF.Copy, scale=sv[:, c0 + ct:c0 + ct + 1]))
                    used(("stg",), "scalar")
                    dn = p.count(s_sc)
                    p.wait("sync", s_sc, dn)
                    dma("sync", dr["s_kv"][vi][sl], stg16[:, 0:1024])
                    used(("stg16",), "sync")
                    acquire(("stg16",), "scalar")

            def plain_cast(src_t, dst_t, n_t, rows=128, cols=1024, scale=1.0):
                for s in range(n_t):
                    acquire(("stg",), "sync")
                    wdone = dma("sync", stg[0:rows, 0:cols], src_t[s])
                    p.wait("scalar", s_w, wdone)
                    sc(lambda e, r=rows, cl=cols, sca=scale: e.activation(
                        out=stg16[0:r, 0:cl], in_=stg[0:r, 0:cl], func=AF.Copy,
                        scale=sca))
                    used(("stg",), "scalar")
                    dn = p.count(s_sc)
                    p.wait("sync", s_sc, dn)
                    dma("sync", dst_t[s], stg16[0:rows, 0:cols])
                    used(("stg16",), "sync")
                    acquire(("stg16",), "scalar")

            plain_cast(dr["woT_t"], dr["s_wo"], 8, scale=0.5)
            plain_cast(dr["winT_t"], dr["s_win"], 32)
            plain_cast(dr["wg_t"], dr["s_wg"], 8)
            for ee in range(E):
                for ct in range(CT):
                    acquire(("stg",), "sync")
                    wdone = dma("sync", stg[:, 0:1024], dr["we_t"][ee][ct])
                    p.wait("scalar", s_w, wdone)
                    sc(lambda e: e.activation(out=stg16[:, 0:1024],
                                              in_=stg[:, 0:1024], func=AF.Copy))
                    used(("stg",), "scalar")
                    dn = p.count(s_sc)
                    p.wait("sync", s_sc, dn)
                    for fh in range(2):
                        dma("sync",
                            dr["s_we"][ee][fh][:, ct * 512:(ct + 1) * 512],
                            stg16[:, fh * 512:fh * 512 + 512])
                    used(("stg16",), "sync")
                    acquire(("stg16",), "scalar")
            for co in range(8):
                acquire(("stg",), "sync")
                wdone = dma("sync", stg[:, 0:2048], dr["woutT_t"][co])
                p.wait("scalar", s_w, wdone)
                sc(lambda e: e.activation(out=stg16[:, 0:2048],
                                          in_=stg[:, 0:2048], func=AF.Copy))
                used(("stg",), "scalar")
                dn = p.count(s_sc)
                p.wait("sync", s_sc, dn)
                dma("sync", dr["s_wout"][co], stg16[:, 0:2048])
                used(("stg16",), "sync")
                acquire(("stg16",), "scalar")
            barrier()

        x_res = sb("x_res", [128, CT * T])
        for ct in range(CT):
            dma("sync", x_res[:, ts(ct, T)], dr["xT"][ct])
        barrier()

        def ffn0_phase(tag):
            with ExitStack() as sco:
                m16 = sb(f"{tag}m", [128, 8 * C], F16, c=sco)
                xn16 = sb(f"{tag}xn", [128, CT * TC], F16, c=sco)
                xf16 = sb(f"{tag}xf", [128, CT * TC], F16, c=sco)
                mb = sb(f"{tag}mb", [128, TC], c=sco)[:]
                rb = sb(f"{tag}rb", [128, TC], c=sco)[:]
                t1 = sb(f"{tag}t1", [128, TC], c=sco)[:]
                var = sb(f"{tag}v", [128, TC], c=sco)[:]
                for ct in range(CT):
                    dma("sync", m16[:, ts(ct, C)], dr["s_m"][ct])
                mw = p.count(s_w)
                p.wait("tensor", s_w, mw)
                if DEBUG == "m16" and tag == "p1":
                    p.wait("scalar", s_w, mw)
                    for ct in range(CT):
                        dbg_f32(m16[:, ts(ct, C)], ct * 1024, mb)
                for ch in range(NCH):
                    ln_rows(ch, xf16, None, mb, rb, t1, var, 0, 1)
                    acquire(("xn", id(xn16)), "vector")
                    for ct in range(CT):
                        xs = x_res[:, ts(ct, T)][:, ts(ch, TC)]
                        ve(lambda e, xs=xs: e.tensor_tensor(
                            out=t1, in0=xs, in1=rb, op=OP.mult))
                        ve(lambda e, ct=ct: e.tensor_tensor(
                            out=xn16[:, ts(ct, TC)], in0=t1, in1=var,
                            op=OP.subtract))
                    app = p.count(s_v)
                    for co in range(CT):
                        b = 2 + (co % 2)
                        acquire(b, "tensor")
                        p.wait("tensor", s_v, app)
                        for k in range(CT):
                            te(lambda e, co=co, k=k, b=b: e.matmul(
                                out=pb[b][:], lhsT=m16[:, ts(k, C)][:, ts(co)],
                                rhs=xn16[:, ts(k, TC)],
                                start=(k == 0), stop=(k == CT - 1)),
                               inc=(k == CT - 1))
                        used(("xn", id(xn16)), "tensor")
                        mmd = p.count(s_te)
                        p.wait("vector", s_te, mmd)
                        xs = x_res[:, ts(co, T)][:, ts(ch, TC)]
                        ve(lambda e, b=b, xs=xs, co=co: e.scalar_tensor_tensor(
                            out=t1, in0=pb[b][:], scalar=bias05[:, co:co + 1],
                            in1=xs, op0=OP.add, op1=OP.add))
                        used(b, "vector")
                        ve(lambda e, xs=xs: e.tensor_copy(out=xs, in_=t1))
                    used(("lnrows", id(mb)), "vector")
                barrier()

        if PHASES > 1 and 1 not in SKIPS:
            ffn0_phase("p1")

        def rwkv_phase():
            with ExitStack() as sco:
                xn16 = sb("p2xn", [128, CT * (TC + 1)], F16, c=sco)
                xf16 = sb("p2xf", [128, CT * TC], F16, c=sco)
                mb = sb("p2mb", [128, TC], c=sco)[:]; rb = sb("p2rb", [128, TC], c=sco)[:]
                t1 = sb("p2t1", [128, TC], c=sco)[:]; var = sb("p2v", [128, TC], c=sco)[:]
                kv16 = [sb("p2kv0", [128, 6 * C], F16, c=sco)] * 2
                wo16 = [sb(f"p2wo{i}", [128, C], F16, c=sco) for i in range(2)]
                rk16 = sb("p2rk", [128, 8 * TC], F16, c=sco)
                ek = sb("p2ek", [128, TC], c=sco)[:]; uu = sb("p2u", [128, TC], c=sco)[:]
                As = sb("p2A", [128, TC + 1], c=sco)
                Bs = sb("p2B", [128, TC + 1], c=sco)
                num = sb("p2n", [128, TC], c=sco)[:]; den = sb("p2d", [128, TC], c=sco)[:]
                rec = sb("p2r", [128, TC], c=sco)[:]; th = sb("p2th", [128, TC], c=sco)[:]

                def xnsl(ct):
                    return xn16[:, ct * (TC + 1):(ct + 1) * (TC + 1)]

                for ch in range(NCH):
                    ln_rows(ch, xf16, None, mb, rb, t1, var, 0, 1)
                    acquire("xn2", "vector")
                    for ct in range(CT):
                        ve(lambda e, ct=ct: e.tensor_copy(
                            out=xnsl(ct)[:, 0:1], in_=xsh0[:, ct:ct + 1]))
                    for ct in range(CT):
                        xs = x_res[:, ts(ct, T)][:, ts(ch, TC)]
                        ve(lambda e, xs=xs: e.tensor_tensor(
                            out=t1, in0=xs, in1=rb, op=OP.mult))
                        ve(lambda e, ct=ct: e.tensor_tensor(
                            out=xnsl(ct)[:, 1:TC + 1], in0=t1, in1=var,
                            op=OP.subtract))
                    for ct in range(CT):
                        ve(lambda e, ct=ct: e.tensor_copy(
                            out=xsh0[:, ct:ct + 1], in_=xnsl(ct)[:, TC:TC + 1]))
                    xn_done = p.count(s_v)
                    used(("lnrows", id(mb)), "vector")
                    acquire("rk2", "vector")

                    for sl in range(8):
                        kvb = kv16[sl % 2]
                        acquire(("kv", id(kvb)), "sync")
                        wvd = None
                        for v6 in range(6):
                            wvd = dma("sync", kvb[:, ts(v6, C)],
                                      dr["s_kv"][v6][sl])
                        p.wait("tensor", s_w, wvd)
                        p.wait("tensor", s_v, xn_done)
                        for wi in range(3):
                            b = wi
                            acquire(b, "tensor")
                            for ct in range(CT):
                                te(lambda e, wi=wi, ct=ct, b=b, kvb=kvb: e.matmul(
                                    out=pb[b][:],
                                    lhsT=kvb[:, ts(2 * wi, C)][:, ts(ct)],
                                    rhs=xnsl(ct)[:, 1:TC + 1],
                                    start=(ct == 0), stop=False), inc=False)
                            for ct in range(CT):
                                te(lambda e, wi=wi, ct=ct, b=b, kvb=kvb: e.matmul(
                                    out=pb[b][:],
                                    lhsT=kvb[:, ts(2 * wi + 1, C)][:, ts(ct)],
                                    rhs=xnsl(ct)[:, 0:TC],
                                    start=False, stop=(ct == CT - 1)),
                                   inc=(ct == CT - 1))
                        used("xn2", "tensor")
                        used(("kv", id(kvb)), "tensor")
                        mmd = p.count(s_te)
                        p.wait("scalar", s_te, mmd - 2)
                        sc(lambda e: e.activation(out=ek, in_=pb[0][:], func=AF.Exp))
                        used(0, "scalar")
                        ek_done = p.count(s_sc)
                        p.wait("scalar", s_te, mmd)
                        sc(lambda e: e.activation(out=th, in_=pb[2][:],
                                                  func=AF.Tanh, scale=0.5))
                        used(2, "scalar")
                        th_done = p.count(s_sc)
                        p.wait("vector", s_sc, ek_done)
                        p.wait("vector", s_te, mmd - 1)
                        ve(lambda e: e.tensor_tensor(out=uu, in0=ek, in1=pb[1][:],
                                                     op=OP.mult))
                        used(1, "vector")
                        ve(lambda e, sl=sl: e.tensor_copy(
                            out=As[:, 0:1], in_=stA[:, sl:sl + 1]))
                        ve(lambda e, sl=sl: e.tensor_copy(
                            out=Bs[:, 0:1], in_=stB[:, sl:sl + 1]))
                        ve(lambda e, sl=sl: e.tensor_tensor_scan(
                            out=As[:, 1:TC + 1],
                            data0=lam8[:, sl:sl + 1].broadcast_to((128, TC)),
                            data1=uu, initial=As[:, 0:1],
                            op0=OP.mult, op1=OP.add))
                        ve(lambda e, sl=sl: e.tensor_tensor_scan(
                            out=Bs[:, 1:TC + 1],
                            data0=lam8[:, sl:sl + 1].broadcast_to((128, TC)),
                            data1=ek, initial=Bs[:, 0:1],
                            op0=OP.mult, op1=OP.add))
                        ve(lambda e, sl=sl: e.tensor_copy(
                            out=stA[:, sl:sl + 1], in_=As[:, TC:TC + 1]))
                        ve(lambda e, sl=sl: e.tensor_copy(
                            out=stB[:, sl:sl + 1], in_=Bs[:, TC:TC + 1]))
                        ve(lambda e, sl=sl: e.scalar_tensor_tensor(
                            out=num, in0=uu, scalar=etf8[:, sl:sl + 1],
                            in1=As[:, 0:TC], op0=OP.mult, op1=OP.add))
                        ve(lambda e, sl=sl: e.scalar_tensor_tensor(
                            out=den, in0=ek, scalar=etf8[:, sl:sl + 1],
                            in1=Bs[:, 0:TC], op0=OP.mult, op1=OP.add))
                        ve(lambda e: e.reciprocal(out=rec, in_=den))
                        ve(lambda e: e.tensor_tensor(out=num, in0=num, in1=rec,
                                                     op=OP.mult))
                        p.wait("vector", s_sc, th_done)
                        ve(lambda e, sl=sl: e.scalar_tensor_tensor(
                            out=rk16[:, ts(sl, TC)], in0=th, scalar=1.0,
                            in1=num, op0=OP.add, op1=OP.mult))
                        if DEBUG == "rwkv" and ch == 0 and sl == 0 and not dbg_done:
                            dbg_done.append(1)
                            dv = p.count(s_v)
                            p.wait("scalar", s_v, dv)
                            dbg_f32(ek, 0, dbgst[:])
                            dbg_f32(uu, 512, dbgst[:])
                            dbg_f32(As[:, 1:TC + 1], 1024, dbgst[:])
                            dbg_f32(Bs[:, 1:TC + 1], 1536, dbgst[:])
                            dbg_f32(num, 2048, dbgst[:])
                            dbg_f32(th, 2560, dbgst[:])
                            dbg_f32(rk16[:, 0:TC], 3072, dbgst[:])
                            dbg_f32(xn16[:, 0:TC + 1], 3584, dbgst[:])
                    rk_done = p.count(s_v)
                    p.wait("tensor", s_v, rk_done)
                    for co in range(CT):
                        wob = wo16[co % 2]
                        acquire(("p2wo", id(wob)), "sync")
                        wod = dma("sync", wob[:], dr["s_wo"][co])
                        b = 3 + (co % 2)
                        acquire(b, "tensor")
                        p.wait("tensor", s_w, wod)
                        for ds in range(8):
                            te(lambda e, co=co, ds=ds, b=b, wob=wob: e.matmul(
                                out=pb[b][:], lhsT=wob[:, ts(ds)],
                                rhs=rk16[:, ts(ds, TC)],
                                start=(ds == 0), stop=(ds == 7)), inc=(ds == 7))
                        used(("p2wo", id(wob)), "tensor")
                        mmd = p.count(s_te)
                        p.wait("vector", s_te, mmd)
                        xs = x_res[:, ts(co, T)][:, ts(ch, TC)]
                        ve(lambda e, b=b, xs=xs: e.tensor_tensor(
                            out=xs, in0=pb[b][:], in1=xs, op=OP.add))
                        used(b, "vector")
                    used("rk2", "tensor")
                    acquire("xn2", "vector")
                barrier()

        if PHASES > 2 and 2 not in SKIPS:
            rwkv_phase()
        if PHASES > 3 and 3 not in SKIPS:
            ffn0_phase("p3")

        def moe_phase(tag):
            p.serial = (MOESER == 1 or (MOESER == 2 and tag == "p6")
                        or bool(SERALL))
            with ExitStack() as sco:
                xn16 = sb(f"{tag}xn", [128, CT * TC], F16, c=sco)
                xf16 = sb(f"{tag}xf", [128, CT * TC], F16, c=sco)
                mb = sb(f"{tag}mb", [128, TC], c=sco)[:]
                rb = sb(f"{tag}rb", [128, TC], c=sco)[:]
                t1 = sb(f"{tag}t1", [128, TC], c=sco)[:]
                var = sb(f"{tag}v", [128, TC], c=sco)[:]
                wu = sb(f"{tag}wu", [128, 8 * C], F16, c=sco)
                wg16 = wu
                we16 = [wu[:, 0:8 * 512], wu[:, 8 * 512:8 * C]]
                mx16 = [sb(f"{tag}mx0", [128, CT * TC], F16, c=sco)] * 2
                q32 = sb(f"{tag}q", [128, 1024], c=sco)
                mcol = sb(f"{tag}mc", [128, 4 * 8], c=sco)
                mrow = sb(f"{tag}mr", [8, 512], F16, c=sco)
                mb16 = sb(f"{tag}mb16", [128, 2 * TC], F16, c=sco)
                outT = sb(f"{tag}oT", [128, 4 * 512], c=sco)
                p05 = sb(f"{tag}p05", [128, 4], c=sco)
                mxc = sb(f"{tag}mxc", [128, 8], c=sco)
                eidxu = sb(f"{tag}eiu", [128, 1], U32, c=sco)
                eidxf = sb(f"{tag}eif", [128, 1], c=sco)
                sumc = sb(f"{tag}sum", [128, 2], c=sco)
                iota16 = sb(f"{tag}io", [128, 1024], F16, c=sco)
                p.emit("gpsimd", lambda e: e.iota(
                    out=iota16[:], pattern=[[1, 1024]], base=-2048,
                    channel_multiplier=0,
                    allow_small_or_imprecise_dtypes=True), inc=s_g, n=1)
                p.wait("vector", s_g, p.count(s_g))

                for ch in range(NCH):
                    acquire(("wu", id(we16[0])), "sync")
                    acquire(("wu", id(we16[1])), "sync")
                    for ct in range(CT):
                        dma("sync", wg16[:, ts(ct, C)], dr["s_wg"][ct])
                    wg_w = p.count(s_w)
                    p.wait("tensor", s_w, wg_w)
                    ln_rows(ch, xf16, None, mb, rb, t1, var, 0, 1)
                    acquire(("xn", id(xn16)), "vector")
                    for ct in range(CT):
                        xs = x_res[:, ts(ct, T)][:, ts(ch, TC)]
                        ve(lambda e, xs=xs: e.tensor_tensor(
                            out=t1, in0=xs, in1=rb, op=OP.mult))
                        ve(lambda e, ct=ct: e.tensor_tensor(
                            out=xn16[:, ts(ct, TC)], in0=t1, in1=var,
                            op=OP.subtract))
                    app = p.count(s_v)
                    used(("lnrows", id(mb)), "vector")
                    acquire(("lnxf", id(xf16)), "gpsimd")
                    bgw = dma("gpsimd", xf16[0:1, 0:1024], dr["s_bg"])
                    p.wait("tensor", s_g, bgw)
                    acquire(("mcol",), "vector")
                    acquire(("p05",), "vector")
                    for su in range(4):
                        acquire("pbig", "tensor")
                        p.wait("tensor", s_v, app)
                        for half in range(2):
                            for k in range(CT):
                                te(lambda e, su=su, h=half, k=k: e.matmul(
                                    out=pbig[:, ts(h, 512)],
                                    lhsT=xn16[:, ts(k, TC)][:, ts(su)],
                                    rhs=wg16[:, ts(k, C)][:, ts(h, 512)],
                                    start=(k == 0), stop=False), inc=False)
                            te(lambda e, h=half: e.matmul(
                                out=pbig[:, ts(h, 512)],
                                lhsT=ones16[0:1, :],
                                rhs=xf16[0:1, ts(h, 512)],
                                start=False, stop=True), inc=True)
                        used(("wu", id(we16[0])), "tensor")
                        used(("wu", id(we16[1])), "tensor")
                        used(("xn", id(xn16)), "tensor")
                        used(("lnxf", id(xf16)), "tensor")
                        mmd = p.count(s_te)
                        if DEBUG == "moe2" and su == 0 and not dbg_done:
                            dbg_done.append(1)
                            p.wait("scalar", s_te, mmd)
                            p.wait("scalar", s_v, app)
                            dbg_f32(pbig[:], 0, dbgst[:])
                            dbg_f32(xn16[:, 0:1024], 1024, dbgst[:])
                            dbg_f32(wg16[:, 0:1024], 2048, dbgst[:])
                            p.wait("vector", s_sc, p.count(s_sc))
                        p.wait("vector", s_te, mmd)
                        for h in range(2):
                            ve(lambda e, h=h: e.reduce_max(
                                out=mxc[:, 2 + h:3 + h], in_=pbig[:, ts(h, 512)],
                                axis=mybir.AxisListType.X))
                        vfence()
                        ve(lambda e: e.tensor_tensor(
                            out=mxc[:, 0:1], in0=mxc[:, 2:3], in1=mxc[:, 3:4],
                            op=OP.max))
                        vfence()
                        ve(lambda e: e.tensor_scalar(
                            out=mxc[:, 1:2], in0=mxc[:, 0:1], scalar1=-1.0,
                            scalar2=None, op0=OP.mult))
                        neg_done = p.count(s_v)
                        acquire(("q32",), "scalar")
                        p.wait("scalar", s_v, neg_done)
                        for h in range(2):
                            sc(lambda e, h=h: e.activation(
                                out=q32[:, ts(h, 512)], in_=pbig[:, ts(h, 512)],
                                func=AF.Exp, bias=mxc[:, 1:2], scale=1.0))
                        q_done = p.count(s_sc)
                        p.wait("vector", s_sc, q_done)
                        for h in range(2):
                            ve(lambda e, h=h: e.reduce_sum(
                                out=sumc[:, h:h + 1], in_=q32[:, ts(h, 512)],
                                axis=mybir.AxisListType.X))
                        vfence()
                        ve(lambda e: e.tensor_tensor(
                            out=sumc[:, 0:1], in0=sumc[:, 0:1],
                            in1=sumc[:, 1:2], op=OP.add))
                        vfence()
                        ve(lambda e: e.tensor_scalar(
                            out=sumc[:, 0:1], in0=sumc[:, 0:1], scalar1=2.0,
                            scalar2=None, op0=OP.mult))
                        vfence()
                        ve(lambda e, su=su: e.reciprocal(out=p05[:, su:su + 1],
                                                         in_=sumc[:, 0:1]))
                        for h in range(2):
                            ve(lambda e, h=h: e.tensor_scalar(
                                out=q32[:, ts(h, 512)], in0=pbig[:, ts(h, 512)],
                                scalar1=mxc[:, 0:1], scalar2=None,
                                op0=OP.is_equal))
                        used("pbig", "vector")
                        ve(lambda e: e.tensor_tensor(
                            out=q32[:], in0=q32[:], in1=iota16[:], op=OP.mult))
                        ve(lambda e: e.tensor_reduce(
                            out=eidxf[:], in_=q32[:],
                            axis=mybir.AxisListType.X, op=OP.min))
                        used(("q32",), "vector")
                        vfence()
                        ve(lambda e: e.tensor_scalar(
                            out=eidxf[:], in0=eidxf[:], scalar1=2048.0,
                            scalar2=None, op0=OP.add))
                        vfence()
                        ve(lambda e: e.tensor_copy(out=eidxu[:], in_=eidxf[:]))
                        vfence()
                        ve(lambda e: e.tensor_scalar(
                            out=eidxu[:], in0=eidxu[:], scalar1=7,
                            scalar2=None, op0=OP.bitwise_and))
                        vfence()
                        ve(lambda e: e.tensor_copy(out=eidxf[:], in_=eidxu[:]))
                        vfence()
                        for ee in range(E):
                            ve(lambda e, su=su, ee=ee: e.tensor_scalar(
                                out=mcol[:, su * 8 + ee:su * 8 + ee + 1],
                                in0=eidxf[:], scalar1=float(ee), scalar2=None,
                                op0=OP.is_equal))
                        if DEBUG == "moe" and su == 0 and not dbg_done:
                            dbg_done.append(1)
                            dv = p.count(s_v)
                            p.wait("scalar", s_v, dv)
                            dbg_f32(q32[:], 0, dbgst[:])
                            dbg_f32(mxc[:, 0:8], 1024, dbgst[:])
                            dbg_f32(sumc[:, 0:1], 1032, dbgst[:])
                            dbg_f32(p05[:, 0:1], 1033, dbgst[:])
                            dbg_f32(eidxf[:], 1034, dbgst[:])
                            dbg_f32(xn16[:, 0:512], 2048, dbgst[:])
                    mask_done = p.count(s_v)
                    p.wait("tensor", s_v, mask_done)
                    acquire(("mrow",), "scalar")
                    for su in range(4):
                        b = su % 2
                        acquire(b, "tensor")
                        te(lambda e, su=su, b=b: e.transpose(
                            out=pb[b][0:8, 0:128], in_=mcol[:, su * 8:su * 8 + 8],
                            identity=ident32[:]))
                        trd = p.count(s_te)
                        p.wait("scalar", s_te, trd)
                        sc(lambda e, su=su, b=b: e.activation(
                            out=mrow[:, ts(su)], in_=pb[b][0:8, 0:128],
                            func=AF.Copy))
                        used(b, "scalar")
                    used(("mcol",), "tensor")
                    mrow_done = p.count(s_sc)
                    p.wait("tensor", s_sc, mrow_done)
                    p.wait("gpsimd", s_sc, mrow_done)
                    fm_done = p.count(s_sc)
                    p.wait("tensor", s_sc, mrow_done)
                    if MOEBAR & 1:
                        barrier()
                    for fh in range(2 if MOESUB >= 2 else 0):
                        for ee in range(E):
                            web = we16[ee % 2]
                            acquire(("wu", id(we16[ee % 2])), "sync")
                            wed = dma("sync", web, dr["s_we"][ee][fh])
                            half = ee % 2
                            acquire(("mslot",), "gpsimd")
                            msw = dma("gpsimd", mb16[0:1, 512:1024],
                                      mrow[ee:ee + 1, :])
                            acquire("pbig", "tensor")
                            p.wait("tensor", s_g, msw)
                            te(lambda e, ee=ee, h=half: e.matmul(
                                out=pbig[:, ts(h, 512)],
                                lhsT=ones16[0:1, :],
                                rhs=mb16[0:1, 512:1024],
                                start=True, stop=True))
                            used(("mslot",), "tensor")
                            bd = p.count(s_te)
                            acquire(("mb16", 0), "scalar")
                            p.wait("scalar", s_te, bd)
                            sc(lambda e, ee=ee, h=half: e.activation(
                                out=mb16[:, 0:TC],
                                in_=pbig[:, ts(h, 512)], func=AF.Copy))
                            used("pbig", "scalar")
                            mbd = p.count(s_sc)
                            mx = mx16[ee % 2]
                            acquire(("mx", id(mx)), "vector")
                            p.wait("vector", s_sc, mbd)
                            for ct in range(CT):
                                ve(lambda e, ct=ct, ee=ee, mx=mx: e.tensor_tensor(
                                    out=mx[:, ts(ct, TC)],
                                    in0=xn16[:, ts(ct, TC)],
                                    in1=mb16[:, 0:TC], op=OP.mult))
                            used(("mb16", 0), "vector")
                            mxd = p.count(s_v)
                            if ee == 0:
                                for su in range(4):
                                    acquire(su, "tensor")
                            p.wait("tensor", s_v, mxd)
                            p.wait("tensor", s_w, wed)
                            for su in range(4):
                                for k in range(CT):
                                    te(lambda e, fh=fh, su=su, k=k, ee=ee,
                                       mx=mx, web=web: e.matmul(
                                        out=pb[su][:],
                                        lhsT=mx[:, ts(k, TC)][:, ts(su)],
                                        rhs=web[:, ts(k, 512)],
                                        start=(ee == 0 and k == 0), stop=False),
                                       inc=False)
                            p.emit("tensor", lambda e: e.nop(), inc=s_te)
                            used(("wu", id(we16[ee % 2])), "tensor")
                            used(("mx", id(mx)), "tensor")
                        for su in range(4):
                            te(lambda e, fh=fh, su=su: e.matmul(
                                out=pb[su][:], lhsT=mrow[:, ts(su)],
                                rhs=be16[0:8, ts(fh, 512)],
                                start=False, stop=True), inc=True)
                        used(("mrow",), "tensor")
                        emm = p.count(s_te)
                        acquire(("outT",), "scalar")
                        p.wait("scalar", s_te, emm)
                        for su in range(4):
                            sc(lambda e, su=su: e.activation(
                                out=outT[:, ts(su, 512)],
                                in_=pb[su][:], func=AF.Relu,
                                scale=p05[:, su:su + 1]))
                            used(su, "scalar")
                        used(("p05",), "scalar")
                        ev_done = p.count(s_sc)
                        if (DEBUG == "p4x" and ch == 0 and fh == 0
                                and not dbg_done):
                            dbg_done.append(1)
                            dbg_f32(outT[:, 0:2048], 0, dbgst[:])
                            dbg_f32(mrow[0:8, 0:512], 2048, dbgst[:])
                            dbg_f32(mb16[:, 0:1024], 2560, dbgst[:])
                            dbg_f32(p05[:, 0:4], 3584, dbgst[:])
                            dbg_f32(mcol[:, 0:32], 3588, dbgst[:])
                        p.wait("tensor", s_sc, ev_done)
                        for ft in range(4):
                            b = 4 + (ft % 2)
                            acquire(b, "tensor")
                            for su in range(4):
                                te(lambda e, ft=ft, su=su, b=b: e.transpose(
                                    out=pb[b][:, ts(su)],
                                    in_=outT[:, su * 512 + ft * 128:
                                             su * 512 + (ft + 1) * 128],
                                    identity=ident32[:]), inc=(su == 3))
                            trd = p.count(s_te)
                            if (DEBUG == "p4add" and ch == 0 and fh == 0
                                    and ft == 0 and not dbg_done):
                                dbg_done.append(1)
                                p.wait("scalar", s_te, trd)
                                dbg_f32(pb[b][:], 0, dbgst[:])
                                dbg_f32(outT[:, 0:512], 512, dbgst[:])
                                dbg_f32(x_res[:, 0:512], 1024, dbgst[:])
                                p.wait("vector", s_sc, p.count(s_sc))
                            p.wait("vector", s_te, trd)
                            co = fh * 4 + ft
                            xs = x_res[:, ts(co, T)][:, ts(ch, TC)]
                            if MOESUB >= 3:
                                ve(lambda e, b=b, xs=xs: e.tensor_tensor(
                                    out=xs, in0=pb[b][:], in1=xs, op=OP.add))
                            else:
                                ve(lambda e, b=b: e.tensor_copy(
                                    out=var, in_=pb[b][:]))
                            used(b, "vector")
                        used(("outT",), "tensor")
                        if MOEBAR & 2:
                            barrier()
                    if MOEBAR & 4:
                        barrier()
                    if DEBUG == "p4end" and ch == NCH - 1:
                        dv = p.count(s_v)
                        p.wait("scalar", s_v, dv)
                        dbg_f32(x_res[:, 0:512], 0, dbgst[:])
                        dbg_f32(x_res[:, 7 * T + 3584:7 * T + 4096], 512, dbgst[:])
                barrier()

        if PHASES > 4 and 4 not in SKIPS:
            moe_phase("p4")
            p.serial = bool(SERALL)

        def mamba_phase(sub=99):
            if sub < 1:
                return
            p.serial = bool(P5SER & 1) or bool(SERALL)
            with ExitStack() as sco:
                xn16 = sb("p5xn", [128, CT * TC], F16, c=sco)
                xf16 = sb("p5xf", [128, CT * TC], F16, c=sco)
                mb = sb("p5mb", [128, TC], c=sco)[:]; rb = sb("p5rb", [128, TC], c=sco)[:]
                t1 = sb("p5t1", [128, TC], c=sco)[:]; var = sb("p5v", [128, TC], c=sco)[:]
                wi16 = [sb(f"p5wi{i}", [128, 8 * 128], F16, c=sco) for i in range(2)]
                xin = sb("p5xi", [128, TC + 4], c=sco)
                cv0 = rb
                cv1 = t1
                xcv = sb("p5xcv", [128, DSL * TC], F16, c=sco)
                szb = sb("p5sz", [128, DSL * TC], F16, c=sco)
                xdst = sb("p5xd", [96, TC], F16, c=sco)
                if DEBUG == "p5a":
                    dbg_f32(x_res[:, 0:512], 0, dbgst[:])
                for ch in range(NCH):
                    if DEBUG == "p5a" and ch == 1:
                        dv = p.count(s_v)
                        p.wait("scalar", s_v, dv)
                        dbg_f32(x_res[:, 0:512], 512, dbgst[:])
                        dbg_f32(xn16[:, 0:512], 1024, dbgst[:])
                        dbg_f32(xcv[:, 0:512], 1536, dbgst[:])
                    ln_rows(ch, xf16, None, mb, rb, t1, var, 3, 4)
                    acquire("xn5", "vector")
                    for ct in range(CT):
                        xs = x_res[:, ts(ct, T)][:, ts(ch, TC)]
                        ve(lambda e, xs=xs: e.tensor_tensor(
                            out=t1, in0=xs, in1=rb, op=OP.mult))
                        ve(lambda e, ct=ct: e.tensor_tensor(
                            out=xn16[:, ts(ct, TC)], in0=t1, in1=var,
                            op=OP.subtract))
                    app = p.count(s_v)
                    p.wait("tensor", s_v, app)
                    acquire(2, "tensor")
                    for half, nsl in ((0, DSL), (1, DSL)):
                        for sl in range(nsl):
                            wib = wi16[sl % 2]
                            acquire(("wi", id(wib)), "sync")
                            wid = dma("sync", wib[:], dr["s_win"][half * DSL + sl])
                            b = sl % 2
                            acquire(b, "tensor")
                            p.wait("tensor", s_w, wid)
                            for k in range(CT):
                                te(lambda e, k=k, b=b, wib=wib: e.matmul(
                                    out=pb[b][:], lhsT=wib[:, ts(k)],
                                    rhs=xn16[:, ts(k, TC)],
                                    start=(k == 0), stop=(k == CT - 1)),
                                   inc=(k == CT - 1))
                            used(("wi", id(wib)), "tensor")
                            used("xn5", "tensor")
                            mmd = p.count(s_te)
                            if half == 0:
                                acquire(("xin",), "scalar")
                                sc(lambda e, sl=sl: e.activation(
                                    out=xin[:, 0:3],
                                    in_=ctail[:, sl * 4:sl * 4 + 3],
                                    func=AF.Copy))
                                p.wait("scalar", s_te, mmd)
                                sc(lambda e, b=b: e.activation(
                                    out=xin[:, 3:TC + 3], in_=pb[b][:],
                                    func=AF.Copy))
                                used(b, "scalar")
                                sc(lambda e, sl=sl: e.activation(
                                    out=ctail[:, sl * 4:sl * 4 + 3],
                                    in_=xin[:, TC:TC + 3], func=AF.Copy))
                                xin_done = p.count(s_sc)
                                p.wait("vector", s_sc, xin_done)
                                ve(lambda e, sl=sl: e.tensor_scalar(
                                    out=cv0[:], in0=xin[:, 0:TC],
                                    scalar1=sd_[:, sl * 4:sl * 4 + 1],
                                    scalar2=None, op0=OP.mult))
                                ve(lambda e, sl=sl: e.scalar_tensor_tensor(
                                    out=cv1[:], in0=xin[:, 1:TC + 1],
                                    scalar=sd_[:, sl * 4 + 1:sl * 4 + 2],
                                    in1=cv0[:], op0=OP.mult, op1=OP.add))
                                ve(lambda e, sl=sl: e.scalar_tensor_tensor(
                                    out=cv0[:], in0=xin[:, 2:TC + 2],
                                    scalar=sd_[:, sl * 4 + 2:sl * 4 + 3],
                                    in1=cv1[:], op0=OP.mult, op1=OP.add))
                                ve(lambda e, sl=sl: e.scalar_tensor_tensor(
                                    out=cv1[:], in0=xin[:, 3:TC + 3],
                                    scalar=sd_[:, sl * 4 + 3:sl * 4 + 4],
                                    in1=cv0[:], op0=OP.mult, op1=OP.add))
                                used(("xin",), "vector")
                                cvd = p.count(s_v)
                                p.wait("scalar", s_v, cvd)
                                sc(lambda e, sl=sl: e.activation(
                                    out=xcv[:, ts(sl, TC)], in_=cv1[:],
                                    func=AF.Silu,
                                    bias=sd_[:, 64 + sl:64 + sl + 1]))
                                xcvd = p.count(s_sc)
                                p.wait("tensor", s_sc, xcvd)
                                te(lambda e, sl=sl: e.matmul(
                                    out=pb[2][0:96, :],
                                    lhsT=wx16[:, sl * 96:(sl + 1) * 96],
                                    rhs=xcv[:, ts(sl, TC)],
                                    start=(sl == 0), stop=(sl == DSL - 1)),
                                   inc=(sl == DSL - 1))
                            else:
                                p.wait("scalar", s_te, mmd)
                                sc(lambda e, sl=sl, b=b: e.activation(
                                    out=szb[:, ts(sl, TC)], in_=pb[b][:],
                                    func=AF.Silu))
                                used(b, "scalar")
                        if half == 0:
                            xd_mm = p.count(s_te)
                            p.wait("scalar", s_te, xd_mm)
                            sc(lambda e: e.activation(out=xdst[:], in_=pb[2][0:96, :],
                                                      func=AF.Copy))
                            used(2, "scalar")
                            xd_done = p.count(s_sc)
                            p.wait("gpsimd", s_sc, xd_done)
                            dma("gpsimd", dr["s_xdbl"][ch], xdst[:])
                            dma("gpsimd", dr["s_xcv"][ch], xcv[:])
                            used(("xcvbuf",), "gpsimd")
                    used(("lnrows", id(mb)), "vector")
                    szdone = p.count(s_sc)
                    p.wait("gpsimd", s_sc, szdone)
                    dma("gpsimd", dr["s_sz"][ch], szb[:])
                    gcount = p.count(s_g)
                    p.wait("scalar", s_g, gcount)
                barrier()
            if sub < 2:
                return
            for ct in range(CT):
                dma("sync", dr["s_xsp"][ct], x_res[:, ts(ct, T)])
            barrier()
            if sub < 3:
                return
            p.serial = bool(P5SER & 2) or bool(SERALL)
            with ExitStack() as sco:
                yf16 = sb("p5yf", [128, DSL * TC], F16, c=sco)
                bbcb = sb("p5bb", [128, 2 * NS * TC], F16, c=sco)
                xdbl = [sb("p5cx0", [96, TC], F16, c=sco)] * 2
                dt32 = [sb("p5dt0", [128, TC], c=sco)] * 2
                dtx = sb("p5dtx", [128, TC], F16, c=sco)
                da32 = [sb(f"p5da{i}", [128, TC], c=sco) for i in range(2)]
                u16 = sb("p5u", [128, TC], F16, c=sco)
                s16 = [sb(f"p5s{i}", [128, TC], F16, c=sco) for i in range(2)]
                tmp16 = u16
                yy = sb("p5y0", [128, TC], c=sco)[:]
                xcvs = [sb("p5xv0", [128, TC], F16, c=sco)] * 2
                szs = [sb("p5zs0", [128, TC], F16, c=sco)] * 2
                woc = [sb(f"p5wo{i}", [128, DI // 4], F16, c=sco) for i in range(2)]
                for ch in range(NCH):
                    xdb = xdbl[ch % 2]
                    acquire(("xdbl", id(xdb)), "gpsimd")
                    xdd = dma("gpsimd", xdb[:], dr["s_xdbl"][ch])
                    p.wait("gpsimd", s_g, xdd)
                    acquire(("bbcb",), "scalar")
                    for kind in range(2):
                        acquire(("yf",), "gpsimd")
                        rw = p.emit("gpsimd", lambda e, k=kind, xdb=xdb:
                                    e.dma_start(
                                        out=yf16[0:1, 0:NS * TC],
                                        in_=xdb[64 + k * NS:64 + (k + 1) * NS, :],
                                        allow_slow_non_contiguous=True),
                                    inc=s_g, n=16)
                        p.wait("tensor", s_g, rw)
                        for n in range(NS):
                            b = 4 + (n % 2)
                            slot = n * 2 + kind
                            acquire(b, "tensor")
                            te(lambda e, n=n, b=b: e.matmul(
                                out=pb[b][:], lhsT=ones16[0:1, :],
                                rhs=yf16[0:1, n * TC:(n + 1) * TC],
                                start=True, stop=True))
                            bd = p.count(s_te)
                            p.wait("scalar", s_te, bd)
                            sc(lambda e, slot=slot, b=b: e.activation(
                                out=bbcb[:, slot * TC:(slot + 1) * TC],
                                in_=pb[b][:], func=AF.Copy))
                            used(b, "scalar")
                        used(("yf",), "tensor")
                    used(("xdbl", id(xdb)), "gpsimd")
                    bb_done = p.count(s_sc)
                    p.wait("vector", s_sc, bb_done)
                    acquire(("yf",), "vector")
                    for sl in range(DSL):
                        dtb = dt32[sl % 2]
                        acquire(0, "tensor")
                        te(lambda e, sl=sl, xdb=xdb: e.matmul(
                            out=pb[0][:], lhsT=wdt16[:, ts(sl)],
                            rhs=xdb[0:RK, :], start=True, stop=True))
                        dtd = p.count(s_te)
                        acquire(("dt", id(dtb)), "scalar")
                        p.wait("scalar", s_te, dtd)
                        sc(lambda e, sl=sl, dtb=dtb: e.activation(
                            out=dtb[:], in_=pb[0][:], func=AF.Exp,
                            bias=sd_[:, 96 + sl:96 + sl + 1]))
                        used(0, "scalar")
                        sc(lambda e, dtb=dtb: e.activation(
                            out=dtb[:], in_=dtb[:], func=AF.Ln, bias=1.0))
                        dt_done = p.count(s_sc)
                        xcb = xcvs[sl % 2]; szc = szs[sl % 2]
                        acquire(("xcv", id(xcb)), "gpsimd")
                        xcd = dma("gpsimd", xcb[:],
                                  dr["s_xcv"][ch][:, ts(sl, TC)])
                        acquire(("szs", id(szc)), "gpsimd")
                        szd = dma("gpsimd", szc[:],
                                  dr["s_sz"][ch][:, ts(sl, TC)])
                        p.wait("vector", s_sc, dt_done)
                        p.wait("vector", s_g, xcd)
                        acquire(("dtx",), "vector")
                        ve(lambda e, dtb=dtb, xcb=xcb: e.tensor_tensor(
                            out=dtx[:], in0=dtb[:], in1=xcb[:], op=OP.mult))
                        for n in range(NS):
                            dab = da32[n % 2]
                            acquire(("da", id(dab)), "scalar")
                            sc(lambda e, sl=sl, n=n, dab=dab, dtb=dtb:
                               e.activation(out=dab[:], in_=dtb[:], func=AF.Exp,
                                            scale=aneg[:, sl * NS + n:
                                                       sl * NS + n + 1]))
                            dad = p.count(s_sc)
                            ve(lambda e, n=n: e.tensor_tensor(
                                out=u16[:], in0=dtx[:],
                                in1=bbcb[:, (n * 2) * TC:(n * 2 + 1) * TC],
                                op=OP.mult))
                            p.wait("vector", s_sc, dad)
                            sb_ = s16[n % 2]
                            acquire(("s16", id(sb_)), "vector")
                            ve(lambda e, sl=sl, n=n, sb_=sb_, dab=dab:
                               e.tensor_tensor_scan(
                                   out=sb_[:], data0=dab[:], data1=u16[:],
                                   initial=stS[:, sl * NS + n:sl * NS + n + 1],
                                   op0=OP.mult, op1=OP.add))
                            used(("da", id(dab)), "vector")
                            scn = p.count(s_v)
                            p.wait("scalar", s_v, scn)
                            sc(lambda e, sl=sl, n=n, sb_=sb_: e.activation(
                                out=stS[:, sl * NS + n:sl * NS + n + 1],
                                in_=sb_[:, TC - 1:TC], func=AF.Copy))
                            used(("s16", id(sb_)), "scalar")
                            ve(lambda e, n=n, sb_=sb_: e.tensor_tensor(
                                out=tmp16[:], in0=sb_[:],
                                in1=bbcb[:, (n * 2 + 1) * TC:(n * 2 + 2) * TC],
                                op=OP.mult))
                            used(("s16", id(sb_)), "vector")
                            if n == 0:
                                ve(lambda e: e.tensor_copy(out=yy, in_=tmp16[:]))
                            else:
                                ve(lambda e: e.tensor_tensor(
                                    out=yy, in0=tmp16[:], in1=yy, op=OP.add))
                        used(("dt", id(dtb)), "vector")
                        used(("dtx",), "vector")
                        if (DEBUG == "p5c" and ch == 0 and sl == 0
                                and not dbg_done):
                            dbg_done.append(1)
                            dv = p.count(s_v)
                            p.wait("scalar", s_v, dv)
                            dbg_f32(dtb[:], 0, dbgst[:])
                            dbg_f32(dtx[:], 512, dbgst[:])
                            dbg_f32(s16[(NS - 1) % 2][:], 1024, dbgst[:])
                            dbg_f32(yy, 1536, dbgst[:])
                            dbg_f32(bbcb[:, 0:512], 2048, dbgst[:])
                            dbg_f32(bbcb[:, 512:1024], 2560, dbgst[:])
                            dbg_f32(xcvs[0][:], 3072, dbgst[:])
                            dbg_f32(xdbl[0][0:64, 0:512], 3584, dbgst[:])
                        p.wait("vector", s_g, szd)
                        ve(lambda e, szc=szc: e.tensor_tensor(
                            out=yy, in0=yy, in1=szc[:], op=OP.mult))
                        used(("szs", id(szc)), "vector")
                        ve(lambda e, sl=sl, xcb=xcb: e.scalar_tensor_tensor(
                            out=yf16[:, ts(sl, TC)], in0=xcb[:],
                            scalar=sd_[:, 80 + sl:80 + sl + 1], in1=yy,
                            op0=OP.mult, op1=OP.add))
                        used(("xcv", id(xcb)), "vector")
                    used(("bbcb",), "vector")
                    yf_done = p.count(s_v)
                    p.wait("tensor", s_v, yf_done)
                    for co in range(CT):
                        b = 1 + (co % 2)
                        acquire(b, "tensor")
                        for dh in range(4):
                            wob = woc[dh % 2]
                            acquire(("woc", id(wob)), "sync")
                            wod = dma("sync", wob[:],
                                      dr["s_wout"][co][:, dh * 512:(dh + 1) * 512])
                            p.wait("tensor", s_w, wod)
                            for j in range(4):
                                sl = dh * 4 + j
                                te(lambda e, sl=sl, j=j, b=b, wob=wob, dh=dh:
                                   e.matmul(
                                    out=pb[b][:], lhsT=wob[:, ts(j)],
                                    rhs=yf16[:, ts(sl, TC)],
                                    start=(sl == 0), stop=(sl == DSL - 1)),
                                   inc=(j == 3))
                            used(("woc", id(wob)), "tensor")
                        mmd = p.count(s_te)
                        p.wait("scalar", s_te, mmd)
                        xs = x_res[:, ts(co, T)][:, ts(ch, TC)]
                        sc(lambda e, b=b, xs=xs: e.activation(
                            out=xs, in_=pb[b][:], func=AF.Copy))
                        used(b, "scalar")
                    used(("yf",), "tensor")
                    used(("xdbl", id(xdb)), "tensor")
                barrier()
            if sub < 4:
                return
            p.serial = bool(SERALL)
            with ExitStack() as sco:
                bnc = [sb(f"p5bn{i}", [128, T], c=sco) for i in range(2)]
                for ct in range(CT):
                    bb = bnc[ct % 2]
                    acquire(("bnc", id(bb)), "gpsimd")
                    bd = dma("gpsimd", bb[:], dr["s_xsp"][ct])
                    p.wait("vector", s_g, bd)
                    ve(lambda e, ct=ct, bb=bb: e.tensor_tensor(
                        out=x_res[:, ts(ct, T)], in0=bb[:],
                        in1=x_res[:, ts(ct, T)], op=OP.add))
                    used(("bnc", id(bb)), "vector")
                barrier()

        if PHASES > 5:
            mamba_phase(P5SUB)
            p.serial = bool(SERALL)
        if PHASES > 6:
            moe_phase("p6")

        for ct in range(CT):
            dma("sync", out_ap[ct], x_res[:, ts(ct, T)])
        p.wait("sync", s_w)
        barrier()
        p.run(block)
    return nc


_NC_CACHE = {}


def _tile8(w):
    a = w.shape[0] // 128
    b = w.shape[1] // 128
    return np.ascontiguousarray(
        w.reshape(a, 128, b, 128).transpose(2, 1, 0, 3).reshape(b, 128, a * 128))


def kernel(L0_attn_x, L0_attn_w_k, L0_attn_w_v, L0_attn_w_r, L0_attn_w_o,
           L0_attn_time_decay, L0_attn_time_first, L0_attn_time_mix_key,
           L0_attn_time_mix_value, L0_attn_time_mix_receptance,
           L0_ffn_w_freq_real, L0_ffn_w_freq_imag, L0_ffn_bias,
           L1_attn_W_in, L1_attn_conv_w, L1_attn_conv_b, L1_attn_W_x,
           L1_attn_W_dt, L1_attn_b_dt, L1_attn_A_log, L1_attn_D,
           L1_attn_W_out, L1_ffn_Wg, L1_ffn_bg, L1_ffn_We, L1_ffn_be):
    f32 = lambda v: np.ascontiguousarray(np.asarray(v, dtype=np.float32))
    x = f32(L0_attn_x)
    B = x.shape[0]

    n_, d_ = np.meshgrid(np.arange(C), np.arange(C), indexing="ij")
    ang = (-2.0 * np.pi / C) * (n_ * d_ % C)
    fr16 = np.cos(ang).astype(np.float16).reshape(8, 128, C)
    fi16 = np.sin(ang).astype(np.float16).reshape(8, 128, C)
    ident16 = np.eye(128, dtype=np.float16)

    def colpack(v, k):
        return np.ascontiguousarray(f32(v).reshape(k, 128).T)

    smallv = np.zeros((128, 48), np.float32)
    smallv[:, 0:8] = colpack(L0_attn_time_decay, 8)
    smallv[:, 8:16] = colpack(L0_attn_time_first, 8)
    smallv[:, 16:24] = colpack(L0_ffn_bias, 8)
    smallv[:, 24:32] = colpack(np.asarray(L0_attn_time_mix_key).reshape(-1), 8)
    smallv[:, 32:40] = colpack(np.asarray(L0_attn_time_mix_value).reshape(-1), 8)
    smallv[:, 40:48] = colpack(np.asarray(L0_attn_time_mix_receptance).reshape(-1), 8)

    smalld = np.zeros((128, 368), np.float32)
    cw = f32(L1_attn_conv_w).reshape(DI, 4)
    smalld[:, 0:64] = cw.reshape(DSL, 128, 4).transpose(1, 0, 2).reshape(128, 64)
    smalld[:, 64:80] = colpack(L1_attn_conv_b, DSL)
    smalld[:, 80:96] = colpack(L1_attn_D, DSL)
    smalld[:, 96:112] = colpack(L1_attn_b_dt, DSL)
    al = f32(L1_attn_A_log)
    smalld[:, 112:112 + DSL * NS] = (
        al.reshape(DSL, 128, NS).transpose(1, 0, 2).reshape(128, DSL * NS))

    common = {
        "fr16": fr16, "fi16": fi16, "ident16": ident16,
        "wfr": f32(L0_ffn_w_freq_real).reshape(8, 128, C),
        "wfi": f32(L0_ffn_w_freq_imag).reshape(8, 128, C),
        "wkT_t": _tile8(f32(L0_attn_w_k).T),
        "wvT_t": _tile8(f32(L0_attn_w_v).T),
        "wrT_t": _tile8(f32(L0_attn_w_r).T),
        "woT_t": _tile8(f32(L0_attn_w_o).T),
        "winT_t": _tile8(f32(L1_attn_W_in).T),
        "woutT_t": _tile8(f32(L1_attn_W_out).T),
        "wxT_t": np.ascontiguousarray(
            f32(L1_attn_W_x).T.reshape(DSL, 128, 96).transpose(1, 0, 2)
            .reshape(128, DSL * 96)),
        "wdtT_t": f32(L1_attn_W_dt).T,
        "wg_t": f32(L1_ffn_Wg).reshape(8, 128, C),
        "we_t": f32(L1_ffn_We).reshape(E, 8, 128, C),
        "smallv": smallv, "smalld": smalld,
        "be": f32(L1_ffn_be), "bgrow": f32(L1_ffn_bg).reshape(1, C),
    }

    key = "prog"
    if key not in _NC_CACHE:
        _NC_CACHE[key] = build_program()
    nc = _NC_CACHE[key]

    in_maps = []
    for b in range(B):
        m = dict(common)
        m["xT"] = np.ascontiguousarray(x[b].T).reshape(CT, 128, T)
        in_maps.append(m)
    res = run_bass_kernel_spmd(nc, in_maps, core_ids=list(range(B)))
    out = np.empty((B, T, C), np.float32)
    for b in range(B):
        o = res.results[b]["out"].reshape(C, T)
        out[b] = o.T
    return out




# version 09
# speedup vs baseline: 1.6899x; 1.0022x over previous
import numpy as np
import concourse.bass as bass
import concourse.mybir as mybir
from concourse.bass_utils import run_bass_kernel_spmd
from contextlib import ExitStack

F32 = mybir.dt.float32
F16 = mybir.dt.float16
U32 = mybir.dt.uint32
AF = mybir.ActivationFunctionType
OP = mybir.AluOpType

ENGINES = ("sync", "scalar", "vector", "tensor", "gpsimd")
C, CT, T, TC = 1024, 8, 4096, 512
NCH = T // TC
DI, DSL, NS, RK, E = 2048, 16, 16, 64, 8
PHASES = 7
DEBUG = ""
P5SUB = 4
SKIPS = ()
MOESUB = 3
MOEBAR = 0
MOESER = 3
MOESEG = 15
P5SER = 0
SERALL = 0


def ts(i, n=128):
    return slice(i * n, (i + 1) * n)


class Prog:
    def __init__(self):
        self.q = {e: [] for e in ENGINES}
        self.led = {}
        self.serial = False
        self.sems = None

    def emit(self, eng, fn, inc=None, n=1):
        if self.serial and self.sems is not None:
            for oe, osem in self.sems.items():
                if oe != eng:
                    self.wait(eng, osem)
        if inc is not None:
            self.led[id(inc)] = self.led.get(id(inc), 0) + n
            def wrapped(e, fn=fn, inc=inc, n=n):
                fn(e).then_inc(inc, n)
        else:
            def wrapped(e, fn=fn):
                fn(e)
        self.q[eng].append(wrapped)
        return self.led.get(id(inc), 0) if inc is not None else None

    def count(self, sem):
        return self.led.get(id(sem), 0)

    def wait(self, eng, sem, val=None):
        v = self.count(sem) if val is None else val
        if v > 0:
            self.q[eng].append(lambda e, sem=sem, v=v: e.wait_ge(sem, v))

    def run(self, block):
        for e in ENGINES:
            def make(e):
                def _(eng):
                    for fn in self.q[e]:
                        fn(eng)
                return _
            getattr(block, e)(make(e))


def build_program():
    nc = bass.Bass("TRN2", target_bir_lowering=False)
    dr = {}

    def din(name, shape, dtype=F32):
        dr[name] = nc.dram_tensor(name, shape, dtype, kind="ExternalInput").ap()

    def dscr(name, shape, dtype=F16):
        dr[name] = nc.dram_tensor(name, shape, dtype, kind="Internal").ap()

    din("xT", [CT, 128, T])
    din("fr16", [8, 128, C], F16); din("fi16", [8, 128, C], F16)
    din("ident16", [128, 128], F16)
    din("wfr", [8, 128, C]); din("wfi", [8, 128, C])
    din("wkT_t", [8, 128, C]); din("wvT_t", [8, 128, C]); din("wrT_t", [8, 128, C])
    din("woT_t", [8, 128, C])
    din("winT_t", [32, 128, C])
    din("woutT_t", [8, 128, DI])
    din("wxT_t", [128, DSL * 96])
    din("wdtT_t", [RK, DI])
    din("wg_t", [8, 128, C])
    din("we_t", [E, 8, 128, C])
    din("smallv", [128, 48])
    din("smalld", [128, 368])
    din("be", [8, C])
    din("bgrow", [1, C])
    dscr("s_m", [8, 128, C])
    dscr("s_kv", [6, 8, 128, C])
    dscr("s_wo", [8, 128, C])
    dscr("s_win", [32, 128, C])
    dscr("s_wout", [8, 128, DI])
    dscr("s_wg", [8, 128, C])
    dscr("s_bg", [1, C])
    dscr("s_we", [E, 2, 128, 8 * 512])
    dscr("s_xcv", [NCH, 128, DSL * TC])
    dscr("s_sz", [NCH, 128, DSL * TC])
    dscr("s_xdbl", [NCH, 96, TC])
    dscr("s_xsp", [CT, 128, T], F32)
    out_ap = nc.dram_tensor("out", [CT, 128, T], F32, kind="ExternalOutput").ap()
    dbg_ap = nc.dram_tensor("dbg", [128, 8192], F32, kind="ExternalOutput").ap()

    ctx = ExitStack()
    with ctx:
        def sb(name, shape, dtype=F32, c=None):
            return (c or ctx).enter_context(nc.sbuf_tensor(name, shape, dtype))

        ident = sb("ident", [128, 128], F16)
        ones16 = sb("ones16", [128, 128], F16)
        sv = sb("sv", [128, 72])
        lam8 = sb("lam8", [128, 8]); etf8 = sb("etf8", [128, 8])
        bias05 = sb("bias05", [128, 8])
        sd_ = sb("sd_", [128, 368])
        aneg = sb("aneg", [128, DSL * NS])
        be16 = sb("be16", [33, C], F16)
        bg16 = be16[32:33, :]
        wx16 = sb("wx16", [128, DSL * 96], F16)
        wdt16 = sb("wdt16", [RK, DI], F16)
        cpool = sb("cpool", [128, 4])
        ident32 = sb("ident32", [128, 128])
        nc.const_aps.register(F32, 1e-5, cpool[:, 0:1])
        nc.const_aps.register(F32, 1.0, cpool[:, 1:2])
        stA = sb("stA", [128, 8]); stB = sb("stB", [128, 8])
        stS = sb("stS", [128, DSL * NS])
        ctail = sb("ctail", [128, DSL * 4])
        xsh0 = sb("xsh0", [128, 8], F16)

        pb = [ctx.enter_context(nc.psum_tensor(f"pb{i}", [128, 512], F32))
              for i in range(6)]
        pbig = ctx.enter_context(nc.psum_tensor("pbig", [128, 1024], F32))

        sems = {e: ctx.enter_context(nc.semaphore(f"bar_{e}")) for e in ENGINES}
        s_w = ctx.enter_context(nc.semaphore("s_w"))
        s_g = ctx.enter_context(nc.semaphore("s_g"))
        s_te = ctx.enter_context(nc.semaphore("s_te"))
        s_sc = ctx.enter_context(nc.semaphore("s_sc"))
        s_v = ctx.enter_context(nc.semaphore("s_v"))
        block = ctx.enter_context(nc.Block())

        p = Prog()
        SEM = {"tensor": s_te, "scalar": s_sc, "vector": s_v,
               "sync": s_w, "gpsimd": s_g}
        p.sems = SEM
        if SERALL:
            p.serial = True

        def dma(queue, out, in_):
            return p.emit(queue, lambda e, o=out, i=in_: e.dma_start(out=o, in_=i),
                          inc=SEM[queue], n=16)

        def te(fn, inc=True):
            return p.emit("tensor", fn, inc=s_te if inc else None)

        def sc(fn):
            return p.emit("scalar", fn, inc=s_sc)

        def ve(fn):
            return p.emit("vector", fn, inc=s_v)

        def vfence():
            p.wait("vector", s_v, p.count(s_v))

        def barrier():
            p.wait("sync", s_w)
            p.wait("gpsimd", s_g)
            for e in ENGINES:
                p.emit(e, lambda eng: eng.nop(), inc=sems[e], n=1)
            tgt = {e: p.count(sems[e]) for e in ENGINES}
            for e in ENGINES:
                for o in ENGINES:
                    if o != e:
                        p.wait(e, sems[o], tgt[o])

        dbg_done = []
        dbgst = sb("dbgst", [128, 96]) if DEBUG else None
        guards = {}

        def acquire(res, producer_eng):
            for sem, cnt in guards.pop(res, ()):
                p.wait(producer_eng, sem, cnt)

        def used(res, eng):
            guards.setdefault(res, []).append((SEM[eng], p.count(SEM[eng])))

        def dbg_f32(ap, col0, stage):
            n = ap.shape[1]
            sw = stage.shape[1]
            for o in range(0, n, sw):
                w = min(sw, n - o)
                np_ = ap.shape[0]
                sc(lambda e, ap=ap, o=o, w=w, np_=np_: e.activation(
                    out=stage[0:np_, 0:w], in_=ap[:, o:o + w], func=AF.Copy))
                dn = p.count(s_sc)
                p.wait("sync", s_sc, dn)
                p.emit("sync", lambda e, o=o, w=w, np_=np_: e.dma_start(
                    out=dbg_ap[0:np_, col0 + o:col0 + o + w],
                    in_=stage[0:np_, 0:w],
                    allow_slow_non_contiguous=True), inc=s_w, n=16)
                wd = p.count(s_w)
                p.wait("scalar", s_w, wd)

        def ln_rows(ch, xf16, _unused, mb, rb, t1, var, b0, b1):
            acquire(("lnxf", id(xf16)), "scalar")
            acquire(("lnrows", id(mb)), "scalar")
            for ct in range(CT):
                xs = x_res[:, ts(ct, T)][:, ts(ch, TC)]
                sc(lambda e, xs=xs, ct=ct: e.activation(
                    out=xf16[:, ts(ct, TC)], in_=xs, func=AF.Copy))
            cast_done = p.count(s_sc)
            if DEBUG == "ones" and not dbg_done:
                dbg_done.append(1)
                p.wait("scalar", s_sc, cast_done)
                dbg_f32(ones16[:], 0, mb)
                dbg_f32(xf16[:, 0:1024], 1024, mb)
                for ct in range(2):
                    dbg_f32(x_res[:, ts(ct, T)][:, 0:512], 2048 + ct * 512, mb)
            acquire(b0, "tensor"); acquire(b1, "tensor")
            p.wait("tensor", s_sc, cast_done)
            for ct in range(CT):
                te(lambda e, ct=ct: e.matmul(
                    out=pb[b0][:], lhsT=ones16[:], rhs=xf16[:, ts(ct, TC)],
                    start=(ct == 0), stop=(ct == CT - 1)), inc=(ct == CT - 1))
            s_done = p.count(s_te)
            if DEBUG == "psum" and not dbg_done:
                dbg_done.append(1)
                p.wait("scalar", s_te, s_done)
                dbg_f32(pb[b0][:], 0, mb)
            p.wait("scalar", s_te, s_done)
            for ct in range(CT):
                xs = x_res[:, ts(ct, T)][:, ts(ch, TC)]
                sc(lambda e, xs=xs, ct=ct: e.activation(
                    out=xf16[:, ts(ct, TC)], in_=xs, func=AF.Square))
            sqcast_done = p.count(s_sc)
            p.wait("tensor", s_sc, sqcast_done)
            for ct in range(CT):
                te(lambda e, ct=ct: e.matmul(
                    out=pb[b1][:], lhsT=ones16[:], rhs=xf16[:, ts(ct, TC)],
                    start=(ct == 0), stop=(ct == CT - 1)), inc=(ct == CT - 1))
            q_done = p.count(s_te)
            used(("lnxf", id(xf16)), "tensor")
            p.wait("scalar", s_te, s_done)
            sc(lambda e: e.activation(out=mb, in_=pb[b0][:], func=AF.Copy,
                                      scale=1.0 / 1024))
            used(b0, "scalar")
            if DEBUG == "mb" and not dbg_done:
                dbg_done.append(1)
                dbg_f32(mb, 0, var)

            sc(lambda e: e.activation(out=t1, in_=mb, func=AF.Square))
            sq_done = p.count(s_sc)
            p.wait("vector", s_te, q_done)
            p.wait("vector", s_sc, sq_done)
            ve(lambda e: e.scalar_tensor_tensor(
                out=var, in0=pb[b1][:], scalar=1.0 / 1024, in1=t1,
                op0=OP.mult, op1=OP.subtract))
            used(b1, "vector")
            var_done = p.count(s_v)
            p.wait("scalar", s_v, var_done)
            sc(lambda e: e.activation(out=t1, in_=var, func=AF.Sqrt, bias=1e-5))
            sqrt_done = p.count(s_sc)
            p.wait("vector", s_sc, sqrt_done)
            ve(lambda e: e.reciprocal(out=rb, in_=t1))
            ve(lambda e: e.tensor_tensor(out=var, in0=mb, in1=rb, op=OP.mult))
            used(("lnrows", id(mb)), "vector")
            if DEBUG == "ln" and ch == 0 and not dbg_done:
                dbg_done.append(1)
                dv = p.count(s_v)
                p.wait("scalar", s_v, dv)
                dbg_f32(mb, 0, t1)
                dbg_f32(rb, 512, t1)
                dbg_f32(var, 1024, t1)

        dma("sync", ident[:], dr["ident16"])
        dma("sync", sv[:, 0:48], dr["smallv"])
        dma("sync", sd_[:], dr["smalld"])
        sm_w = p.count(s_w)
        ve(lambda e: e.memset(ones16[:], 1.0))
        ve(lambda e: e.memset(cpool[:, 0:1], 1e-5))
        ve(lambda e: e.memset(cpool[:, 1:2], 1.0))
        ve(lambda e: e.memset(stA[:], 0.0))
        ve(lambda e: e.memset(stB[:], 0.0))
        ve(lambda e: e.memset(stS[:], 0.0))
        ve(lambda e: e.memset(ctail[:], 0.0))
        ve(lambda e: e.memset(xsh0[:], 0.0))
        p.wait("scalar", s_w, sm_w)
        sc(lambda e: e.activation(out=cpool[:, 2:3], in_=cpool[:, 0:1],
                                  func=AF.Copy, accum_out=cpool[:, 3:4]))
        sc(lambda e: e.activation(out=ident32[:], in_=ident[:], func=AF.Copy))
        sc(lambda e: e.activation(out=lam8[:], in_=sv[:, 0:8], func=AF.Exp))
        sc(lambda e: e.activation(out=lam8[:], in_=lam8[:], func=AF.Exp, scale=-1.0))
        sc(lambda e: e.activation(out=etf8[:], in_=sv[:, 8:16], func=AF.Exp))
        sc(lambda e: e.activation(out=bias05[:], in_=sv[:, 16:24], func=AF.Copy,
                                  scale=0.5))
        sc(lambda e: e.activation(out=aneg[:], in_=sd_[:, 112:112 + DSL * NS],
                                  func=AF.Exp))
        d0 = p.count(s_sc)
        p.wait("vector", s_sc, d0)
        ve(lambda e: e.tensor_scalar(out=aneg[:], in0=aneg[:], scalar1=-1.0,
                                     scalar2=None, op0=OP.mult))
        for j, src0 in ((0, 24), (1, 32), (2, 40)):
            ve(lambda e, j=j, s0=src0: e.tensor_scalar(
                out=sv[:, 48 + 8 * j:56 + 8 * j], in0=sv[:, s0:s0 + 8],
                scalar1=-1.0, scalar2=1.0, op0=OP.mult, op1=OP.add))
        smalls_v = p.count(s_v)

        with ExitStack() as prep:
            stg = sb("stg", [128, 4096], F32, c=prep)
            stg16 = sb("stg16", [128, 2048], F16, c=prep)
            fr = sb("fr", [128, 8 * C], F16, c=prep)
            fi = sb("fi", [128, 8 * C], F16, c=prep)
            w16a = sb("w16a", [128, 8 * C], F16, c=prep)
            w16b = sb("w16b", [128, 8 * C], F16, c=prep)
            w16c = sb("w16c", [128, 8 * C], F16, c=prep)
            ptre = sb("ptre", [128, 8 * C], F16, c=prep)
            ptim = sb("ptim", [128, 8 * C], F16, c=prep)

            dma("sync", stg[0:8, 0:1024], dr["be"])
            dma("sync", stg[0:1, 1024:2048], dr["bgrow"])
            dma("sync", stg[:, 2048:2048 + DSL * 96], dr["wxT_t"])
            w0 = p.count(s_w)
            p.wait("scalar", s_w, w0)
            p.wait("scalar", s_v, smalls_v)
            sc(lambda e: e.activation(out=be16[0:8, :], in_=stg[0:8, 0:1024],
                                      func=AF.Copy))
            sc(lambda e: e.activation(out=bg16, in_=stg[0:1, 1024:2048],
                                      func=AF.Copy))
            bgc = p.count(s_sc)
            p.wait("sync", s_sc, bgc)
            dma("sync", dr["s_bg"], bg16)
            sc(lambda e: e.activation(out=wx16[:], in_=stg[:, 2048:2048 + DSL * 96],
                                      func=AF.Copy))
            used(("stg",), "scalar")
            acquire(("stg",), "sync")
            w0b = dma("sync", stg[0:RK, 0:DI], dr["wdtT_t"])
            p.wait("scalar", s_w, w0b)
            sc(lambda e: e.activation(
                out=wdt16[:], in_=stg[0:RK, 0:DI], func=AF.Copy))
            used(("stg",), "scalar")

            for ct in range(CT):
                dma("sync", fr[:, ts(ct, C)], dr["fr16"][ct])
                dma("sync", fi[:, ts(ct, C)], dr["fi16"][ct])
            fw = p.count(s_w)

            def cast_sq(dst16, src_t, scale=1.0, extra=None, extrascale=None):
                for g in range(2):
                    wdone = None
                    acquire(("stg",), "sync")
                    for j in range(4):
                        wdone = dma("sync", stg[:, ts(j, C)], src_t[g * 4 + j])
                    p.wait("scalar", s_w, wdone)
                    for j in range(4):
                        ctt = g * 4 + j
                        sc(lambda e, j=j, ctt=ctt: e.activation(
                            out=dst16[:, ts(ctt, C)], in_=stg[:, ts(j, C)],
                            func=AF.Copy, scale=scale))
                        if extra is not None:
                            sc(lambda e, j=j, ctt=ctt: e.activation(
                                out=extra[:, ts(ctt, C)], in_=stg[:, ts(j, C)],
                                func=AF.Copy, scale=extrascale))
                    used(("stg",), "scalar")

            cast_sq(w16a, dr["wfr"])
            cast_sq(w16b, dr["wfi"], extra=w16c, extrascale=-1.0)
            casts_sc = p.count(s_sc)
            p.wait("tensor", s_w, fw)
            p.wait("tensor", s_sc, casts_sc)

            def big_mm(dst16, lhs1, rhs1, lhs2, rhs2, scale=1.0):
                for och in range(8):
                    for half in range(2):
                        b = half
                        acquire(b, "tensor")
                        for k in range(8):
                            te(lambda e, o=och, h=half, k=k, b=b, l=lhs1, r=rhs1:
                               e.matmul(out=pb[b][:],
                                        lhsT=l[:, ts(k, C)][:, ts(o)],
                                        rhs=r[:, ts(k, C)][:, ts(h, 512)],
                                        start=(k == 0), stop=False), inc=False)
                        for k in range(8):
                            te(lambda e, o=och, h=half, k=k, b=b, l=lhs2, r=rhs2:
                               e.matmul(out=pb[b][:],
                                        lhsT=l[:, ts(k, C)][:, ts(o)],
                                        rhs=r[:, ts(k, C)][:, ts(h, 512)],
                                        start=False, stop=(k == 7)),
                               inc=(k == 7))
                        mmd = p.count(s_te)
                        p.wait("scalar", s_te, mmd)
                        sc(lambda e, o=och, h=half, b=b, d=dst16, sca=scale:
                           e.activation(out=d[:, ts(o, C)][:, ts(h, 512)],
                                        in_=pb[b][:], func=AF.Copy, scale=sca))
                        used(b, "scalar")

            big_mm(ptre, w16a, fr, w16c, fi)
            big_mm(ptim, w16a, fi, w16b, fr)
            m16 = w16c
            big_mm(m16, ptre, fr, ptim, fi, scale=0.5 / 1024)
            mdone = p.count(s_sc)
            p.wait("sync", s_sc, mdone)
            for ct in range(CT):
                dma("sync", dr["s_m"][ct], m16[:, ts(ct, C)])

            fold_plan = [("wkT_t", 0, 24, 0), ("wkT_t", 1, 48, 0),
                         ("wvT_t", 2, 32, 0), ("wvT_t", 3, 56, 0),
                         ("wrT_t", 4, 40, 0), ("wrT_t", 5, 64, 0)]
            for src, vi, col0, _ in fold_plan:
                for sl in range(8):
                    acquire(("stg",), "sync")
                    wdone = dma("sync", stg[:, 0:1024], dr[src][sl])
                    p.wait("scalar", s_w, wdone)
                    for ct in range(CT):
                        sc(lambda e, ct=ct, c0=col0: e.activation(
                            out=stg16[:, ts(ct)], in_=stg[:, ts(ct)],
                            func=AF.Copy, scale=sv[:, c0 + ct:c0 + ct + 1]))
                    used(("stg",), "scalar")
                    dn = p.count(s_sc)
                    p.wait("sync", s_sc, dn)
                    dma("sync", dr["s_kv"][vi][sl], stg16[:, 0:1024])
                    used(("stg16",), "sync")
                    acquire(("stg16",), "scalar")

            def plain_cast(src_t, dst_t, n_t, rows=128, cols=1024, scale=1.0):
                for s in range(n_t):
                    acquire(("stg",), "sync")
                    wdone = dma("sync", stg[0:rows, 0:cols], src_t[s])
                    p.wait("scalar", s_w, wdone)
                    sc(lambda e, r=rows, cl=cols, sca=scale: e.activation(
                        out=stg16[0:r, 0:cl], in_=stg[0:r, 0:cl], func=AF.Copy,
                        scale=sca))
                    used(("stg",), "scalar")
                    dn = p.count(s_sc)
                    p.wait("sync", s_sc, dn)
                    dma("sync", dst_t[s], stg16[0:rows, 0:cols])
                    used(("stg16",), "sync")
                    acquire(("stg16",), "scalar")

            plain_cast(dr["woT_t"], dr["s_wo"], 8, scale=0.5)
            plain_cast(dr["winT_t"], dr["s_win"], 32)
            plain_cast(dr["wg_t"], dr["s_wg"], 8)
            for ee in range(E):
                for ct in range(CT):
                    acquire(("stg",), "sync")
                    wdone = dma("sync", stg[:, 0:1024], dr["we_t"][ee][ct])
                    p.wait("scalar", s_w, wdone)
                    sc(lambda e: e.activation(out=stg16[:, 0:1024],
                                              in_=stg[:, 0:1024], func=AF.Copy))
                    used(("stg",), "scalar")
                    dn = p.count(s_sc)
                    p.wait("sync", s_sc, dn)
                    for fh in range(2):
                        dma("sync",
                            dr["s_we"][ee][fh][:, ct * 512:(ct + 1) * 512],
                            stg16[:, fh * 512:fh * 512 + 512])
                    used(("stg16",), "sync")
                    acquire(("stg16",), "scalar")
            for co in range(8):
                acquire(("stg",), "sync")
                wdone = dma("sync", stg[:, 0:2048], dr["woutT_t"][co])
                p.wait("scalar", s_w, wdone)
                sc(lambda e: e.activation(out=stg16[:, 0:2048],
                                          in_=stg[:, 0:2048], func=AF.Copy))
                used(("stg",), "scalar")
                dn = p.count(s_sc)
                p.wait("sync", s_sc, dn)
                dma("sync", dr["s_wout"][co], stg16[:, 0:2048])
                used(("stg16",), "sync")
                acquire(("stg16",), "scalar")
            barrier()

        x_res = sb("x_res", [128, CT * T])
        for ct in range(CT):
            dma("sync", x_res[:, ts(ct, T)], dr["xT"][ct])
        barrier()

        def ffn0_phase(tag):
            with ExitStack() as sco:
                m16 = sb(f"{tag}m", [128, 8 * C], F16, c=sco)
                xn16 = sb(f"{tag}xn", [128, CT * TC], F16, c=sco)
                xf16 = sb(f"{tag}xf", [128, CT * TC], F16, c=sco)
                mb = sb(f"{tag}mb", [128, TC], c=sco)[:]
                rb = sb(f"{tag}rb", [128, TC], c=sco)[:]
                t1 = sb(f"{tag}t1", [128, TC], c=sco)[:]
                var = sb(f"{tag}v", [128, TC], c=sco)[:]
                for ct in range(CT):
                    dma("sync", m16[:, ts(ct, C)], dr["s_m"][ct])
                mw = p.count(s_w)
                p.wait("tensor", s_w, mw)
                if DEBUG == "m16" and tag == "p1":
                    p.wait("scalar", s_w, mw)
                    for ct in range(CT):
                        dbg_f32(m16[:, ts(ct, C)], ct * 1024, mb)
                for ch in range(NCH):
                    ln_rows(ch, xf16, None, mb, rb, t1, var, 0, 1)
                    acquire(("xn", id(xn16)), "vector")
                    for ct in range(CT):
                        xs = x_res[:, ts(ct, T)][:, ts(ch, TC)]
                        ve(lambda e, xs=xs: e.tensor_tensor(
                            out=t1, in0=xs, in1=rb, op=OP.mult))
                        ve(lambda e, ct=ct: e.tensor_tensor(
                            out=xn16[:, ts(ct, TC)], in0=t1, in1=var,
                            op=OP.subtract))
                    app = p.count(s_v)
                    for co in range(CT):
                        b = 2 + (co % 2)
                        acquire(b, "tensor")
                        p.wait("tensor", s_v, app)
                        for k in range(CT):
                            te(lambda e, co=co, k=k, b=b: e.matmul(
                                out=pb[b][:], lhsT=m16[:, ts(k, C)][:, ts(co)],
                                rhs=xn16[:, ts(k, TC)],
                                start=(k == 0), stop=(k == CT - 1)),
                               inc=(k == CT - 1))
                        used(("xn", id(xn16)), "tensor")
                        mmd = p.count(s_te)
                        p.wait("vector", s_te, mmd)
                        xs = x_res[:, ts(co, T)][:, ts(ch, TC)]
                        ve(lambda e, b=b, xs=xs, co=co: e.scalar_tensor_tensor(
                            out=t1, in0=pb[b][:], scalar=bias05[:, co:co + 1],
                            in1=xs, op0=OP.add, op1=OP.add))
                        used(b, "vector")
                        ve(lambda e, xs=xs: e.tensor_copy(out=xs, in_=t1))
                    used(("lnrows", id(mb)), "vector")
                barrier()

        if PHASES > 1 and 1 not in SKIPS:
            ffn0_phase("p1")

        def rwkv_phase():
            with ExitStack() as sco:
                xn16 = sb("p2xn", [128, CT * (TC + 1)], F16, c=sco)
                xf16 = sb("p2xf", [128, CT * TC], F16, c=sco)
                mb = sb("p2mb", [128, TC], c=sco)[:]; rb = sb("p2rb", [128, TC], c=sco)[:]
                t1 = sb("p2t1", [128, TC], c=sco)[:]; var = sb("p2v", [128, TC], c=sco)[:]
                kv16 = [sb("p2kv0", [128, 6 * C], F16, c=sco)] * 2
                wo16 = [sb(f"p2wo{i}", [128, C], F16, c=sco) for i in range(2)]
                rk16 = sb("p2rk", [128, 8 * TC], F16, c=sco)
                ek = sb("p2ek", [128, TC], c=sco)[:]; uu = sb("p2u", [128, TC], c=sco)[:]
                As = sb("p2A", [128, TC + 1], c=sco)
                Bs = sb("p2B", [128, TC + 1], c=sco)
                num = sb("p2n", [128, TC], c=sco)[:]; den = sb("p2d", [128, TC], c=sco)[:]
                rec = sb("p2r", [128, TC], c=sco)[:]; th = sb("p2th", [128, TC], c=sco)[:]

                def xnsl(ct):
                    return xn16[:, ct * (TC + 1):(ct + 1) * (TC + 1)]

                for ch in range(NCH):
                    ln_rows(ch, xf16, None, mb, rb, t1, var, 0, 1)
                    acquire("xn2", "vector")
                    for ct in range(CT):
                        ve(lambda e, ct=ct: e.tensor_copy(
                            out=xnsl(ct)[:, 0:1], in_=xsh0[:, ct:ct + 1]))
                    for ct in range(CT):
                        xs = x_res[:, ts(ct, T)][:, ts(ch, TC)]
                        ve(lambda e, xs=xs: e.tensor_tensor(
                            out=t1, in0=xs, in1=rb, op=OP.mult))
                        ve(lambda e, ct=ct: e.tensor_tensor(
                            out=xnsl(ct)[:, 1:TC + 1], in0=t1, in1=var,
                            op=OP.subtract))
                    for ct in range(CT):
                        ve(lambda e, ct=ct: e.tensor_copy(
                            out=xsh0[:, ct:ct + 1], in_=xnsl(ct)[:, TC:TC + 1]))
                    xn_done = p.count(s_v)
                    used(("lnrows", id(mb)), "vector")
                    acquire("rk2", "vector")

                    for sl in range(8):
                        kvb = kv16[sl % 2]
                        acquire(("kv", id(kvb)), "sync")
                        wvd = None
                        for v6 in range(6):
                            wvd = dma("sync", kvb[:, ts(v6, C)],
                                      dr["s_kv"][v6][sl])
                        p.wait("tensor", s_w, wvd)
                        p.wait("tensor", s_v, xn_done)
                        for wi in range(3):
                            b = wi
                            acquire(b, "tensor")
                            for ct in range(CT):
                                te(lambda e, wi=wi, ct=ct, b=b, kvb=kvb: e.matmul(
                                    out=pb[b][:],
                                    lhsT=kvb[:, ts(2 * wi, C)][:, ts(ct)],
                                    rhs=xnsl(ct)[:, 1:TC + 1],
                                    start=(ct == 0), stop=False), inc=False)
                            for ct in range(CT):
                                te(lambda e, wi=wi, ct=ct, b=b, kvb=kvb: e.matmul(
                                    out=pb[b][:],
                                    lhsT=kvb[:, ts(2 * wi + 1, C)][:, ts(ct)],
                                    rhs=xnsl(ct)[:, 0:TC],
                                    start=False, stop=(ct == CT - 1)),
                                   inc=(ct == CT - 1))
                        used("xn2", "tensor")
                        used(("kv", id(kvb)), "tensor")
                        mmd = p.count(s_te)
                        p.wait("scalar", s_te, mmd - 2)
                        sc(lambda e: e.activation(out=ek, in_=pb[0][:], func=AF.Exp))
                        used(0, "scalar")
                        ek_done = p.count(s_sc)
                        p.wait("scalar", s_te, mmd)
                        sc(lambda e: e.activation(out=th, in_=pb[2][:],
                                                  func=AF.Tanh, scale=0.5))
                        used(2, "scalar")
                        th_done = p.count(s_sc)
                        p.wait("vector", s_sc, ek_done)
                        p.wait("vector", s_te, mmd - 1)
                        ve(lambda e: e.tensor_tensor(out=uu, in0=ek, in1=pb[1][:],
                                                     op=OP.mult))
                        used(1, "vector")
                        ve(lambda e, sl=sl: e.tensor_copy(
                            out=As[:, 0:1], in_=stA[:, sl:sl + 1]))
                        ve(lambda e, sl=sl: e.tensor_copy(
                            out=Bs[:, 0:1], in_=stB[:, sl:sl + 1]))
                        ve(lambda e, sl=sl: e.tensor_tensor_scan(
                            out=As[:, 1:TC + 1],
                            data0=lam8[:, sl:sl + 1].broadcast_to((128, TC)),
                            data1=uu, initial=As[:, 0:1],
                            op0=OP.mult, op1=OP.add))
                        ve(lambda e, sl=sl: e.tensor_tensor_scan(
                            out=Bs[:, 1:TC + 1],
                            data0=lam8[:, sl:sl + 1].broadcast_to((128, TC)),
                            data1=ek, initial=Bs[:, 0:1],
                            op0=OP.mult, op1=OP.add))
                        ve(lambda e, sl=sl: e.tensor_copy(
                            out=stA[:, sl:sl + 1], in_=As[:, TC:TC + 1]))
                        ve(lambda e, sl=sl: e.tensor_copy(
                            out=stB[:, sl:sl + 1], in_=Bs[:, TC:TC + 1]))
                        ve(lambda e, sl=sl: e.scalar_tensor_tensor(
                            out=num, in0=uu, scalar=etf8[:, sl:sl + 1],
                            in1=As[:, 0:TC], op0=OP.mult, op1=OP.add))
                        ve(lambda e, sl=sl: e.scalar_tensor_tensor(
                            out=den, in0=ek, scalar=etf8[:, sl:sl + 1],
                            in1=Bs[:, 0:TC], op0=OP.mult, op1=OP.add))
                        ve(lambda e: e.reciprocal(out=rec, in_=den))
                        ve(lambda e: e.tensor_tensor(out=num, in0=num, in1=rec,
                                                     op=OP.mult))
                        p.wait("vector", s_sc, th_done)
                        ve(lambda e, sl=sl: e.scalar_tensor_tensor(
                            out=rk16[:, ts(sl, TC)], in0=th, scalar=1.0,
                            in1=num, op0=OP.add, op1=OP.mult))
                        if DEBUG == "rwkv" and ch == 0 and sl == 0 and not dbg_done:
                            dbg_done.append(1)
                            dv = p.count(s_v)
                            p.wait("scalar", s_v, dv)
                            dbg_f32(ek, 0, dbgst[:])
                            dbg_f32(uu, 512, dbgst[:])
                            dbg_f32(As[:, 1:TC + 1], 1024, dbgst[:])
                            dbg_f32(Bs[:, 1:TC + 1], 1536, dbgst[:])
                            dbg_f32(num, 2048, dbgst[:])
                            dbg_f32(th, 2560, dbgst[:])
                            dbg_f32(rk16[:, 0:TC], 3072, dbgst[:])
                            dbg_f32(xn16[:, 0:TC + 1], 3584, dbgst[:])
                    rk_done = p.count(s_v)
                    p.wait("tensor", s_v, rk_done)
                    for co in range(CT):
                        wob = wo16[co % 2]
                        acquire(("p2wo", id(wob)), "sync")
                        wod = dma("sync", wob[:], dr["s_wo"][co])
                        b = 3 + (co % 2)
                        acquire(b, "tensor")
                        p.wait("tensor", s_w, wod)
                        for ds in range(8):
                            te(lambda e, co=co, ds=ds, b=b, wob=wob: e.matmul(
                                out=pb[b][:], lhsT=wob[:, ts(ds)],
                                rhs=rk16[:, ts(ds, TC)],
                                start=(ds == 0), stop=(ds == 7)), inc=(ds == 7))
                        used(("p2wo", id(wob)), "tensor")
                        mmd = p.count(s_te)
                        p.wait("vector", s_te, mmd)
                        xs = x_res[:, ts(co, T)][:, ts(ch, TC)]
                        ve(lambda e, b=b, xs=xs: e.tensor_tensor(
                            out=xs, in0=pb[b][:], in1=xs, op=OP.add))
                        used(b, "vector")
                    used("rk2", "tensor")
                    acquire("xn2", "vector")
                barrier()

        if PHASES > 2 and 2 not in SKIPS:
            rwkv_phase()
        if PHASES > 3 and 3 not in SKIPS:
            ffn0_phase("p3")

        def moe_phase(tag):
            p.serial = (MOESER == 1 or (MOESER == 2 and tag == "p6")
                        or bool(SERALL))

            def seg(bit):
                if MOESER == 3 and tag == "p6" and not SERALL:
                    p.serial = bool(MOESEG & bit)
            with ExitStack() as sco:
                xn16 = sb(f"{tag}xn", [128, CT * TC], F16, c=sco)
                xf16 = sb(f"{tag}xf", [128, CT * TC], F16, c=sco)
                mb = sb(f"{tag}mb", [128, TC], c=sco)[:]
                rb = sb(f"{tag}rb", [128, TC], c=sco)[:]
                t1 = sb(f"{tag}t1", [128, TC], c=sco)[:]
                var = sb(f"{tag}v", [128, TC], c=sco)[:]
                wu = sb(f"{tag}wu", [128, 8 * C], F16, c=sco)
                wg16 = wu
                we16 = [wu[:, 0:8 * 512], wu[:, 8 * 512:8 * C]]
                mx16 = [sb(f"{tag}mx0", [128, CT * TC], F16, c=sco)] * 2
                q32 = sb(f"{tag}q", [128, 1024], c=sco)
                mcol = sb(f"{tag}mc", [128, 4 * 8], c=sco)
                mrow = sb(f"{tag}mr", [8, 512], F16, c=sco)
                mb16 = sb(f"{tag}mb16", [128, 2 * TC], F16, c=sco)
                outT = sb(f"{tag}oT", [128, 4 * 512], c=sco)
                p05 = sb(f"{tag}p05", [128, 4], c=sco)
                mxc = sb(f"{tag}mxc", [128, 8], c=sco)
                eidxu = sb(f"{tag}eiu", [128, 1], U32, c=sco)
                eidxf = sb(f"{tag}eif", [128, 1], c=sco)
                sumc = sb(f"{tag}sum", [128, 2], c=sco)
                iota16 = sb(f"{tag}io", [128, 1024], F16, c=sco)
                p.emit("gpsimd", lambda e: e.iota(
                    out=iota16[:], pattern=[[1, 1024]], base=-2048,
                    channel_multiplier=0,
                    allow_small_or_imprecise_dtypes=True), inc=s_g, n=1)
                p.wait("vector", s_g, p.count(s_g))

                for ch in range(NCH):
                    seg(1)
                    acquire(("wu", id(we16[0])), "sync")
                    acquire(("wu", id(we16[1])), "sync")
                    for ct in range(CT):
                        dma("sync", wg16[:, ts(ct, C)], dr["s_wg"][ct])
                    wg_w = p.count(s_w)
                    p.wait("tensor", s_w, wg_w)
                    ln_rows(ch, xf16, None, mb, rb, t1, var, 0, 1)
                    acquire(("xn", id(xn16)), "vector")
                    for ct in range(CT):
                        xs = x_res[:, ts(ct, T)][:, ts(ch, TC)]
                        ve(lambda e, xs=xs: e.tensor_tensor(
                            out=t1, in0=xs, in1=rb, op=OP.mult))
                        ve(lambda e, ct=ct: e.tensor_tensor(
                            out=xn16[:, ts(ct, TC)], in0=t1, in1=var,
                            op=OP.subtract))
                    app = p.count(s_v)
                    used(("lnrows", id(mb)), "vector")
                    acquire(("lnxf", id(xf16)), "gpsimd")
                    bgw = dma("gpsimd", xf16[0:1, 0:1024], dr["s_bg"])
                    p.wait("tensor", s_g, bgw)
                    acquire(("mcol",), "vector")
                    acquire(("p05",), "scalar")
                    for su in range(4):
                        acquire("pbig", "tensor")
                        p.wait("tensor", s_v, app)
                        for half in range(2):
                            for k in range(CT):
                                te(lambda e, su=su, h=half, k=k: e.matmul(
                                    out=pbig[:, ts(h, 512)],
                                    lhsT=xn16[:, ts(k, TC)][:, ts(su)],
                                    rhs=wg16[:, ts(k, C)][:, ts(h, 512)],
                                    start=(k == 0), stop=False), inc=False)
                            te(lambda e, h=half: e.matmul(
                                out=pbig[:, ts(h, 512)],
                                lhsT=ones16[0:1, :],
                                rhs=xf16[0:1, ts(h, 512)],
                                start=False, stop=True), inc=True)
                        used(("wu", id(we16[0])), "tensor")
                        used(("wu", id(we16[1])), "tensor")
                        used(("xn", id(xn16)), "tensor")
                        used(("lnxf", id(xf16)), "tensor")
                        mmd = p.count(s_te)
                        if DEBUG == "moe2" and su == 0 and not dbg_done:
                            dbg_done.append(1)
                            p.wait("scalar", s_te, mmd)
                            p.wait("scalar", s_v, app)
                            dbg_f32(pbig[:], 0, dbgst[:])
                            dbg_f32(xn16[:, 0:1024], 1024, dbgst[:])
                            dbg_f32(wg16[:, 0:1024], 2048, dbgst[:])
                            p.wait("vector", s_sc, p.count(s_sc))
                        p.wait("vector", s_te, mmd)
                        for h in range(2):
                            ve(lambda e, h=h: e.reduce_max(
                                out=mxc[:, 2 + h:3 + h], in_=pbig[:, ts(h, 512)],
                                axis=mybir.AxisListType.X))
                        vfence()
                        ve(lambda e: e.tensor_tensor(
                            out=mxc[:, 0:1], in0=mxc[:, 2:3], in1=mxc[:, 3:4],
                            op=OP.max))
                        vfence()
                        acquire(("outT",), "vector")
                        dstab = outT[:, 0:1024]
                        ve(lambda e, dstab=dstab: e.tensor_tensor(
                            out=dstab, in0=pbig[:],
                            in1=mxc[:, 0:1].broadcast_to((128, 1024)),
                            op=OP.subtract))
                        used("pbig", "vector")
                        d_done = p.count(s_v)
                        acquire(("q32",), "scalar")
                        p.wait("scalar", s_v, d_done)
                        sc(lambda e, dstab=dstab: e.activation(
                            out=q32[:], in_=dstab, func=AF.Exp))
                        q_done = p.count(s_sc)
                        p.wait("vector", s_sc, q_done)
                        for h in range(2):
                            ve(lambda e, h=h: e.reduce_sum(
                                out=sumc[:, h:h + 1], in_=q32[:, ts(h, 512)],
                                axis=mybir.AxisListType.X))
                        vfence()
                        ve(lambda e: e.tensor_tensor(
                            out=sumc[:, 0:1], in0=sumc[:, 0:1],
                            in1=sumc[:, 1:2], op=OP.add))
                        vfence()
                        ve(lambda e: e.tensor_scalar(
                            out=sumc[:, 0:1], in0=sumc[:, 0:1], scalar1=2.0,
                            scalar2=None, op0=OP.mult))
                        vfence()
                        ve(lambda e: e.reciprocal(out=sumc[:, 1:2],
                                                  in_=sumc[:, 0:1]))
                        rp_done = p.count(s_v)
                        p.wait("scalar", s_v, rp_done)
                        sc(lambda e, su=su: e.activation(
                            out=p05[:, su:su + 1], in_=sumc[:, 1:2],
                            func=AF.Copy))
                        used(("q32",), "scalar")
                        ve(lambda e, dstab=dstab: e.tensor_scalar(
                            out=dstab, in0=dstab, scalar1=0.0, scalar2=None,
                            op0=OP.is_equal))
                        ve(lambda e, dstab=dstab: e.tensor_tensor(
                            out=dstab, in0=dstab, in1=iota16[:], op=OP.mult))
                        ve(lambda e, dstab=dstab: e.tensor_reduce(
                            out=eidxf[:], in_=dstab,
                            axis=mybir.AxisListType.X, op=OP.min))
                        used(("outT",), "vector")
                        vfence()
                        ve(lambda e: e.tensor_scalar(
                            out=eidxf[:], in0=eidxf[:], scalar1=2048.0,
                            scalar2=None, op0=OP.add))
                        vfence()
                        ve(lambda e: e.tensor_copy(out=eidxu[:], in_=eidxf[:]))
                        vfence()
                        ve(lambda e: e.tensor_scalar(
                            out=eidxu[:], in0=eidxu[:], scalar1=7,
                            scalar2=None, op0=OP.bitwise_and))
                        vfence()
                        ve(lambda e: e.tensor_copy(out=eidxf[:], in_=eidxu[:]))
                        vfence()
                        for ee in range(E):
                            ve(lambda e, su=su, ee=ee: e.tensor_scalar(
                                out=mcol[:, su * 8 + ee:su * 8 + ee + 1],
                                in0=eidxf[:], scalar1=float(ee), scalar2=None,
                                op0=OP.is_equal))
                        if DEBUG == "moe" and su == 0 and not dbg_done:
                            dbg_done.append(1)
                            dv = p.count(s_v)
                            p.wait("scalar", s_v, dv)
                            dbg_f32(q32[:], 0, dbgst[:])
                            dbg_f32(mxc[:, 0:8], 1024, dbgst[:])
                            dbg_f32(sumc[:, 0:1], 1032, dbgst[:])
                            dbg_f32(p05[:, 0:1], 1033, dbgst[:])
                            dbg_f32(eidxf[:], 1034, dbgst[:])
                            dbg_f32(xn16[:, 0:512], 2048, dbgst[:])
                    mask_done = p.count(s_v)
                    seg(2)
                    p.wait("tensor", s_v, mask_done)
                    acquire(("mrow",), "scalar")
                    for su in range(4):
                        b = su % 2
                        acquire(b, "tensor")
                        te(lambda e, su=su, b=b: e.transpose(
                            out=pb[b][0:8, 0:128], in_=mcol[:, su * 8:su * 8 + 8],
                            identity=ident32[:]))
                        trd = p.count(s_te)
                        p.wait("scalar", s_te, trd)
                        sc(lambda e, su=su, b=b: e.activation(
                            out=mrow[:, ts(su)], in_=pb[b][0:8, 0:128],
                            func=AF.Copy))
                        used(b, "scalar")
                    used(("mcol",), "tensor")
                    mrow_done = p.count(s_sc)
                    p.wait("tensor", s_sc, mrow_done)
                    p.wait("gpsimd", s_sc, mrow_done)
                    fm_done = p.count(s_sc)
                    p.wait("tensor", s_sc, mrow_done)
                    if MOEBAR & 1:
                        barrier()
                    for fh in range(2 if MOESUB >= 2 else 0):
                        seg(4)
                        for ee in range(E):
                            web = we16[ee % 2]
                            acquire(("wu", id(we16[ee % 2])), "sync")
                            wed = dma("sync", web, dr["s_we"][ee][fh])
                            half = ee % 2
                            acquire(("mslot",), "gpsimd")
                            msw = dma("gpsimd", mb16[0:1, 512:1024],
                                      mrow[ee:ee + 1, :])
                            acquire("pbig", "tensor")
                            p.wait("tensor", s_g, msw)
                            te(lambda e, ee=ee, h=half: e.matmul(
                                out=pbig[:, ts(h, 512)],
                                lhsT=ones16[0:1, :],
                                rhs=mb16[0:1, 512:1024],
                                start=True, stop=True))
                            used(("mslot",), "tensor")
                            bd = p.count(s_te)
                            acquire(("mb16", 0), "scalar")
                            p.wait("scalar", s_te, bd)
                            sc(lambda e, ee=ee, h=half: e.activation(
                                out=mb16[:, 0:TC],
                                in_=pbig[:, ts(h, 512)], func=AF.Copy))
                            used("pbig", "scalar")
                            mbd = p.count(s_sc)
                            mx = mx16[ee % 2]
                            acquire(("mx", id(mx)), "vector")
                            p.wait("vector", s_sc, mbd)
                            for ct in range(CT):
                                ve(lambda e, ct=ct, ee=ee, mx=mx: e.tensor_tensor(
                                    out=mx[:, ts(ct, TC)],
                                    in0=xn16[:, ts(ct, TC)],
                                    in1=mb16[:, 0:TC], op=OP.mult))
                            used(("mb16", 0), "vector")
                            mxd = p.count(s_v)
                            if ee == 0:
                                for su in range(4):
                                    acquire(su, "tensor")
                            p.wait("tensor", s_v, mxd)
                            p.wait("tensor", s_w, wed)
                            for su in range(4):
                                for k in range(CT):
                                    te(lambda e, fh=fh, su=su, k=k, ee=ee,
                                       mx=mx, web=web: e.matmul(
                                        out=pb[su][:],
                                        lhsT=mx[:, ts(k, TC)][:, ts(su)],
                                        rhs=web[:, ts(k, 512)],
                                        start=(ee == 0 and k == 0), stop=False),
                                       inc=False)
                            p.emit("tensor", lambda e: e.nop(), inc=s_te)
                            used(("wu", id(we16[ee % 2])), "tensor")
                            used(("mx", id(mx)), "tensor")
                        for su in range(4):
                            te(lambda e, fh=fh, su=su: e.matmul(
                                out=pb[su][:], lhsT=mrow[:, ts(su)],
                                rhs=be16[0:8, ts(fh, 512)],
                                start=False, stop=True), inc=True)
                        used(("mrow",), "tensor")
                        emm = p.count(s_te)
                        seg(8)
                        acquire(("outT",), "scalar")
                        p.wait("scalar", s_te, emm)
                        for su in range(4):
                            sc(lambda e, su=su: e.activation(
                                out=outT[:, ts(su, 512)],
                                in_=pb[su][:], func=AF.Relu,
                                scale=p05[:, su:su + 1]))
                            used(su, "scalar")
                        used(("p05",), "scalar")
                        ev_done = p.count(s_sc)
                        if (DEBUG == "p4x" and ch == 0 and fh == 0
                                and not dbg_done):
                            dbg_done.append(1)
                            dbg_f32(outT[:, 0:2048], 0, dbgst[:])
                            dbg_f32(mrow[0:8, 0:512], 2048, dbgst[:])
                            dbg_f32(mb16[:, 0:1024], 2560, dbgst[:])
                            dbg_f32(p05[:, 0:4], 3584, dbgst[:])
                            dbg_f32(mcol[:, 0:32], 3588, dbgst[:])
                        p.wait("tensor", s_sc, ev_done)
                        for ft in range(4):
                            b = 4 + (ft % 2)
                            acquire(b, "tensor")
                            for su in range(4):
                                te(lambda e, ft=ft, su=su, b=b: e.transpose(
                                    out=pb[b][:, ts(su)],
                                    in_=outT[:, su * 512 + ft * 128:
                                             su * 512 + (ft + 1) * 128],
                                    identity=ident32[:]), inc=(su == 3))
                            trd = p.count(s_te)
                            if (DEBUG == "p4add" and ch == 0 and fh == 0
                                    and ft == 0 and not dbg_done):
                                dbg_done.append(1)
                                p.wait("scalar", s_te, trd)
                                dbg_f32(pb[b][:], 0, dbgst[:])
                                dbg_f32(outT[:, 0:512], 512, dbgst[:])
                                dbg_f32(x_res[:, 0:512], 1024, dbgst[:])
                                p.wait("vector", s_sc, p.count(s_sc))
                            p.wait("vector", s_te, trd)
                            co = fh * 4 + ft
                            xs = x_res[:, ts(co, T)][:, ts(ch, TC)]
                            if MOESUB >= 3:
                                ve(lambda e, b=b, xs=xs: e.tensor_tensor(
                                    out=xs, in0=pb[b][:], in1=xs, op=OP.add))
                            else:
                                ve(lambda e, b=b: e.tensor_copy(
                                    out=var, in_=pb[b][:]))
                            used(b, "vector")
                        used(("outT",), "tensor")
                        if MOEBAR & 2:
                            barrier()
                    if MOEBAR & 4:
                        barrier()
                    if DEBUG == "p4end" and ch == NCH - 1:
                        dv = p.count(s_v)
                        p.wait("scalar", s_v, dv)
                        dbg_f32(x_res[:, 0:512], 0, dbgst[:])
                        dbg_f32(x_res[:, 7 * T + 3584:7 * T + 4096], 512, dbgst[:])
                barrier()

        if PHASES > 4 and 4 not in SKIPS:
            moe_phase("p4")
            p.serial = bool(SERALL)

        def mamba_phase(sub=99):
            if sub < 1:
                return
            p.serial = bool(P5SER & 1) or bool(SERALL)
            with ExitStack() as sco:
                xn16 = sb("p5xn", [128, CT * TC], F16, c=sco)
                xf16 = sb("p5xf", [128, CT * TC], F16, c=sco)
                mb = sb("p5mb", [128, TC], c=sco)[:]; rb = sb("p5rb", [128, TC], c=sco)[:]
                t1 = sb("p5t1", [128, TC], c=sco)[:]; var = sb("p5v", [128, TC], c=sco)[:]
                wi16 = [sb(f"p5wi{i}", [128, 8 * 128], F16, c=sco) for i in range(2)]
                xin = sb("p5xi", [128, TC + 4], c=sco)
                cv0 = rb
                cv1 = t1
                xcv = sb("p5xcv", [128, DSL * TC], F16, c=sco)
                szb = sb("p5sz", [128, DSL * TC], F16, c=sco)
                xdst = sb("p5xd", [96, TC], F16, c=sco)
                if DEBUG == "p5a":
                    dbg_f32(x_res[:, 0:512], 0, dbgst[:])
                for ch in range(NCH):
                    if DEBUG == "p5a" and ch == 1:
                        dv = p.count(s_v)
                        p.wait("scalar", s_v, dv)
                        dbg_f32(x_res[:, 0:512], 512, dbgst[:])
                        dbg_f32(xn16[:, 0:512], 1024, dbgst[:])
                        dbg_f32(xcv[:, 0:512], 1536, dbgst[:])
                    ln_rows(ch, xf16, None, mb, rb, t1, var, 3, 4)
                    acquire("xn5", "vector")
                    for ct in range(CT):
                        xs = x_res[:, ts(ct, T)][:, ts(ch, TC)]
                        ve(lambda e, xs=xs: e.tensor_tensor(
                            out=t1, in0=xs, in1=rb, op=OP.mult))
                        ve(lambda e, ct=ct: e.tensor_tensor(
                            out=xn16[:, ts(ct, TC)], in0=t1, in1=var,
                            op=OP.subtract))
                    app = p.count(s_v)
                    p.wait("tensor", s_v, app)
                    acquire(2, "tensor")
                    for half, nsl in ((0, DSL), (1, DSL)):
                        for sl in range(nsl):
                            wib = wi16[sl % 2]
                            acquire(("wi", id(wib)), "sync")
                            wid = dma("sync", wib[:], dr["s_win"][half * DSL + sl])
                            b = sl % 2
                            acquire(b, "tensor")
                            p.wait("tensor", s_w, wid)
                            for k in range(CT):
                                te(lambda e, k=k, b=b, wib=wib: e.matmul(
                                    out=pb[b][:], lhsT=wib[:, ts(k)],
                                    rhs=xn16[:, ts(k, TC)],
                                    start=(k == 0), stop=(k == CT - 1)),
                                   inc=(k == CT - 1))
                            used(("wi", id(wib)), "tensor")
                            used("xn5", "tensor")
                            mmd = p.count(s_te)
                            if half == 0:
                                acquire(("xin",), "scalar")
                                sc(lambda e, sl=sl: e.activation(
                                    out=xin[:, 0:3],
                                    in_=ctail[:, sl * 4:sl * 4 + 3],
                                    func=AF.Copy))
                                p.wait("scalar", s_te, mmd)
                                sc(lambda e, b=b: e.activation(
                                    out=xin[:, 3:TC + 3], in_=pb[b][:],
                                    func=AF.Copy))
                                used(b, "scalar")
                                sc(lambda e, sl=sl: e.activation(
                                    out=ctail[:, sl * 4:sl * 4 + 3],
                                    in_=xin[:, TC:TC + 3], func=AF.Copy))
                                xin_done = p.count(s_sc)
                                p.wait("vector", s_sc, xin_done)
                                ve(lambda e, sl=sl: e.tensor_scalar(
                                    out=cv0[:], in0=xin[:, 0:TC],
                                    scalar1=sd_[:, sl * 4:sl * 4 + 1],
                                    scalar2=None, op0=OP.mult))
                                ve(lambda e, sl=sl: e.scalar_tensor_tensor(
                                    out=cv1[:], in0=xin[:, 1:TC + 1],
                                    scalar=sd_[:, sl * 4 + 1:sl * 4 + 2],
                                    in1=cv0[:], op0=OP.mult, op1=OP.add))
                                ve(lambda e, sl=sl: e.scalar_tensor_tensor(
                                    out=cv0[:], in0=xin[:, 2:TC + 2],
                                    scalar=sd_[:, sl * 4 + 2:sl * 4 + 3],
                                    in1=cv1[:], op0=OP.mult, op1=OP.add))
                                ve(lambda e, sl=sl: e.scalar_tensor_tensor(
                                    out=cv1[:], in0=xin[:, 3:TC + 3],
                                    scalar=sd_[:, sl * 4 + 3:sl * 4 + 4],
                                    in1=cv0[:], op0=OP.mult, op1=OP.add))
                                used(("xin",), "vector")
                                cvd = p.count(s_v)
                                p.wait("scalar", s_v, cvd)
                                sc(lambda e, sl=sl: e.activation(
                                    out=xcv[:, ts(sl, TC)], in_=cv1[:],
                                    func=AF.Silu,
                                    bias=sd_[:, 64 + sl:64 + sl + 1]))
                                xcvd = p.count(s_sc)
                                p.wait("tensor", s_sc, xcvd)
                                te(lambda e, sl=sl: e.matmul(
                                    out=pb[2][0:96, :],
                                    lhsT=wx16[:, sl * 96:(sl + 1) * 96],
                                    rhs=xcv[:, ts(sl, TC)],
                                    start=(sl == 0), stop=(sl == DSL - 1)),
                                   inc=(sl == DSL - 1))
                            else:
                                p.wait("scalar", s_te, mmd)
                                sc(lambda e, sl=sl, b=b: e.activation(
                                    out=szb[:, ts(sl, TC)], in_=pb[b][:],
                                    func=AF.Silu))
                                used(b, "scalar")
                        if half == 0:
                            xd_mm = p.count(s_te)
                            p.wait("scalar", s_te, xd_mm)
                            sc(lambda e: e.activation(out=xdst[:], in_=pb[2][0:96, :],
                                                      func=AF.Copy))
                            used(2, "scalar")
                            xd_done = p.count(s_sc)
                            p.wait("gpsimd", s_sc, xd_done)
                            dma("gpsimd", dr["s_xdbl"][ch], xdst[:])
                            dma("gpsimd", dr["s_xcv"][ch], xcv[:])
                            used(("xcvbuf",), "gpsimd")
                    used(("lnrows", id(mb)), "vector")
                    szdone = p.count(s_sc)
                    p.wait("gpsimd", s_sc, szdone)
                    dma("gpsimd", dr["s_sz"][ch], szb[:])
                    gcount = p.count(s_g)
                    p.wait("scalar", s_g, gcount)
                barrier()
            if sub < 2:
                return
            for ct in range(CT):
                dma("sync", dr["s_xsp"][ct], x_res[:, ts(ct, T)])
            barrier()
            if sub < 3:
                return
            p.serial = bool(P5SER & 2) or bool(SERALL)
            with ExitStack() as sco:
                yf16 = sb("p5yf", [128, DSL * TC], F16, c=sco)
                bbcb = sb("p5bb", [128, 2 * NS * TC], F16, c=sco)
                xdbl = [sb("p5cx0", [96, TC], F16, c=sco)] * 2
                dt32 = [sb("p5dt0", [128, TC], c=sco)] * 2
                dtx = sb("p5dtx", [128, TC], F16, c=sco)
                da32 = [sb(f"p5da{i}", [128, TC], c=sco) for i in range(2)]
                u16 = sb("p5u", [128, TC], F16, c=sco)
                s16 = [sb(f"p5s{i}", [128, TC], F16, c=sco) for i in range(2)]
                tmp16 = u16
                yy = sb("p5y0", [128, TC], c=sco)[:]
                xcvs = [sb("p5xv0", [128, TC], F16, c=sco)] * 2
                szs = [sb("p5zs0", [128, TC], F16, c=sco)] * 2
                woc = [sb(f"p5wo{i}", [128, DI // 4], F16, c=sco) for i in range(2)]
                for ch in range(NCH):
                    xdb = xdbl[ch % 2]
                    acquire(("xdbl", id(xdb)), "gpsimd")
                    xdd = dma("gpsimd", xdb[:], dr["s_xdbl"][ch])
                    p.wait("gpsimd", s_g, xdd)
                    acquire(("bbcb",), "scalar")
                    for kind in range(2):
                        acquire(("yf",), "gpsimd")
                        rw = p.emit("gpsimd", lambda e, k=kind, xdb=xdb:
                                    e.dma_start(
                                        out=yf16[0:1, 0:NS * TC],
                                        in_=xdb[64 + k * NS:64 + (k + 1) * NS, :],
                                        allow_slow_non_contiguous=True),
                                    inc=s_g, n=16)
                        p.wait("tensor", s_g, rw)
                        for n in range(NS):
                            b = 4 + (n % 2)
                            slot = n * 2 + kind
                            acquire(b, "tensor")
                            te(lambda e, n=n, b=b: e.matmul(
                                out=pb[b][:], lhsT=ones16[0:1, :],
                                rhs=yf16[0:1, n * TC:(n + 1) * TC],
                                start=True, stop=True))
                            bd = p.count(s_te)
                            p.wait("scalar", s_te, bd)
                            sc(lambda e, slot=slot, b=b: e.activation(
                                out=bbcb[:, slot * TC:(slot + 1) * TC],
                                in_=pb[b][:], func=AF.Copy))
                            used(b, "scalar")
                        used(("yf",), "tensor")
                    used(("xdbl", id(xdb)), "gpsimd")
                    bb_done = p.count(s_sc)
                    p.wait("vector", s_sc, bb_done)
                    acquire(("yf",), "vector")
                    for sl in range(DSL):
                        dtb = dt32[sl % 2]
                        acquire(0, "tensor")
                        te(lambda e, sl=sl, xdb=xdb: e.matmul(
                            out=pb[0][:], lhsT=wdt16[:, ts(sl)],
                            rhs=xdb[0:RK, :], start=True, stop=True))
                        dtd = p.count(s_te)
                        acquire(("dt", id(dtb)), "scalar")
                        p.wait("scalar", s_te, dtd)
                        sc(lambda e, sl=sl, dtb=dtb: e.activation(
                            out=dtb[:], in_=pb[0][:], func=AF.Exp,
                            bias=sd_[:, 96 + sl:96 + sl + 1]))
                        used(0, "scalar")
                        sc(lambda e, dtb=dtb: e.activation(
                            out=dtb[:], in_=dtb[:], func=AF.Ln, bias=1.0))
                        dt_done = p.count(s_sc)
                        xcb = xcvs[sl % 2]; szc = szs[sl % 2]
                        acquire(("xcv", id(xcb)), "gpsimd")
                        xcd = dma("gpsimd", xcb[:],
                                  dr["s_xcv"][ch][:, ts(sl, TC)])
                        acquire(("szs", id(szc)), "gpsimd")
                        szd = dma("gpsimd", szc[:],
                                  dr["s_sz"][ch][:, ts(sl, TC)])
                        p.wait("vector", s_sc, dt_done)
                        p.wait("vector", s_g, xcd)
                        acquire(("dtx",), "vector")
                        ve(lambda e, dtb=dtb, xcb=xcb: e.tensor_tensor(
                            out=dtx[:], in0=dtb[:], in1=xcb[:], op=OP.mult))
                        for n in range(NS):
                            dab = da32[n % 2]
                            acquire(("da", id(dab)), "scalar")
                            sc(lambda e, sl=sl, n=n, dab=dab, dtb=dtb:
                               e.activation(out=dab[:], in_=dtb[:], func=AF.Exp,
                                            scale=aneg[:, sl * NS + n:
                                                       sl * NS + n + 1]))
                            dad = p.count(s_sc)
                            ve(lambda e, n=n: e.tensor_tensor(
                                out=u16[:], in0=dtx[:],
                                in1=bbcb[:, (n * 2) * TC:(n * 2 + 1) * TC],
                                op=OP.mult))
                            p.wait("vector", s_sc, dad)
                            sb_ = s16[n % 2]
                            acquire(("s16", id(sb_)), "vector")
                            ve(lambda e, sl=sl, n=n, sb_=sb_, dab=dab:
                               e.tensor_tensor_scan(
                                   out=sb_[:], data0=dab[:], data1=u16[:],
                                   initial=stS[:, sl * NS + n:sl * NS + n + 1],
                                   op0=OP.mult, op1=OP.add))
                            used(("da", id(dab)), "vector")
                            scn = p.count(s_v)
                            p.wait("scalar", s_v, scn)
                            sc(lambda e, sl=sl, n=n, sb_=sb_: e.activation(
                                out=stS[:, sl * NS + n:sl * NS + n + 1],
                                in_=sb_[:, TC - 1:TC], func=AF.Copy))
                            used(("s16", id(sb_)), "scalar")
                            ve(lambda e, n=n, sb_=sb_: e.tensor_tensor(
                                out=tmp16[:], in0=sb_[:],
                                in1=bbcb[:, (n * 2 + 1) * TC:(n * 2 + 2) * TC],
                                op=OP.mult))
                            used(("s16", id(sb_)), "vector")
                            if n == 0:
                                ve(lambda e: e.tensor_copy(out=yy, in_=tmp16[:]))
                            else:
                                ve(lambda e: e.tensor_tensor(
                                    out=yy, in0=tmp16[:], in1=yy, op=OP.add))
                        used(("dt", id(dtb)), "vector")
                        used(("dtx",), "vector")
                        if (DEBUG == "p5c" and ch == 0 and sl == 0
                                and not dbg_done):
                            dbg_done.append(1)
                            dv = p.count(s_v)
                            p.wait("scalar", s_v, dv)
                            dbg_f32(dtb[:], 0, dbgst[:])
                            dbg_f32(dtx[:], 512, dbgst[:])
                            dbg_f32(s16[(NS - 1) % 2][:], 1024, dbgst[:])
                            dbg_f32(yy, 1536, dbgst[:])
                            dbg_f32(bbcb[:, 0:512], 2048, dbgst[:])
                            dbg_f32(bbcb[:, 512:1024], 2560, dbgst[:])
                            dbg_f32(xcvs[0][:], 3072, dbgst[:])
                            dbg_f32(xdbl[0][0:64, 0:512], 3584, dbgst[:])
                        p.wait("vector", s_g, szd)
                        ve(lambda e, szc=szc: e.tensor_tensor(
                            out=yy, in0=yy, in1=szc[:], op=OP.mult))
                        used(("szs", id(szc)), "vector")
                        ve(lambda e, sl=sl, xcb=xcb: e.scalar_tensor_tensor(
                            out=yf16[:, ts(sl, TC)], in0=xcb[:],
                            scalar=sd_[:, 80 + sl:80 + sl + 1], in1=yy,
                            op0=OP.mult, op1=OP.add))
                        used(("xcv", id(xcb)), "vector")
                    used(("bbcb",), "vector")
                    yf_done = p.count(s_v)
                    p.wait("tensor", s_v, yf_done)
                    for co in range(CT):
                        b = 1 + (co % 2)
                        acquire(b, "tensor")
                        for dh in range(4):
                            wob = woc[dh % 2]
                            acquire(("woc", id(wob)), "sync")
                            wod = dma("sync", wob[:],
                                      dr["s_wout"][co][:, dh * 512:(dh + 1) * 512])
                            p.wait("tensor", s_w, wod)
                            for j in range(4):
                                sl = dh * 4 + j
                                te(lambda e, sl=sl, j=j, b=b, wob=wob, dh=dh:
                                   e.matmul(
                                    out=pb[b][:], lhsT=wob[:, ts(j)],
                                    rhs=yf16[:, ts(sl, TC)],
                                    start=(sl == 0), stop=(sl == DSL - 1)),
                                   inc=(j == 3))
                            used(("woc", id(wob)), "tensor")
                        mmd = p.count(s_te)
                        p.wait("scalar", s_te, mmd)
                        xs = x_res[:, ts(co, T)][:, ts(ch, TC)]
                        sc(lambda e, b=b, xs=xs: e.activation(
                            out=xs, in_=pb[b][:], func=AF.Copy))
                        used(b, "scalar")
                    used(("yf",), "tensor")
                    used(("xdbl", id(xdb)), "tensor")
                barrier()
            if sub < 4:
                return
            p.serial = bool(SERALL)
            with ExitStack() as sco:
                bnc = [sb(f"p5bn{i}", [128, T], c=sco) for i in range(2)]
                for ct in range(CT):
                    bb = bnc[ct % 2]
                    acquire(("bnc", id(bb)), "gpsimd")
                    bd = dma("gpsimd", bb[:], dr["s_xsp"][ct])
                    p.wait("vector", s_g, bd)
                    ve(lambda e, ct=ct, bb=bb: e.tensor_tensor(
                        out=x_res[:, ts(ct, T)], in0=bb[:],
                        in1=x_res[:, ts(ct, T)], op=OP.add))
                    used(("bnc", id(bb)), "vector")
                barrier()

        if PHASES > 5:
            mamba_phase(P5SUB)
            p.serial = bool(SERALL)
        if PHASES > 6:
            moe_phase("p6")

        for ct in range(CT):
            dma("sync", out_ap[ct], x_res[:, ts(ct, T)])
        p.wait("sync", s_w)
        barrier()
        p.run(block)
    return nc


_NC_CACHE = {}


def _tile8(w):
    a = w.shape[0] // 128
    b = w.shape[1] // 128
    return np.ascontiguousarray(
        w.reshape(a, 128, b, 128).transpose(2, 1, 0, 3).reshape(b, 128, a * 128))


def kernel(L0_attn_x, L0_attn_w_k, L0_attn_w_v, L0_attn_w_r, L0_attn_w_o,
           L0_attn_time_decay, L0_attn_time_first, L0_attn_time_mix_key,
           L0_attn_time_mix_value, L0_attn_time_mix_receptance,
           L0_ffn_w_freq_real, L0_ffn_w_freq_imag, L0_ffn_bias,
           L1_attn_W_in, L1_attn_conv_w, L1_attn_conv_b, L1_attn_W_x,
           L1_attn_W_dt, L1_attn_b_dt, L1_attn_A_log, L1_attn_D,
           L1_attn_W_out, L1_ffn_Wg, L1_ffn_bg, L1_ffn_We, L1_ffn_be):
    f32 = lambda v: np.ascontiguousarray(np.asarray(v, dtype=np.float32))
    x = f32(L0_attn_x)
    B = x.shape[0]

    n_, d_ = np.meshgrid(np.arange(C), np.arange(C), indexing="ij")
    ang = (-2.0 * np.pi / C) * (n_ * d_ % C)
    fr16 = np.cos(ang).astype(np.float16).reshape(8, 128, C)
    fi16 = np.sin(ang).astype(np.float16).reshape(8, 128, C)
    ident16 = np.eye(128, dtype=np.float16)

    def colpack(v, k):
        return np.ascontiguousarray(f32(v).reshape(k, 128).T)

    smallv = np.zeros((128, 48), np.float32)
    smallv[:, 0:8] = colpack(L0_attn_time_decay, 8)
    smallv[:, 8:16] = colpack(L0_attn_time_first, 8)
    smallv[:, 16:24] = colpack(L0_ffn_bias, 8)
    smallv[:, 24:32] = colpack(np.asarray(L0_attn_time_mix_key).reshape(-1), 8)
    smallv[:, 32:40] = colpack(np.asarray(L0_attn_time_mix_value).reshape(-1), 8)
    smallv[:, 40:48] = colpack(np.asarray(L0_attn_time_mix_receptance).reshape(-1), 8)

    smalld = np.zeros((128, 368), np.float32)
    cw = f32(L1_attn_conv_w).reshape(DI, 4)
    smalld[:, 0:64] = cw.reshape(DSL, 128, 4).transpose(1, 0, 2).reshape(128, 64)
    smalld[:, 64:80] = colpack(L1_attn_conv_b, DSL)
    smalld[:, 80:96] = colpack(L1_attn_D, DSL)
    smalld[:, 96:112] = colpack(L1_attn_b_dt, DSL)
    al = f32(L1_attn_A_log)
    smalld[:, 112:112 + DSL * NS] = (
        al.reshape(DSL, 128, NS).transpose(1, 0, 2).reshape(128, DSL * NS))

    common = {
        "fr16": fr16, "fi16": fi16, "ident16": ident16,
        "wfr": f32(L0_ffn_w_freq_real).reshape(8, 128, C),
        "wfi": f32(L0_ffn_w_freq_imag).reshape(8, 128, C),
        "wkT_t": _tile8(f32(L0_attn_w_k).T),
        "wvT_t": _tile8(f32(L0_attn_w_v).T),
        "wrT_t": _tile8(f32(L0_attn_w_r).T),
        "woT_t": _tile8(f32(L0_attn_w_o).T),
        "winT_t": _tile8(f32(L1_attn_W_in).T),
        "woutT_t": _tile8(f32(L1_attn_W_out).T),
        "wxT_t": np.ascontiguousarray(
            f32(L1_attn_W_x).T.reshape(DSL, 128, 96).transpose(1, 0, 2)
            .reshape(128, DSL * 96)),
        "wdtT_t": f32(L1_attn_W_dt).T,
        "wg_t": f32(L1_ffn_Wg).reshape(8, 128, C),
        "we_t": f32(L1_ffn_We).reshape(E, 8, 128, C),
        "smallv": smallv, "smalld": smalld,
        "be": f32(L1_ffn_be), "bgrow": f32(L1_ffn_bg).reshape(1, C),
    }

    key = "prog"
    if key not in _NC_CACHE:
        _NC_CACHE[key] = build_program()
    nc = _NC_CACHE[key]

    in_maps = []
    for b in range(B):
        m = dict(common)
        m["xT"] = np.ascontiguousarray(x[b].T).reshape(CT, 128, T)
        in_maps.append(m)
    res = run_bass_kernel_spmd(nc, in_maps, core_ids=list(range(B)))
    out = np.empty((B, T, C), np.float32)
    for b in range(B):
        o = res.results[b]["out"].reshape(C, T)
        out[b] = o.T
    return out




# version 11
# speedup vs baseline: 1.8923x; 1.1197x over previous
import numpy as np
import concourse.bass as bass
import concourse.mybir as mybir
from concourse.bass_utils import run_bass_kernel_spmd
from contextlib import ExitStack

F32 = mybir.dt.float32
F16 = mybir.dt.float16
U32 = mybir.dt.uint32
AF = mybir.ActivationFunctionType
OP = mybir.AluOpType

ENGINES = ("sync", "scalar", "vector", "tensor", "gpsimd")
C, CT, T, TC = 1024, 8, 4096, 512
NCH = T // TC
DI, DSL, NS, RK, E = 2048, 16, 16, 64, 8
PHASES = 7
DEBUG = ""
P5SUB = 4
SKIPS = ()
MOESUB = 3
MOEBAR = 0
MOESER = 3
MOESEG = 1
P5SER = 0
SERALL = 0


def ts(i, n=128):
    return slice(i * n, (i + 1) * n)


class Prog:
    def __init__(self):
        self.q = {e: [] for e in ENGINES}
        self.led = {}
        self.serial = False
        self.sems = None

    def emit(self, eng, fn, inc=None, n=1):
        if self.serial and self.sems is not None:
            for oe, osem in self.sems.items():
                if oe != eng:
                    self.wait(eng, osem)
        if inc is not None:
            self.led[id(inc)] = self.led.get(id(inc), 0) + n
            def wrapped(e, fn=fn, inc=inc, n=n):
                fn(e).then_inc(inc, n)
        else:
            def wrapped(e, fn=fn):
                fn(e)
        self.q[eng].append(wrapped)
        return self.led.get(id(inc), 0) if inc is not None else None

    def count(self, sem):
        return self.led.get(id(sem), 0)

    def wait(self, eng, sem, val=None):
        v = self.count(sem) if val is None else val
        if v > 0:
            self.q[eng].append(lambda e, sem=sem, v=v: e.wait_ge(sem, v))

    def run(self, block):
        for e in ENGINES:
            def make(e):
                def _(eng):
                    for fn in self.q[e]:
                        fn(eng)
                return _
            getattr(block, e)(make(e))


def build_program():
    nc = bass.Bass("TRN2", target_bir_lowering=False)
    dr = {}

    def din(name, shape, dtype=F32):
        dr[name] = nc.dram_tensor(name, shape, dtype, kind="ExternalInput").ap()

    def dscr(name, shape, dtype=F16):
        dr[name] = nc.dram_tensor(name, shape, dtype, kind="Internal").ap()

    din("xT", [CT, 128, T])
    din("fr16", [8, 128, C], F16); din("fi16", [8, 128, C], F16)
    din("ident16", [128, 128], F16)
    din("wfr", [8, 128, C]); din("wfi", [8, 128, C])
    din("wkT_t", [8, 128, C]); din("wvT_t", [8, 128, C]); din("wrT_t", [8, 128, C])
    din("woT_t", [8, 128, C])
    din("winT_t", [32, 128, C])
    din("woutT_t", [8, 128, DI])
    din("wxT_t", [128, DSL * 96])
    din("wdtT_t", [RK, DI])
    din("wg_t", [8, 128, C])
    din("we_t", [E, 8, 128, C])
    din("smallv", [128, 48])
    din("smalld", [128, 368])
    din("be", [8, C])
    din("bgrow", [1, C])
    dscr("s_m", [8, 128, C])
    dscr("s_kv", [6, 8, 128, C])
    dscr("s_wo", [8, 128, C])
    dscr("s_win", [32, 128, C])
    dscr("s_wout", [8, 128, DI])
    dscr("s_wg", [8, 128, C])
    dscr("s_bg", [1, C])
    dscr("s_we", [E, 2, 128, 8 * 512])
    dscr("s_xcv", [NCH, 128, DSL * TC])
    dscr("s_sz", [NCH, 128, DSL * TC])
    dscr("s_xdbl", [NCH, 96, TC])
    dscr("s_xsp", [CT, 128, T], F32)
    out_ap = nc.dram_tensor("out", [CT, 128, T], F32, kind="ExternalOutput").ap()
    dbg_ap = nc.dram_tensor("dbg", [128, 8192], F32, kind="ExternalOutput").ap()

    ctx = ExitStack()
    with ctx:
        def sb(name, shape, dtype=F32, c=None):
            return (c or ctx).enter_context(nc.sbuf_tensor(name, shape, dtype))

        ident = sb("ident", [128, 128], F16)
        ones16 = sb("ones16", [128, 128], F16)
        sv = sb("sv", [128, 72])
        lam8 = sb("lam8", [128, 8]); etf8 = sb("etf8", [128, 8])
        bias05 = sb("bias05", [128, 8])
        sd_ = sb("sd_", [128, 368])
        aneg = sb("aneg", [128, DSL * NS])
        be16 = sb("be16", [33, C], F16)
        bg16 = be16[32:33, :]
        wx16 = sb("wx16", [128, DSL * 96], F16)
        wdt16 = sb("wdt16", [RK, DI], F16)
        cpool = sb("cpool", [128, 4])
        ident32 = sb("ident32", [128, 128])
        nc.const_aps.register(F32, 1e-5, cpool[:, 0:1])
        nc.const_aps.register(F32, 1.0, cpool[:, 1:2])
        stA = sb("stA", [128, 8]); stB = sb("stB", [128, 8])
        stS = sb("stS", [128, DSL * NS])
        ctail = sb("ctail", [128, DSL * 4])
        xsh0 = sb("xsh0", [128, 8], F16)

        pb = [ctx.enter_context(nc.psum_tensor(f"pb{i}", [128, 512], F32))
              for i in range(6)]
        pbig = ctx.enter_context(nc.psum_tensor("pbig", [128, 1024], F32))

        sems = {e: ctx.enter_context(nc.semaphore(f"bar_{e}")) for e in ENGINES}
        s_w = ctx.enter_context(nc.semaphore("s_w"))
        s_g = ctx.enter_context(nc.semaphore("s_g"))
        s_te = ctx.enter_context(nc.semaphore("s_te"))
        s_sc = ctx.enter_context(nc.semaphore("s_sc"))
        s_v = ctx.enter_context(nc.semaphore("s_v"))
        block = ctx.enter_context(nc.Block())

        p = Prog()
        SEM = {"tensor": s_te, "scalar": s_sc, "vector": s_v,
               "sync": s_w, "gpsimd": s_g}
        p.sems = SEM
        if SERALL:
            p.serial = True

        def dma(queue, out, in_):
            return p.emit(queue, lambda e, o=out, i=in_: e.dma_start(out=o, in_=i),
                          inc=SEM[queue], n=16)

        def te(fn, inc=True):
            return p.emit("tensor", fn, inc=s_te if inc else None)

        def sc(fn):
            return p.emit("scalar", fn, inc=s_sc)

        def ve(fn):
            return p.emit("vector", fn, inc=s_v)

        def vfence():
            p.wait("vector", s_v, p.count(s_v))

        def barrier():
            p.wait("sync", s_w)
            p.wait("gpsimd", s_g)
            for e in ENGINES:
                p.emit(e, lambda eng: eng.nop(), inc=sems[e], n=1)
            tgt = {e: p.count(sems[e]) for e in ENGINES}
            for e in ENGINES:
                for o in ENGINES:
                    if o != e:
                        p.wait(e, sems[o], tgt[o])

        dbg_done = []
        dbgst = sb("dbgst", [128, 96]) if DEBUG else None
        guards = {}

        def acquire(res, producer_eng):
            for sem, cnt in guards.pop(res, ()):
                p.wait(producer_eng, sem, cnt)

        def used(res, eng):
            guards.setdefault(res, []).append((SEM[eng], p.count(SEM[eng])))

        def dbg_f32(ap, col0, stage):
            n = ap.shape[1]
            sw = stage.shape[1]
            for o in range(0, n, sw):
                w = min(sw, n - o)
                np_ = ap.shape[0]
                sc(lambda e, ap=ap, o=o, w=w, np_=np_: e.activation(
                    out=stage[0:np_, 0:w], in_=ap[:, o:o + w], func=AF.Copy))
                dn = p.count(s_sc)
                p.wait("sync", s_sc, dn)
                p.emit("sync", lambda e, o=o, w=w, np_=np_: e.dma_start(
                    out=dbg_ap[0:np_, col0 + o:col0 + o + w],
                    in_=stage[0:np_, 0:w],
                    allow_slow_non_contiguous=True), inc=s_w, n=16)
                wd = p.count(s_w)
                p.wait("scalar", s_w, wd)

        def ln_rows(ch, xf16, _unused, mb, rb, t1, var, b0, b1):
            acquire(("lnxf", id(xf16)), "scalar")
            acquire(("lnrows", id(mb)), "scalar")
            for ct in range(CT):
                xs = x_res[:, ts(ct, T)][:, ts(ch, TC)]
                sc(lambda e, xs=xs, ct=ct: e.activation(
                    out=xf16[:, ts(ct, TC)], in_=xs, func=AF.Copy))
            cast_done = p.count(s_sc)
            if DEBUG == "ones" and not dbg_done:
                dbg_done.append(1)
                p.wait("scalar", s_sc, cast_done)
                dbg_f32(ones16[:], 0, mb)
                dbg_f32(xf16[:, 0:1024], 1024, mb)
                for ct in range(2):
                    dbg_f32(x_res[:, ts(ct, T)][:, 0:512], 2048 + ct * 512, mb)
            acquire(b0, "tensor"); acquire(b1, "tensor")
            p.wait("tensor", s_sc, cast_done)
            for ct in range(CT):
                te(lambda e, ct=ct: e.matmul(
                    out=pb[b0][:], lhsT=ones16[:], rhs=xf16[:, ts(ct, TC)],
                    start=(ct == 0), stop=(ct == CT - 1)), inc=(ct == CT - 1))
            s_done = p.count(s_te)
            if DEBUG == "psum" and not dbg_done:
                dbg_done.append(1)
                p.wait("scalar", s_te, s_done)
                dbg_f32(pb[b0][:], 0, mb)
            p.wait("scalar", s_te, s_done)
            for ct in range(CT):
                xs = x_res[:, ts(ct, T)][:, ts(ch, TC)]
                sc(lambda e, xs=xs, ct=ct: e.activation(
                    out=xf16[:, ts(ct, TC)], in_=xs, func=AF.Square))
            sqcast_done = p.count(s_sc)
            p.wait("tensor", s_sc, sqcast_done)
            for ct in range(CT):
                te(lambda e, ct=ct: e.matmul(
                    out=pb[b1][:], lhsT=ones16[:], rhs=xf16[:, ts(ct, TC)],
                    start=(ct == 0), stop=(ct == CT - 1)), inc=(ct == CT - 1))
            q_done = p.count(s_te)
            used(("lnxf", id(xf16)), "tensor")
            p.wait("scalar", s_te, s_done)
            sc(lambda e: e.activation(out=mb, in_=pb[b0][:], func=AF.Copy,
                                      scale=1.0 / 1024))
            used(b0, "scalar")
            if DEBUG == "mb" and not dbg_done:
                dbg_done.append(1)
                dbg_f32(mb, 0, var)

            sc(lambda e: e.activation(out=t1, in_=mb, func=AF.Square))
            sq_done = p.count(s_sc)
            p.wait("vector", s_te, q_done)
            p.wait("vector", s_sc, sq_done)
            ve(lambda e: e.scalar_tensor_tensor(
                out=var, in0=pb[b1][:], scalar=1.0 / 1024, in1=t1,
                op0=OP.mult, op1=OP.subtract))
            used(b1, "vector")
            var_done = p.count(s_v)
            p.wait("scalar", s_v, var_done)
            sc(lambda e: e.activation(out=t1, in_=var, func=AF.Sqrt, bias=1e-5))
            sqrt_done = p.count(s_sc)
            p.wait("vector", s_sc, sqrt_done)
            ve(lambda e: e.reciprocal(out=rb, in_=t1))
            ve(lambda e: e.tensor_tensor(out=var, in0=mb, in1=rb, op=OP.mult))
            used(("lnrows", id(mb)), "vector")
            if DEBUG == "ln" and ch == 0 and not dbg_done:
                dbg_done.append(1)
                dv = p.count(s_v)
                p.wait("scalar", s_v, dv)
                dbg_f32(mb, 0, t1)
                dbg_f32(rb, 512, t1)
                dbg_f32(var, 1024, t1)

        dma("sync", ident[:], dr["ident16"])
        dma("sync", sv[:, 0:48], dr["smallv"])
        dma("sync", sd_[:], dr["smalld"])
        sm_w = p.count(s_w)
        ve(lambda e: e.memset(ones16[:], 1.0))
        ve(lambda e: e.memset(cpool[:, 0:1], 1e-5))
        ve(lambda e: e.memset(cpool[:, 1:2], 1.0))
        ve(lambda e: e.memset(stA[:], 0.0))
        ve(lambda e: e.memset(stB[:], 0.0))
        ve(lambda e: e.memset(stS[:], 0.0))
        ve(lambda e: e.memset(ctail[:], 0.0))
        ve(lambda e: e.memset(xsh0[:], 0.0))
        p.wait("scalar", s_w, sm_w)
        sc(lambda e: e.activation(out=cpool[:, 2:3], in_=cpool[:, 0:1],
                                  func=AF.Copy, accum_out=cpool[:, 3:4]))
        sc(lambda e: e.activation(out=ident32[:], in_=ident[:], func=AF.Copy))
        sc(lambda e: e.activation(out=lam8[:], in_=sv[:, 0:8], func=AF.Exp))
        sc(lambda e: e.activation(out=lam8[:], in_=lam8[:], func=AF.Exp, scale=-1.0))
        sc(lambda e: e.activation(out=etf8[:], in_=sv[:, 8:16], func=AF.Exp))
        sc(lambda e: e.activation(out=bias05[:], in_=sv[:, 16:24], func=AF.Copy,
                                  scale=0.5))
        sc(lambda e: e.activation(out=aneg[:], in_=sd_[:, 112:112 + DSL * NS],
                                  func=AF.Exp))
        d0 = p.count(s_sc)
        p.wait("vector", s_sc, d0)
        ve(lambda e: e.tensor_scalar(out=aneg[:], in0=aneg[:], scalar1=-1.0,
                                     scalar2=None, op0=OP.mult))
        for j, src0 in ((0, 24), (1, 32), (2, 40)):
            ve(lambda e, j=j, s0=src0: e.tensor_scalar(
                out=sv[:, 48 + 8 * j:56 + 8 * j], in0=sv[:, s0:s0 + 8],
                scalar1=-1.0, scalar2=1.0, op0=OP.mult, op1=OP.add))
        smalls_v = p.count(s_v)

        with ExitStack() as prep:
            stg = sb("stg", [128, 4096], F32, c=prep)
            stg16 = sb("stg16", [128, 2048], F16, c=prep)
            fr = sb("fr", [128, 8 * C], F16, c=prep)
            fi = sb("fi", [128, 8 * C], F16, c=prep)
            w16a = sb("w16a", [128, 8 * C], F16, c=prep)
            w16b = sb("w16b", [128, 8 * C], F16, c=prep)
            w16c = sb("w16c", [128, 8 * C], F16, c=prep)
            ptre = sb("ptre", [128, 8 * C], F16, c=prep)
            ptim = sb("ptim", [128, 8 * C], F16, c=prep)

            dma("sync", stg[0:8, 0:1024], dr["be"])
            dma("sync", stg[0:1, 1024:2048], dr["bgrow"])
            dma("sync", stg[:, 2048:2048 + DSL * 96], dr["wxT_t"])
            w0 = p.count(s_w)
            p.wait("scalar", s_w, w0)
            p.wait("scalar", s_v, smalls_v)
            sc(lambda e: e.activation(out=be16[0:8, :], in_=stg[0:8, 0:1024],
                                      func=AF.Copy))
            sc(lambda e: e.activation(out=bg16, in_=stg[0:1, 1024:2048],
                                      func=AF.Copy))
            bgc = p.count(s_sc)
            p.wait("sync", s_sc, bgc)
            dma("sync", dr["s_bg"], bg16)
            sc(lambda e: e.activation(out=wx16[:], in_=stg[:, 2048:2048 + DSL * 96],
                                      func=AF.Copy))
            used(("stg",), "scalar")
            acquire(("stg",), "sync")
            w0b = dma("sync", stg[0:RK, 0:DI], dr["wdtT_t"])
            p.wait("scalar", s_w, w0b)
            sc(lambda e: e.activation(
                out=wdt16[:], in_=stg[0:RK, 0:DI], func=AF.Copy))
            used(("stg",), "scalar")

            for ct in range(CT):
                dma("sync", fr[:, ts(ct, C)], dr["fr16"][ct])
                dma("sync", fi[:, ts(ct, C)], dr["fi16"][ct])
            fw = p.count(s_w)

            def cast_sq(dst16, src_t, scale=1.0, extra=None, extrascale=None):
                for g in range(2):
                    wdone = None
                    acquire(("stg",), "sync")
                    for j in range(4):
                        wdone = dma("sync", stg[:, ts(j, C)], src_t[g * 4 + j])
                    p.wait("scalar", s_w, wdone)
                    for j in range(4):
                        ctt = g * 4 + j
                        sc(lambda e, j=j, ctt=ctt: e.activation(
                            out=dst16[:, ts(ctt, C)], in_=stg[:, ts(j, C)],
                            func=AF.Copy, scale=scale))
                        if extra is not None:
                            sc(lambda e, j=j, ctt=ctt: e.activation(
                                out=extra[:, ts(ctt, C)], in_=stg[:, ts(j, C)],
                                func=AF.Copy, scale=extrascale))
                    used(("stg",), "scalar")

            cast_sq(w16a, dr["wfr"])
            cast_sq(w16b, dr["wfi"], extra=w16c, extrascale=-1.0)
            casts_sc = p.count(s_sc)
            p.wait("tensor", s_w, fw)
            p.wait("tensor", s_sc, casts_sc)

            def big_mm(dst16, lhs1, rhs1, lhs2, rhs2, scale=1.0):
                for och in range(8):
                    for half in range(2):
                        b = half
                        acquire(b, "tensor")
                        for k in range(8):
                            te(lambda e, o=och, h=half, k=k, b=b, l=lhs1, r=rhs1:
                               e.matmul(out=pb[b][:],
                                        lhsT=l[:, ts(k, C)][:, ts(o)],
                                        rhs=r[:, ts(k, C)][:, ts(h, 512)],
                                        start=(k == 0), stop=False), inc=False)
                        for k in range(8):
                            te(lambda e, o=och, h=half, k=k, b=b, l=lhs2, r=rhs2:
                               e.matmul(out=pb[b][:],
                                        lhsT=l[:, ts(k, C)][:, ts(o)],
                                        rhs=r[:, ts(k, C)][:, ts(h, 512)],
                                        start=False, stop=(k == 7)),
                               inc=(k == 7))
                        mmd = p.count(s_te)
                        p.wait("scalar", s_te, mmd)
                        sc(lambda e, o=och, h=half, b=b, d=dst16, sca=scale:
                           e.activation(out=d[:, ts(o, C)][:, ts(h, 512)],
                                        in_=pb[b][:], func=AF.Copy, scale=sca))
                        used(b, "scalar")

            big_mm(ptre, w16a, fr, w16c, fi)
            big_mm(ptim, w16a, fi, w16b, fr)
            m16 = w16c
            big_mm(m16, ptre, fr, ptim, fi, scale=0.5 / 1024)
            mdone = p.count(s_sc)
            p.wait("sync", s_sc, mdone)
            for ct in range(CT):
                dma("sync", dr["s_m"][ct], m16[:, ts(ct, C)])

            fold_plan = [("wkT_t", 0, 24, 0), ("wkT_t", 1, 48, 0),
                         ("wvT_t", 2, 32, 0), ("wvT_t", 3, 56, 0),
                         ("wrT_t", 4, 40, 0), ("wrT_t", 5, 64, 0)]
            for src, vi, col0, _ in fold_plan:
                for sl in range(8):
                    acquire(("stg",), "sync")
                    wdone = dma("sync", stg[:, 0:1024], dr[src][sl])
                    p.wait("scalar", s_w, wdone)
                    for ct in range(CT):
                        sc(lambda e, ct=ct, c0=col0: e.activation(
                            out=stg16[:, ts(ct)], in_=stg[:, ts(ct)],
                            func=AF.Copy, scale=sv[:, c0 + ct:c0 + ct + 1]))
                    used(("stg",), "scalar")
                    dn = p.count(s_sc)
                    p.wait("sync", s_sc, dn)
                    dma("sync", dr["s_kv"][vi][sl], stg16[:, 0:1024])
                    used(("stg16",), "sync")
                    acquire(("stg16",), "scalar")

            def plain_cast(src_t, dst_t, n_t, rows=128, cols=1024, scale=1.0):
                for s in range(n_t):
                    acquire(("stg",), "sync")
                    wdone = dma("sync", stg[0:rows, 0:cols], src_t[s])
                    p.wait("scalar", s_w, wdone)
                    sc(lambda e, r=rows, cl=cols, sca=scale: e.activation(
                        out=stg16[0:r, 0:cl], in_=stg[0:r, 0:cl], func=AF.Copy,
                        scale=sca))
                    used(("stg",), "scalar")
                    dn = p.count(s_sc)
                    p.wait("sync", s_sc, dn)
                    dma("sync", dst_t[s], stg16[0:rows, 0:cols])
                    used(("stg16",), "sync")
                    acquire(("stg16",), "scalar")

            plain_cast(dr["woT_t"], dr["s_wo"], 8, scale=0.5)
            plain_cast(dr["winT_t"], dr["s_win"], 32)
            plain_cast(dr["wg_t"], dr["s_wg"], 8)
            for ee in range(E):
                for ct in range(CT):
                    acquire(("stg",), "sync")
                    wdone = dma("sync", stg[:, 0:1024], dr["we_t"][ee][ct])
                    p.wait("scalar", s_w, wdone)
                    sc(lambda e: e.activation(out=stg16[:, 0:1024],
                                              in_=stg[:, 0:1024], func=AF.Copy))
                    used(("stg",), "scalar")
                    dn = p.count(s_sc)
                    p.wait("sync", s_sc, dn)
                    for fh in range(2):
                        dma("sync",
                            dr["s_we"][ee][fh][:, ct * 512:(ct + 1) * 512],
                            stg16[:, fh * 512:fh * 512 + 512])
                    used(("stg16",), "sync")
                    acquire(("stg16",), "scalar")
            for co in range(8):
                acquire(("stg",), "sync")
                wdone = dma("sync", stg[:, 0:2048], dr["woutT_t"][co])
                p.wait("scalar", s_w, wdone)
                sc(lambda e: e.activation(out=stg16[:, 0:2048],
                                          in_=stg[:, 0:2048], func=AF.Copy))
                used(("stg",), "scalar")
                dn = p.count(s_sc)
                p.wait("sync", s_sc, dn)
                dma("sync", dr["s_wout"][co], stg16[:, 0:2048])
                used(("stg16",), "sync")
                acquire(("stg16",), "scalar")
            barrier()

        x_res = sb("x_res", [128, CT * T])
        for ct in range(CT):
            dma("sync", x_res[:, ts(ct, T)], dr["xT"][ct])
        barrier()

        def ffn0_phase(tag):
            with ExitStack() as sco:
                m16 = sb(f"{tag}m", [128, 8 * C], F16, c=sco)
                xn16 = sb(f"{tag}xn", [128, CT * TC], F16, c=sco)
                xf16 = sb(f"{tag}xf", [128, CT * TC], F16, c=sco)
                mb = sb(f"{tag}mb", [128, TC], c=sco)[:]
                rb = sb(f"{tag}rb", [128, TC], c=sco)[:]
                t1 = sb(f"{tag}t1", [128, TC], c=sco)[:]
                var = sb(f"{tag}v", [128, TC], c=sco)[:]
                for ct in range(CT):
                    dma("sync", m16[:, ts(ct, C)], dr["s_m"][ct])
                mw = p.count(s_w)
                p.wait("tensor", s_w, mw)
                if DEBUG == "m16" and tag == "p1":
                    p.wait("scalar", s_w, mw)
                    for ct in range(CT):
                        dbg_f32(m16[:, ts(ct, C)], ct * 1024, mb)
                for ch in range(NCH):
                    ln_rows(ch, xf16, None, mb, rb, t1, var, 0, 1)
                    acquire(("xn", id(xn16)), "vector")
                    for ct in range(CT):
                        xs = x_res[:, ts(ct, T)][:, ts(ch, TC)]
                        ve(lambda e, xs=xs: e.tensor_tensor(
                            out=t1, in0=xs, in1=rb, op=OP.mult))
                        ve(lambda e, ct=ct: e.tensor_tensor(
                            out=xn16[:, ts(ct, TC)], in0=t1, in1=var,
                            op=OP.subtract))
                    app = p.count(s_v)
                    for co in range(CT):
                        b = 2 + (co % 2)
                        acquire(b, "tensor")
                        p.wait("tensor", s_v, app)
                        for k in range(CT):
                            te(lambda e, co=co, k=k, b=b: e.matmul(
                                out=pb[b][:], lhsT=m16[:, ts(k, C)][:, ts(co)],
                                rhs=xn16[:, ts(k, TC)],
                                start=(k == 0), stop=(k == CT - 1)),
                               inc=(k == CT - 1))
                        used(("xn", id(xn16)), "tensor")
                        mmd = p.count(s_te)
                        p.wait("vector", s_te, mmd)
                        xs = x_res[:, ts(co, T)][:, ts(ch, TC)]
                        ve(lambda e, b=b, xs=xs, co=co: e.scalar_tensor_tensor(
                            out=t1, in0=pb[b][:], scalar=bias05[:, co:co + 1],
                            in1=xs, op0=OP.add, op1=OP.add))
                        used(b, "vector")
                        ve(lambda e, xs=xs: e.tensor_copy(out=xs, in_=t1))
                    used(("lnrows", id(mb)), "vector")
                barrier()

        if PHASES > 1 and 1 not in SKIPS:
            ffn0_phase("p1")

        def rwkv_phase():
            with ExitStack() as sco:
                xn16 = sb("p2xn", [128, CT * (TC + 1)], F16, c=sco)
                xf16 = sb("p2xf", [128, CT * TC], F16, c=sco)
                mb = sb("p2mb", [128, TC], c=sco)[:]; rb = sb("p2rb", [128, TC], c=sco)[:]
                t1 = sb("p2t1", [128, TC], c=sco)[:]; var = sb("p2v", [128, TC], c=sco)[:]
                kv16 = [sb("p2kv0", [128, 6 * C], F16, c=sco)] * 2
                wo16 = [sb(f"p2wo{i}", [128, C], F16, c=sco) for i in range(2)]
                rk16 = sb("p2rk", [128, 8 * TC], F16, c=sco)
                ek = sb("p2ek", [128, TC], c=sco)[:]; uu = sb("p2u", [128, TC], c=sco)[:]
                As = sb("p2A", [128, TC + 1], c=sco)
                Bs = sb("p2B", [128, TC + 1], c=sco)
                num = sb("p2n", [128, TC], c=sco)[:]; den = sb("p2d", [128, TC], c=sco)[:]
                rec = sb("p2r", [128, TC], c=sco)[:]; th = sb("p2th", [128, TC], c=sco)[:]

                def xnsl(ct):
                    return xn16[:, ct * (TC + 1):(ct + 1) * (TC + 1)]

                for ch in range(NCH):
                    ln_rows(ch, xf16, None, mb, rb, t1, var, 0, 1)
                    acquire("xn2", "vector")
                    for ct in range(CT):
                        ve(lambda e, ct=ct: e.tensor_copy(
                            out=xnsl(ct)[:, 0:1], in_=xsh0[:, ct:ct + 1]))
                    for ct in range(CT):
                        xs = x_res[:, ts(ct, T)][:, ts(ch, TC)]
                        ve(lambda e, xs=xs: e.tensor_tensor(
                            out=t1, in0=xs, in1=rb, op=OP.mult))
                        ve(lambda e, ct=ct: e.tensor_tensor(
                            out=xnsl(ct)[:, 1:TC + 1], in0=t1, in1=var,
                            op=OP.subtract))
                    for ct in range(CT):
                        ve(lambda e, ct=ct: e.tensor_copy(
                            out=xsh0[:, ct:ct + 1], in_=xnsl(ct)[:, TC:TC + 1]))
                    xn_done = p.count(s_v)
                    used(("lnrows", id(mb)), "vector")
                    acquire("rk2", "vector")

                    for sl in range(8):
                        kvb = kv16[sl % 2]
                        acquire(("kv", id(kvb)), "sync")
                        wvd = None
                        for v6 in range(6):
                            wvd = dma("sync", kvb[:, ts(v6, C)],
                                      dr["s_kv"][v6][sl])
                        p.wait("tensor", s_w, wvd)
                        p.wait("tensor", s_v, xn_done)
                        for wi in range(3):
                            b = wi
                            acquire(b, "tensor")
                            for ct in range(CT):
                                te(lambda e, wi=wi, ct=ct, b=b, kvb=kvb: e.matmul(
                                    out=pb[b][:],
                                    lhsT=kvb[:, ts(2 * wi, C)][:, ts(ct)],
                                    rhs=xnsl(ct)[:, 1:TC + 1],
                                    start=(ct == 0), stop=False), inc=False)
                            for ct in range(CT):
                                te(lambda e, wi=wi, ct=ct, b=b, kvb=kvb: e.matmul(
                                    out=pb[b][:],
                                    lhsT=kvb[:, ts(2 * wi + 1, C)][:, ts(ct)],
                                    rhs=xnsl(ct)[:, 0:TC],
                                    start=False, stop=(ct == CT - 1)),
                                   inc=(ct == CT - 1))
                        used("xn2", "tensor")
                        used(("kv", id(kvb)), "tensor")
                        mmd = p.count(s_te)
                        p.wait("scalar", s_te, mmd - 2)
                        sc(lambda e: e.activation(out=ek, in_=pb[0][:], func=AF.Exp))
                        used(0, "scalar")
                        ek_done = p.count(s_sc)
                        p.wait("scalar", s_te, mmd)
                        sc(lambda e: e.activation(out=th, in_=pb[2][:],
                                                  func=AF.Tanh, scale=0.5))
                        used(2, "scalar")
                        th_done = p.count(s_sc)
                        p.wait("vector", s_sc, ek_done)
                        p.wait("vector", s_te, mmd - 1)
                        ve(lambda e: e.tensor_tensor(out=uu, in0=ek, in1=pb[1][:],
                                                     op=OP.mult))
                        used(1, "vector")
                        ve(lambda e, sl=sl: e.tensor_copy(
                            out=As[:, 0:1], in_=stA[:, sl:sl + 1]))
                        ve(lambda e, sl=sl: e.tensor_copy(
                            out=Bs[:, 0:1], in_=stB[:, sl:sl + 1]))
                        ve(lambda e, sl=sl: e.tensor_tensor_scan(
                            out=As[:, 1:TC + 1],
                            data0=lam8[:, sl:sl + 1].broadcast_to((128, TC)),
                            data1=uu, initial=As[:, 0:1],
                            op0=OP.mult, op1=OP.add))
                        ve(lambda e, sl=sl: e.tensor_tensor_scan(
                            out=Bs[:, 1:TC + 1],
                            data0=lam8[:, sl:sl + 1].broadcast_to((128, TC)),
                            data1=ek, initial=Bs[:, 0:1],
                            op0=OP.mult, op1=OP.add))
                        ve(lambda e, sl=sl: e.tensor_copy(
                            out=stA[:, sl:sl + 1], in_=As[:, TC:TC + 1]))
                        ve(lambda e, sl=sl: e.tensor_copy(
                            out=stB[:, sl:sl + 1], in_=Bs[:, TC:TC + 1]))
                        ve(lambda e, sl=sl: e.scalar_tensor_tensor(
                            out=num, in0=uu, scalar=etf8[:, sl:sl + 1],
                            in1=As[:, 0:TC], op0=OP.mult, op1=OP.add))
                        ve(lambda e, sl=sl: e.scalar_tensor_tensor(
                            out=den, in0=ek, scalar=etf8[:, sl:sl + 1],
                            in1=Bs[:, 0:TC], op0=OP.mult, op1=OP.add))
                        ve(lambda e: e.reciprocal(out=rec, in_=den))
                        ve(lambda e: e.tensor_tensor(out=num, in0=num, in1=rec,
                                                     op=OP.mult))
                        p.wait("vector", s_sc, th_done)
                        ve(lambda e, sl=sl: e.scalar_tensor_tensor(
                            out=rk16[:, ts(sl, TC)], in0=th, scalar=1.0,
                            in1=num, op0=OP.add, op1=OP.mult))
                        if DEBUG == "rwkv" and ch == 0 and sl == 0 and not dbg_done:
                            dbg_done.append(1)
                            dv = p.count(s_v)
                            p.wait("scalar", s_v, dv)
                            dbg_f32(ek, 0, dbgst[:])
                            dbg_f32(uu, 512, dbgst[:])
                            dbg_f32(As[:, 1:TC + 1], 1024, dbgst[:])
                            dbg_f32(Bs[:, 1:TC + 1], 1536, dbgst[:])
                            dbg_f32(num, 2048, dbgst[:])
                            dbg_f32(th, 2560, dbgst[:])
                            dbg_f32(rk16[:, 0:TC], 3072, dbgst[:])
                            dbg_f32(xn16[:, 0:TC + 1], 3584, dbgst[:])
                    rk_done = p.count(s_v)
                    p.wait("tensor", s_v, rk_done)
                    for co in range(CT):
                        wob = wo16[co % 2]
                        acquire(("p2wo", id(wob)), "sync")
                        wod = dma("sync", wob[:], dr["s_wo"][co])
                        b = 3 + (co % 2)
                        acquire(b, "tensor")
                        p.wait("tensor", s_w, wod)
                        for ds in range(8):
                            te(lambda e, co=co, ds=ds, b=b, wob=wob: e.matmul(
                                out=pb[b][:], lhsT=wob[:, ts(ds)],
                                rhs=rk16[:, ts(ds, TC)],
                                start=(ds == 0), stop=(ds == 7)), inc=(ds == 7))
                        used(("p2wo", id(wob)), "tensor")
                        mmd = p.count(s_te)
                        p.wait("vector", s_te, mmd)
                        xs = x_res[:, ts(co, T)][:, ts(ch, TC)]
                        ve(lambda e, b=b, xs=xs: e.tensor_tensor(
                            out=xs, in0=pb[b][:], in1=xs, op=OP.add))
                        used(b, "vector")
                    used("rk2", "tensor")
                    acquire("xn2", "vector")
                barrier()

        if PHASES > 2 and 2 not in SKIPS:
            rwkv_phase()
        if PHASES > 3 and 3 not in SKIPS:
            ffn0_phase("p3")

        def moe_phase(tag):
            p.serial = (MOESER == 1 or (MOESER == 2 and tag == "p6")
                        or bool(SERALL))

            def seg(bit):
                if MOESER == 3 and tag == "p6" and not SERALL:
                    p.serial = bool(MOESEG & bit)
            with ExitStack() as sco:
                xn16 = sb(f"{tag}xn", [128, CT * TC], F16, c=sco)
                xf16 = sb(f"{tag}xf", [128, CT * TC], F16, c=sco)
                mb = sb(f"{tag}mb", [128, TC], c=sco)[:]
                rb = sb(f"{tag}rb", [128, TC], c=sco)[:]
                t1 = sb(f"{tag}t1", [128, TC], c=sco)[:]
                var = sb(f"{tag}v", [128, TC], c=sco)[:]
                wu = sb(f"{tag}wu", [128, 8 * C], F16, c=sco)
                wg16 = wu
                we16 = [wu[:, 0:8 * 512], wu[:, 8 * 512:8 * C]]
                mx16 = [sb(f"{tag}mx0", [128, CT * TC], F16, c=sco)] * 2
                q32 = sb(f"{tag}q", [128, 1024], c=sco)
                mcol = sb(f"{tag}mc", [128, 4 * 8], c=sco)
                mrow = sb(f"{tag}mr", [8, 512], F16, c=sco)
                mb16 = sb(f"{tag}mb16", [128, 2 * TC], F16, c=sco)
                outT = sb(f"{tag}oT", [128, 4 * 512], c=sco)
                p05 = sb(f"{tag}p05", [128, 4], c=sco)
                mxc = sb(f"{tag}mxc", [128, 8], c=sco)
                eidxu = sb(f"{tag}eiu", [128, 1], U32, c=sco)
                eidxf = sb(f"{tag}eif", [128, 1], c=sco)
                sumc = sb(f"{tag}sum", [128, 2], c=sco)
                iota16 = sb(f"{tag}io", [128, 1024], F16, c=sco)
                p.emit("gpsimd", lambda e: e.iota(
                    out=iota16[:], pattern=[[1, 1024]], base=-2048,
                    channel_multiplier=0,
                    allow_small_or_imprecise_dtypes=True), inc=s_g, n=1)
                p.wait("vector", s_g, p.count(s_g))

                for ch in range(NCH):
                    seg(1)
                    acquire(("wu", id(we16[0])), "sync")
                    acquire(("wu", id(we16[1])), "sync")
                    for ct in range(CT):
                        dma("sync", wg16[:, ts(ct, C)], dr["s_wg"][ct])
                    wg_w = p.count(s_w)
                    p.wait("tensor", s_w, wg_w)
                    ln_rows(ch, xf16, None, mb, rb, t1, var, 0, 1)
                    acquire(("xn", id(xn16)), "vector")
                    for ct in range(CT):
                        xs = x_res[:, ts(ct, T)][:, ts(ch, TC)]
                        ve(lambda e, xs=xs: e.tensor_tensor(
                            out=t1, in0=xs, in1=rb, op=OP.mult))
                        ve(lambda e, ct=ct: e.tensor_tensor(
                            out=xn16[:, ts(ct, TC)], in0=t1, in1=var,
                            op=OP.subtract))
                    app = p.count(s_v)
                    used(("lnrows", id(mb)), "vector")
                    acquire(("lnxf", id(xf16)), "gpsimd")
                    bgw = dma("gpsimd", xf16[0:1, 0:1024], dr["s_bg"])
                    p.wait("tensor", s_g, bgw)
                    acquire(("mcol",), "vector")
                    acquire(("p05",), "scalar")
                    for su in range(4):
                        acquire("pbig", "tensor")
                        p.wait("tensor", s_v, app)
                        for half in range(2):
                            for k in range(CT):
                                te(lambda e, su=su, h=half, k=k: e.matmul(
                                    out=pbig[:, ts(h, 512)],
                                    lhsT=xn16[:, ts(k, TC)][:, ts(su)],
                                    rhs=wg16[:, ts(k, C)][:, ts(h, 512)],
                                    start=(k == 0), stop=False), inc=False)
                            te(lambda e, h=half: e.matmul(
                                out=pbig[:, ts(h, 512)],
                                lhsT=ones16[0:1, :],
                                rhs=xf16[0:1, ts(h, 512)],
                                start=False, stop=True), inc=True)
                        used(("wu", id(we16[0])), "tensor")
                        used(("wu", id(we16[1])), "tensor")
                        used(("xn", id(xn16)), "tensor")
                        used(("lnxf", id(xf16)), "tensor")
                        mmd = p.count(s_te)
                        if DEBUG == "moe2" and su == 0 and not dbg_done:
                            dbg_done.append(1)
                            p.wait("scalar", s_te, mmd)
                            p.wait("scalar", s_v, app)
                            dbg_f32(pbig[:], 0, dbgst[:])
                            dbg_f32(xn16[:, 0:1024], 1024, dbgst[:])
                            dbg_f32(wg16[:, 0:1024], 2048, dbgst[:])
                            p.wait("vector", s_sc, p.count(s_sc))
                        p.wait("vector", s_te, mmd)
                        for h in range(2):
                            ve(lambda e, h=h: e.reduce_max(
                                out=mxc[:, 2 + h:3 + h], in_=pbig[:, ts(h, 512)],
                                axis=mybir.AxisListType.X))
                        vfence()
                        ve(lambda e: e.tensor_tensor(
                            out=mxc[:, 0:1], in0=mxc[:, 2:3], in1=mxc[:, 3:4],
                            op=OP.max))
                        vfence()
                        acquire(("outT",), "vector")
                        dstab = outT[:, 0:1024]
                        ve(lambda e, dstab=dstab: e.tensor_tensor(
                            out=dstab, in0=pbig[:],
                            in1=mxc[:, 0:1].broadcast_to((128, 1024)),
                            op=OP.subtract))
                        used("pbig", "vector")
                        d_done = p.count(s_v)
                        acquire(("q32",), "scalar")
                        p.wait("scalar", s_v, d_done)
                        sc(lambda e, dstab=dstab: e.activation(
                            out=q32[:], in_=dstab, func=AF.Exp))
                        q_done = p.count(s_sc)
                        p.wait("vector", s_sc, q_done)
                        for h in range(2):
                            ve(lambda e, h=h: e.reduce_sum(
                                out=sumc[:, h:h + 1], in_=q32[:, ts(h, 512)],
                                axis=mybir.AxisListType.X))
                        vfence()
                        ve(lambda e: e.tensor_tensor(
                            out=sumc[:, 0:1], in0=sumc[:, 0:1],
                            in1=sumc[:, 1:2], op=OP.add))
                        vfence()
                        ve(lambda e: e.tensor_scalar(
                            out=sumc[:, 0:1], in0=sumc[:, 0:1], scalar1=2.0,
                            scalar2=None, op0=OP.mult))
                        vfence()
                        ve(lambda e: e.reciprocal(out=sumc[:, 1:2],
                                                  in_=sumc[:, 0:1]))
                        rp_done = p.count(s_v)
                        p.wait("scalar", s_v, rp_done)
                        sc(lambda e, su=su: e.activation(
                            out=p05[:, su:su + 1], in_=sumc[:, 1:2],
                            func=AF.Copy))
                        used(("q32",), "scalar")
                        ve(lambda e, dstab=dstab: e.tensor_scalar(
                            out=dstab, in0=dstab, scalar1=0.0, scalar2=None,
                            op0=OP.is_equal))
                        ve(lambda e, dstab=dstab: e.tensor_tensor(
                            out=dstab, in0=dstab, in1=iota16[:], op=OP.mult))
                        ve(lambda e, dstab=dstab: e.tensor_reduce(
                            out=eidxf[:], in_=dstab,
                            axis=mybir.AxisListType.X, op=OP.min))
                        used(("outT",), "vector")
                        vfence()
                        ve(lambda e: e.tensor_scalar(
                            out=eidxf[:], in0=eidxf[:], scalar1=2048.0,
                            scalar2=None, op0=OP.add))
                        vfence()
                        ve(lambda e: e.tensor_copy(out=eidxu[:], in_=eidxf[:]))
                        vfence()
                        ve(lambda e: e.tensor_scalar(
                            out=eidxu[:], in0=eidxu[:], scalar1=7,
                            scalar2=None, op0=OP.bitwise_and))
                        vfence()
                        ve(lambda e: e.tensor_copy(out=eidxf[:], in_=eidxu[:]))
                        vfence()
                        for ee in range(E):
                            ve(lambda e, su=su, ee=ee: e.tensor_scalar(
                                out=mcol[:, su * 8 + ee:su * 8 + ee + 1],
                                in0=eidxf[:], scalar1=float(ee), scalar2=None,
                                op0=OP.is_equal))
                        if DEBUG == "moe" and su == 0 and not dbg_done:
                            dbg_done.append(1)
                            dv = p.count(s_v)
                            p.wait("scalar", s_v, dv)
                            dbg_f32(q32[:], 0, dbgst[:])
                            dbg_f32(mxc[:, 0:8], 1024, dbgst[:])
                            dbg_f32(sumc[:, 0:1], 1032, dbgst[:])
                            dbg_f32(p05[:, 0:1], 1033, dbgst[:])
                            dbg_f32(eidxf[:], 1034, dbgst[:])
                            dbg_f32(xn16[:, 0:512], 2048, dbgst[:])
                    mask_done = p.count(s_v)
                    seg(2)
                    p.wait("tensor", s_v, mask_done)
                    acquire(("mrow",), "scalar")
                    for su in range(4):
                        b = su % 2
                        acquire(b, "tensor")
                        te(lambda e, su=su, b=b: e.transpose(
                            out=pb[b][0:8, 0:128], in_=mcol[:, su * 8:su * 8 + 8],
                            identity=ident32[:]))
                        trd = p.count(s_te)
                        p.wait("scalar", s_te, trd)
                        sc(lambda e, su=su, b=b: e.activation(
                            out=mrow[:, ts(su)], in_=pb[b][0:8, 0:128],
                            func=AF.Copy))
                        used(b, "scalar")
                    used(("mcol",), "tensor")
                    mrow_done = p.count(s_sc)
                    p.wait("tensor", s_sc, mrow_done)
                    p.wait("gpsimd", s_sc, mrow_done)
                    fm_done = p.count(s_sc)
                    p.wait("tensor", s_sc, mrow_done)
                    if MOEBAR & 1:
                        barrier()
                    for fh in range(2 if MOESUB >= 2 else 0):
                        seg(4)
                        for ee in range(E):
                            web = we16[ee % 2]
                            acquire(("wu", id(we16[ee % 2])), "sync")
                            wed = dma("sync", web, dr["s_we"][ee][fh])
                            half = ee % 2
                            acquire(("mslot",), "gpsimd")
                            msw = dma("gpsimd", mb16[0:1, 512:1024],
                                      mrow[ee:ee + 1, :])
                            acquire("pbig", "tensor")
                            p.wait("tensor", s_g, msw)
                            te(lambda e, ee=ee, h=half: e.matmul(
                                out=pbig[:, ts(h, 512)],
                                lhsT=ones16[0:1, :],
                                rhs=mb16[0:1, 512:1024],
                                start=True, stop=True))
                            used(("mslot",), "tensor")
                            bd = p.count(s_te)
                            acquire(("mb16", 0), "scalar")
                            p.wait("scalar", s_te, bd)
                            sc(lambda e, ee=ee, h=half: e.activation(
                                out=mb16[:, 0:TC],
                                in_=pbig[:, ts(h, 512)], func=AF.Copy))
                            used("pbig", "scalar")
                            mbd = p.count(s_sc)
                            mx = mx16[ee % 2]
                            acquire(("mx", id(mx)), "vector")
                            p.wait("vector", s_sc, mbd)
                            for ct in range(CT):
                                ve(lambda e, ct=ct, ee=ee, mx=mx: e.tensor_tensor(
                                    out=mx[:, ts(ct, TC)],
                                    in0=xn16[:, ts(ct, TC)],
                                    in1=mb16[:, 0:TC], op=OP.mult))
                            used(("mb16", 0), "vector")
                            mxd = p.count(s_v)
                            if ee == 0:
                                for su in range(4):
                                    acquire(su, "tensor")
                            p.wait("tensor", s_v, mxd)
                            p.wait("tensor", s_w, wed)
                            for su in range(4):
                                for k in range(CT):
                                    te(lambda e, fh=fh, su=su, k=k, ee=ee,
                                       mx=mx, web=web: e.matmul(
                                        out=pb[su][:],
                                        lhsT=mx[:, ts(k, TC)][:, ts(su)],
                                        rhs=web[:, ts(k, 512)],
                                        start=(ee == 0 and k == 0), stop=False),
                                       inc=False)
                            p.emit("tensor", lambda e: e.nop(), inc=s_te)
                            used(("wu", id(we16[ee % 2])), "tensor")
                            used(("mx", id(mx)), "tensor")
                        for su in range(4):
                            te(lambda e, fh=fh, su=su: e.matmul(
                                out=pb[su][:], lhsT=mrow[:, ts(su)],
                                rhs=be16[0:8, ts(fh, 512)],
                                start=False, stop=True), inc=True)
                        used(("mrow",), "tensor")
                        emm = p.count(s_te)
                        seg(8)
                        acquire(("outT",), "scalar")
                        p.wait("scalar", s_te, emm)
                        for su in range(4):
                            sc(lambda e, su=su: e.activation(
                                out=outT[:, ts(su, 512)],
                                in_=pb[su][:], func=AF.Relu,
                                scale=p05[:, su:su + 1]))
                            used(su, "scalar")
                        used(("p05",), "scalar")
                        ev_done = p.count(s_sc)
                        if (DEBUG == "p4x" and ch == 0 and fh == 0
                                and not dbg_done):
                            dbg_done.append(1)
                            dbg_f32(outT[:, 0:2048], 0, dbgst[:])
                            dbg_f32(mrow[0:8, 0:512], 2048, dbgst[:])
                            dbg_f32(mb16[:, 0:1024], 2560, dbgst[:])
                            dbg_f32(p05[:, 0:4], 3584, dbgst[:])
                            dbg_f32(mcol[:, 0:32], 3588, dbgst[:])
                        p.wait("tensor", s_sc, ev_done)
                        for ft in range(4):
                            b = 4 + (ft % 2)
                            acquire(b, "tensor")
                            for su in range(4):
                                te(lambda e, ft=ft, su=su, b=b: e.transpose(
                                    out=pb[b][:, ts(su)],
                                    in_=outT[:, su * 512 + ft * 128:
                                             su * 512 + (ft + 1) * 128],
                                    identity=ident32[:]), inc=(su == 3))
                            trd = p.count(s_te)
                            if (DEBUG == "p4add" and ch == 0 and fh == 0
                                    and ft == 0 and not dbg_done):
                                dbg_done.append(1)
                                p.wait("scalar", s_te, trd)
                                dbg_f32(pb[b][:], 0, dbgst[:])
                                dbg_f32(outT[:, 0:512], 512, dbgst[:])
                                dbg_f32(x_res[:, 0:512], 1024, dbgst[:])
                                p.wait("vector", s_sc, p.count(s_sc))
                            p.wait("vector", s_te, trd)
                            co = fh * 4 + ft
                            xs = x_res[:, ts(co, T)][:, ts(ch, TC)]
                            if MOESUB >= 3:
                                ve(lambda e, b=b, xs=xs: e.tensor_tensor(
                                    out=xs, in0=pb[b][:], in1=xs, op=OP.add))
                            else:
                                ve(lambda e, b=b: e.tensor_copy(
                                    out=var, in_=pb[b][:]))
                            used(b, "vector")
                        used(("outT",), "tensor")
                        if MOEBAR & 2:
                            barrier()
                    if MOEBAR & 4:
                        barrier()
                    if DEBUG == "p4end" and ch == NCH - 1:
                        dv = p.count(s_v)
                        p.wait("scalar", s_v, dv)
                        dbg_f32(x_res[:, 0:512], 0, dbgst[:])
                        dbg_f32(x_res[:, 7 * T + 3584:7 * T + 4096], 512, dbgst[:])
                barrier()

        if PHASES > 4 and 4 not in SKIPS:
            moe_phase("p4")
            p.serial = bool(SERALL)

        def mamba_phase(sub=99):
            if sub < 1:
                return
            p.serial = bool(P5SER & 1) or bool(SERALL)
            with ExitStack() as sco:
                xn16 = sb("p5xn", [128, CT * TC], F16, c=sco)
                xf16 = sb("p5xf", [128, CT * TC], F16, c=sco)
                mb = sb("p5mb", [128, TC], c=sco)[:]; rb = sb("p5rb", [128, TC], c=sco)[:]
                t1 = sb("p5t1", [128, TC], c=sco)[:]; var = sb("p5v", [128, TC], c=sco)[:]
                wi16 = [sb(f"p5wi{i}", [128, 8 * 128], F16, c=sco) for i in range(2)]
                xin = sb("p5xi", [128, TC + 4], c=sco)
                cv0 = rb
                cv1 = t1
                xcv = sb("p5xcv", [128, DSL * TC], F16, c=sco)
                szb = sb("p5sz", [128, DSL * TC], F16, c=sco)
                xdst = sb("p5xd", [96, TC], F16, c=sco)
                if DEBUG == "p5a":
                    dbg_f32(x_res[:, 0:512], 0, dbgst[:])
                for ch in range(NCH):
                    if DEBUG == "p5a" and ch == 1:
                        dv = p.count(s_v)
                        p.wait("scalar", s_v, dv)
                        dbg_f32(x_res[:, 0:512], 512, dbgst[:])
                        dbg_f32(xn16[:, 0:512], 1024, dbgst[:])
                        dbg_f32(xcv[:, 0:512], 1536, dbgst[:])
                    ln_rows(ch, xf16, None, mb, rb, t1, var, 3, 4)
                    acquire("xn5", "vector")
                    for ct in range(CT):
                        xs = x_res[:, ts(ct, T)][:, ts(ch, TC)]
                        ve(lambda e, xs=xs: e.tensor_tensor(
                            out=t1, in0=xs, in1=rb, op=OP.mult))
                        ve(lambda e, ct=ct: e.tensor_tensor(
                            out=xn16[:, ts(ct, TC)], in0=t1, in1=var,
                            op=OP.subtract))
                    app = p.count(s_v)
                    p.wait("tensor", s_v, app)
                    acquire(2, "tensor")
                    for half, nsl in ((0, DSL), (1, DSL)):
                        for sl in range(nsl):
                            wib = wi16[sl % 2]
                            acquire(("wi", id(wib)), "sync")
                            wid = dma("sync", wib[:], dr["s_win"][half * DSL + sl])
                            b = sl % 2
                            acquire(b, "tensor")
                            p.wait("tensor", s_w, wid)
                            for k in range(CT):
                                te(lambda e, k=k, b=b, wib=wib: e.matmul(
                                    out=pb[b][:], lhsT=wib[:, ts(k)],
                                    rhs=xn16[:, ts(k, TC)],
                                    start=(k == 0), stop=(k == CT - 1)),
                                   inc=(k == CT - 1))
                            used(("wi", id(wib)), "tensor")
                            used("xn5", "tensor")
                            mmd = p.count(s_te)
                            if half == 0:
                                acquire(("xin",), "scalar")
                                sc(lambda e, sl=sl: e.activation(
                                    out=xin[:, 0:3],
                                    in_=ctail[:, sl * 4:sl * 4 + 3],
                                    func=AF.Copy))
                                p.wait("scalar", s_te, mmd)
                                sc(lambda e, b=b: e.activation(
                                    out=xin[:, 3:TC + 3], in_=pb[b][:],
                                    func=AF.Copy))
                                used(b, "scalar")
                                sc(lambda e, sl=sl: e.activation(
                                    out=ctail[:, sl * 4:sl * 4 + 3],
                                    in_=xin[:, TC:TC + 3], func=AF.Copy))
                                xin_done = p.count(s_sc)
                                p.wait("vector", s_sc, xin_done)
                                ve(lambda e, sl=sl: e.tensor_scalar(
                                    out=cv0[:], in0=xin[:, 0:TC],
                                    scalar1=sd_[:, sl * 4:sl * 4 + 1],
                                    scalar2=None, op0=OP.mult))
                                ve(lambda e, sl=sl: e.scalar_tensor_tensor(
                                    out=cv1[:], in0=xin[:, 1:TC + 1],
                                    scalar=sd_[:, sl * 4 + 1:sl * 4 + 2],
                                    in1=cv0[:], op0=OP.mult, op1=OP.add))
                                ve(lambda e, sl=sl: e.scalar_tensor_tensor(
                                    out=cv0[:], in0=xin[:, 2:TC + 2],
                                    scalar=sd_[:, sl * 4 + 2:sl * 4 + 3],
                                    in1=cv1[:], op0=OP.mult, op1=OP.add))
                                ve(lambda e, sl=sl: e.scalar_tensor_tensor(
                                    out=cv1[:], in0=xin[:, 3:TC + 3],
                                    scalar=sd_[:, sl * 4 + 3:sl * 4 + 4],
                                    in1=cv0[:], op0=OP.mult, op1=OP.add))
                                used(("xin",), "vector")
                                cvd = p.count(s_v)
                                p.wait("scalar", s_v, cvd)
                                sc(lambda e, sl=sl: e.activation(
                                    out=xcv[:, ts(sl, TC)], in_=cv1[:],
                                    func=AF.Silu,
                                    bias=sd_[:, 64 + sl:64 + sl + 1]))
                                xcvd = p.count(s_sc)
                                p.wait("tensor", s_sc, xcvd)
                                te(lambda e, sl=sl: e.matmul(
                                    out=pb[2][0:96, :],
                                    lhsT=wx16[:, sl * 96:(sl + 1) * 96],
                                    rhs=xcv[:, ts(sl, TC)],
                                    start=(sl == 0), stop=(sl == DSL - 1)),
                                   inc=(sl == DSL - 1))
                            else:
                                p.wait("scalar", s_te, mmd)
                                sc(lambda e, sl=sl, b=b: e.activation(
                                    out=szb[:, ts(sl, TC)], in_=pb[b][:],
                                    func=AF.Silu))
                                used(b, "scalar")
                        if half == 0:
                            xd_mm = p.count(s_te)
                            p.wait("scalar", s_te, xd_mm)
                            sc(lambda e: e.activation(out=xdst[:], in_=pb[2][0:96, :],
                                                      func=AF.Copy))
                            used(2, "scalar")
                            xd_done = p.count(s_sc)
                            p.wait("gpsimd", s_sc, xd_done)
                            dma("gpsimd", dr["s_xdbl"][ch], xdst[:])
                            dma("gpsimd", dr["s_xcv"][ch], xcv[:])
                            used(("xcvbuf",), "gpsimd")
                    used(("lnrows", id(mb)), "vector")
                    szdone = p.count(s_sc)
                    p.wait("gpsimd", s_sc, szdone)
                    dma("gpsimd", dr["s_sz"][ch], szb[:])
                    gcount = p.count(s_g)
                    p.wait("scalar", s_g, gcount)
                barrier()
            if sub < 2:
                return
            for ct in range(CT):
                dma("sync", dr["s_xsp"][ct], x_res[:, ts(ct, T)])
            barrier()
            if sub < 3:
                return
            p.serial = bool(P5SER & 2) or bool(SERALL)
            with ExitStack() as sco:
                yf16 = sb("p5yf", [128, DSL * TC], F16, c=sco)
                bbcb = sb("p5bb", [128, 2 * NS * TC], F16, c=sco)
                xdbl = [sb("p5cx0", [96, TC], F16, c=sco)] * 2
                dt32 = [sb("p5dt0", [128, TC], c=sco)] * 2
                dtx = sb("p5dtx", [128, TC], F16, c=sco)
                da32 = [sb(f"p5da{i}", [128, TC], c=sco) for i in range(2)]
                u16 = sb("p5u", [128, TC], F16, c=sco)
                s16 = [sb(f"p5s{i}", [128, TC], F16, c=sco) for i in range(2)]
                tmp16 = u16
                yy = sb("p5y0", [128, TC], c=sco)[:]
                xcvs = [sb("p5xv0", [128, TC], F16, c=sco)] * 2
                szs = [sb("p5zs0", [128, TC], F16, c=sco)] * 2
                woc = [sb(f"p5wo{i}", [128, DI // 4], F16, c=sco) for i in range(2)]
                for ch in range(NCH):
                    xdb = xdbl[ch % 2]
                    acquire(("xdbl", id(xdb)), "gpsimd")
                    xdd = dma("gpsimd", xdb[:], dr["s_xdbl"][ch])
                    p.wait("gpsimd", s_g, xdd)
                    acquire(("bbcb",), "scalar")
                    for kind in range(2):
                        acquire(("yf",), "gpsimd")
                        rw = p.emit("gpsimd", lambda e, k=kind, xdb=xdb:
                                    e.dma_start(
                                        out=yf16[0:1, 0:NS * TC],
                                        in_=xdb[64 + k * NS:64 + (k + 1) * NS, :],
                                        allow_slow_non_contiguous=True),
                                    inc=s_g, n=16)
                        p.wait("tensor", s_g, rw)
                        for n in range(NS):
                            b = 4 + (n % 2)
                            slot = n * 2 + kind
                            acquire(b, "tensor")
                            te(lambda e, n=n, b=b: e.matmul(
                                out=pb[b][:], lhsT=ones16[0:1, :],
                                rhs=yf16[0:1, n * TC:(n + 1) * TC],
                                start=True, stop=True))
                            bd = p.count(s_te)
                            p.wait("scalar", s_te, bd)
                            sc(lambda e, slot=slot, b=b: e.activation(
                                out=bbcb[:, slot * TC:(slot + 1) * TC],
                                in_=pb[b][:], func=AF.Copy))
                            used(b, "scalar")
                        used(("yf",), "tensor")
                    used(("xdbl", id(xdb)), "gpsimd")
                    bb_done = p.count(s_sc)
                    p.wait("vector", s_sc, bb_done)
                    acquire(("yf",), "vector")
                    for sl in range(DSL):
                        dtb = dt32[sl % 2]
                        acquire(0, "tensor")
                        te(lambda e, sl=sl, xdb=xdb: e.matmul(
                            out=pb[0][:], lhsT=wdt16[:, ts(sl)],
                            rhs=xdb[0:RK, :], start=True, stop=True))
                        dtd = p.count(s_te)
                        acquire(("dt", id(dtb)), "scalar")
                        p.wait("scalar", s_te, dtd)
                        sc(lambda e, sl=sl, dtb=dtb: e.activation(
                            out=dtb[:], in_=pb[0][:], func=AF.Exp,
                            bias=sd_[:, 96 + sl:96 + sl + 1]))
                        used(0, "scalar")
                        sc(lambda e, dtb=dtb: e.activation(
                            out=dtb[:], in_=dtb[:], func=AF.Ln, bias=1.0))
                        dt_done = p.count(s_sc)
                        xcb = xcvs[sl % 2]; szc = szs[sl % 2]
                        acquire(("xcv", id(xcb)), "gpsimd")
                        xcd = dma("gpsimd", xcb[:],
                                  dr["s_xcv"][ch][:, ts(sl, TC)])
                        acquire(("szs", id(szc)), "gpsimd")
                        szd = dma("gpsimd", szc[:],
                                  dr["s_sz"][ch][:, ts(sl, TC)])
                        p.wait("vector", s_sc, dt_done)
                        p.wait("vector", s_g, xcd)
                        acquire(("dtx",), "vector")
                        ve(lambda e, dtb=dtb, xcb=xcb: e.tensor_tensor(
                            out=dtx[:], in0=dtb[:], in1=xcb[:], op=OP.mult))
                        for n in range(NS):
                            dab = da32[n % 2]
                            acquire(("da", id(dab)), "scalar")
                            sc(lambda e, sl=sl, n=n, dab=dab, dtb=dtb:
                               e.activation(out=dab[:], in_=dtb[:], func=AF.Exp,
                                            scale=aneg[:, sl * NS + n:
                                                       sl * NS + n + 1]))
                            dad = p.count(s_sc)
                            ve(lambda e, n=n: e.tensor_tensor(
                                out=u16[:], in0=dtx[:],
                                in1=bbcb[:, (n * 2) * TC:(n * 2 + 1) * TC],
                                op=OP.mult))
                            p.wait("vector", s_sc, dad)
                            sb_ = s16[n % 2]
                            acquire(("s16", id(sb_)), "vector")
                            ve(lambda e, sl=sl, n=n, sb_=sb_, dab=dab:
                               e.tensor_tensor_scan(
                                   out=sb_[:], data0=dab[:], data1=u16[:],
                                   initial=stS[:, sl * NS + n:sl * NS + n + 1],
                                   op0=OP.mult, op1=OP.add))
                            used(("da", id(dab)), "vector")
                            scn = p.count(s_v)
                            p.wait("scalar", s_v, scn)
                            sc(lambda e, sl=sl, n=n, sb_=sb_: e.activation(
                                out=stS[:, sl * NS + n:sl * NS + n + 1],
                                in_=sb_[:, TC - 1:TC], func=AF.Copy))
                            used(("s16", id(sb_)), "scalar")
                            ve(lambda e, n=n, sb_=sb_: e.tensor_tensor(
                                out=tmp16[:], in0=sb_[:],
                                in1=bbcb[:, (n * 2 + 1) * TC:(n * 2 + 2) * TC],
                                op=OP.mult))
                            used(("s16", id(sb_)), "vector")
                            if n == 0:
                                ve(lambda e: e.tensor_copy(out=yy, in_=tmp16[:]))
                            else:
                                ve(lambda e: e.tensor_tensor(
                                    out=yy, in0=tmp16[:], in1=yy, op=OP.add))
                        used(("dt", id(dtb)), "vector")
                        used(("dtx",), "vector")
                        if (DEBUG == "p5c" and ch == 0 and sl == 0
                                and not dbg_done):
                            dbg_done.append(1)
                            dv = p.count(s_v)
                            p.wait("scalar", s_v, dv)
                            dbg_f32(dtb[:], 0, dbgst[:])
                            dbg_f32(dtx[:], 512, dbgst[:])
                            dbg_f32(s16[(NS - 1) % 2][:], 1024, dbgst[:])
                            dbg_f32(yy, 1536, dbgst[:])
                            dbg_f32(bbcb[:, 0:512], 2048, dbgst[:])
                            dbg_f32(bbcb[:, 512:1024], 2560, dbgst[:])
                            dbg_f32(xcvs[0][:], 3072, dbgst[:])
                            dbg_f32(xdbl[0][0:64, 0:512], 3584, dbgst[:])
                        p.wait("vector", s_g, szd)
                        ve(lambda e, szc=szc: e.tensor_tensor(
                            out=yy, in0=yy, in1=szc[:], op=OP.mult))
                        used(("szs", id(szc)), "vector")
                        ve(lambda e, sl=sl, xcb=xcb: e.scalar_tensor_tensor(
                            out=yf16[:, ts(sl, TC)], in0=xcb[:],
                            scalar=sd_[:, 80 + sl:80 + sl + 1], in1=yy,
                            op0=OP.mult, op1=OP.add))
                        used(("xcv", id(xcb)), "vector")
                    used(("bbcb",), "vector")
                    yf_done = p.count(s_v)
                    p.wait("tensor", s_v, yf_done)
                    for co in range(CT):
                        b = 1 + (co % 2)
                        acquire(b, "tensor")
                        for dh in range(4):
                            wob = woc[dh % 2]
                            acquire(("woc", id(wob)), "sync")
                            wod = dma("sync", wob[:],
                                      dr["s_wout"][co][:, dh * 512:(dh + 1) * 512])
                            p.wait("tensor", s_w, wod)
                            for j in range(4):
                                sl = dh * 4 + j
                                te(lambda e, sl=sl, j=j, b=b, wob=wob, dh=dh:
                                   e.matmul(
                                    out=pb[b][:], lhsT=wob[:, ts(j)],
                                    rhs=yf16[:, ts(sl, TC)],
                                    start=(sl == 0), stop=(sl == DSL - 1)),
                                   inc=(j == 3))
                            used(("woc", id(wob)), "tensor")
                        mmd = p.count(s_te)
                        p.wait("scalar", s_te, mmd)
                        xs = x_res[:, ts(co, T)][:, ts(ch, TC)]
                        sc(lambda e, b=b, xs=xs: e.activation(
                            out=xs, in_=pb[b][:], func=AF.Copy))
                        used(b, "scalar")
                    used(("yf",), "tensor")
                    used(("xdbl", id(xdb)), "tensor")
                barrier()
            if sub < 4:
                return
            p.serial = bool(SERALL)
            with ExitStack() as sco:
                bnc = [sb(f"p5bn{i}", [128, T], c=sco) for i in range(2)]
                for ct in range(CT):
                    bb = bnc[ct % 2]
                    acquire(("bnc", id(bb)), "gpsimd")
                    bd = dma("gpsimd", bb[:], dr["s_xsp"][ct])
                    p.wait("vector", s_g, bd)
                    ve(lambda e, ct=ct, bb=bb: e.tensor_tensor(
                        out=x_res[:, ts(ct, T)], in0=bb[:],
                        in1=x_res[:, ts(ct, T)], op=OP.add))
                    used(("bnc", id(bb)), "vector")
                barrier()

        if PHASES > 5:
            mamba_phase(P5SUB)
            p.serial = bool(SERALL)
        if PHASES > 6:
            moe_phase("p6")

        for ct in range(CT):
            dma("sync", out_ap[ct], x_res[:, ts(ct, T)])
        p.wait("sync", s_w)
        barrier()
        p.run(block)
    return nc


_NC_CACHE = {}


def _tile8(w):
    a = w.shape[0] // 128
    b = w.shape[1] // 128
    return np.ascontiguousarray(
        w.reshape(a, 128, b, 128).transpose(2, 1, 0, 3).reshape(b, 128, a * 128))


def kernel(L0_attn_x, L0_attn_w_k, L0_attn_w_v, L0_attn_w_r, L0_attn_w_o,
           L0_attn_time_decay, L0_attn_time_first, L0_attn_time_mix_key,
           L0_attn_time_mix_value, L0_attn_time_mix_receptance,
           L0_ffn_w_freq_real, L0_ffn_w_freq_imag, L0_ffn_bias,
           L1_attn_W_in, L1_attn_conv_w, L1_attn_conv_b, L1_attn_W_x,
           L1_attn_W_dt, L1_attn_b_dt, L1_attn_A_log, L1_attn_D,
           L1_attn_W_out, L1_ffn_Wg, L1_ffn_bg, L1_ffn_We, L1_ffn_be):
    f32 = lambda v: np.ascontiguousarray(np.asarray(v, dtype=np.float32))
    x = f32(L0_attn_x)
    B = x.shape[0]

    n_, d_ = np.meshgrid(np.arange(C), np.arange(C), indexing="ij")
    ang = (-2.0 * np.pi / C) * (n_ * d_ % C)
    fr16 = np.cos(ang).astype(np.float16).reshape(8, 128, C)
    fi16 = np.sin(ang).astype(np.float16).reshape(8, 128, C)
    ident16 = np.eye(128, dtype=np.float16)

    def colpack(v, k):
        return np.ascontiguousarray(f32(v).reshape(k, 128).T)

    smallv = np.zeros((128, 48), np.float32)
    smallv[:, 0:8] = colpack(L0_attn_time_decay, 8)
    smallv[:, 8:16] = colpack(L0_attn_time_first, 8)
    smallv[:, 16:24] = colpack(L0_ffn_bias, 8)
    smallv[:, 24:32] = colpack(np.asarray(L0_attn_time_mix_key).reshape(-1), 8)
    smallv[:, 32:40] = colpack(np.asarray(L0_attn_time_mix_value).reshape(-1), 8)
    smallv[:, 40:48] = colpack(np.asarray(L0_attn_time_mix_receptance).reshape(-1), 8)

    smalld = np.zeros((128, 368), np.float32)
    cw = f32(L1_attn_conv_w).reshape(DI, 4)
    smalld[:, 0:64] = cw.reshape(DSL, 128, 4).transpose(1, 0, 2).reshape(128, 64)
    smalld[:, 64:80] = colpack(L1_attn_conv_b, DSL)
    smalld[:, 80:96] = colpack(L1_attn_D, DSL)
    smalld[:, 96:112] = colpack(L1_attn_b_dt, DSL)
    al = f32(L1_attn_A_log)
    smalld[:, 112:112 + DSL * NS] = (
        al.reshape(DSL, 128, NS).transpose(1, 0, 2).reshape(128, DSL * NS))

    common = {
        "fr16": fr16, "fi16": fi16, "ident16": ident16,
        "wfr": f32(L0_ffn_w_freq_real).reshape(8, 128, C),
        "wfi": f32(L0_ffn_w_freq_imag).reshape(8, 128, C),
        "wkT_t": _tile8(f32(L0_attn_w_k).T),
        "wvT_t": _tile8(f32(L0_attn_w_v).T),
        "wrT_t": _tile8(f32(L0_attn_w_r).T),
        "woT_t": _tile8(f32(L0_attn_w_o).T),
        "winT_t": _tile8(f32(L1_attn_W_in).T),
        "woutT_t": _tile8(f32(L1_attn_W_out).T),
        "wxT_t": np.ascontiguousarray(
            f32(L1_attn_W_x).T.reshape(DSL, 128, 96).transpose(1, 0, 2)
            .reshape(128, DSL * 96)),
        "wdtT_t": f32(L1_attn_W_dt).T,
        "wg_t": f32(L1_ffn_Wg).reshape(8, 128, C),
        "we_t": f32(L1_ffn_We).reshape(E, 8, 128, C),
        "smallv": smallv, "smalld": smalld,
        "be": f32(L1_ffn_be), "bgrow": f32(L1_ffn_bg).reshape(1, C),
    }

    key = "prog"
    if key not in _NC_CACHE:
        _NC_CACHE[key] = build_program()
    nc = _NC_CACHE[key]

    in_maps = []
    for b in range(B):
        m = dict(common)
        m["xT"] = np.ascontiguousarray(x[b].T).reshape(CT, 128, T)
        in_maps.append(m)
    res = run_bass_kernel_spmd(nc, in_maps, core_ids=list(range(B)))
    out = np.empty((B, T, C), np.float32)
    for b in range(B):
        o = res.results[b]["out"].reshape(C, T)
        out[b] = o.T
    return out


